# version 33
# baseline (speedup 1.0000x reference)
import math
import contextlib
import numpy as np
import ml_dtypes

import concourse.bass as bass
import concourse.mybir as mybir
from concourse.bass_utils import run_bass_kernel_spmd

F32 = mybir.dt.float32
BF16 = mybir.dt.bfloat16
I32 = mybir.dt.int32
U8 = mybir.dt.uint8
AF = mybir.ActivationFunctionType
ALU = mybir.AluOpType
AX = mybir.AxisListType

B, L, A, H, HS, M, HID = 16, 4096, 1024, 16, 64, 64, 3072
NCORES = 8
SB = B // NCORES
NT = L // 512
NG = 4
NCH = A // 128
NHC = HID // 128
NPAIR = H // 2
S127 = float(np.float32(1.0 / 127.0))
S127SQ = float(np.float32(np.float32(1.0 / 127.0) * np.float32(1.0 / 127.0)))

PHASES = ("P1", "P2", "P3A", "P3B")
DBG = {"p3a_tiles": None, "p3a_pairs": None, "p3a_wo": True, "p3a_norm": True, "p2_nointerleave": False, "p2_nopinv": False}


class Prog:

    ENGS = ("sync", "vector", "scalar", "tensor", "gpsimd")

    def __init__(self, nc, es):
        self.nc = nc
        self.es = es
        self.ops = {e: [] for e in self.ENGS}
        self.last_wait = {}
        self.sems = {}
        self.cnt = {}

    def sem(self, name):
        if name not in self.sems:
            self.sems[name] = self.es.enter_context(self.nc.semaphore(name))
            self.cnt[name] = 0
        return self.sems[name]

    def val(self, name):
        self.sem(name)
        return self.cnt[name]

    def do(self, eng, fn, inc=None):
        if inc is None:
            self.ops[eng].append(fn)
            return None
        name, n = inc
        sem = self.sem(name)
        self.cnt[name] += n
        self.ops[eng].append(lambda e: fn(e).then_inc(sem, n))
        return self.cnt[name]

    def wait(self, eng, name, v):
        if v is None or v <= 0:
            return
        assert v <= self.cnt.get(name, 0), f"wait on future value {name}={v}"
        key = (eng, name)
        if self.last_wait.get(key, 0) >= v:
            return
        self.last_wait[key] = v
        sem = self.sem(name)
        self.ops[eng].append(lambda e: e.wait_ge(sem, v))

    def emit(self, block):
        for ename in self.ENGS:
            fns = self.ops[ename]
            if not fns:
                continue

            def runner(eng, fns=fns):
                for f in fns:
                    f(eng)

            getattr(block, ename)(runner)


class Res:

    def __init__(self, prog):
        self.p = prog
        self.rel = []

    def acquire(self, eng):
        for name, v in self.rel:
            self.p.wait(eng, name, v)
        self.rel = []

    def used_until(self, name, v):
        if v:
            self.rel.append((name, v))


def _carver(arena):
    def carve(off, shape, dtype):
        per = math.prod(shape[1:]) * mybir.dt.size(dtype)
        v = arena[0 : shape[0], off : off + per].bitcast(dtype)
        if len(shape) > 2:
            names = [f"d{i}" for i in range(1, len(shape))]
            v = v.rearrange(
                "p (" + " ".join(names) + ") -> p " + " ".join(names),
                **{n: shape[i + 1] for i, n in enumerate(names)},
            )
        return v

    return carve


def build_program(phases=PHASES, debug_spills=False):
    nc = bass.Bass("TRN2", target_bir_lowering=False)
    ns = {}

    x_in = nc.dram_tensor("x", [SB, L, A], F32, kind="ExternalInput").ap()
    Wq_d = nc.dram_tensor("Wq", [A, A], F32, kind="ExternalInput").ap()
    Wk_d = nc.dram_tensor("Wk", [A, A], F32, kind="ExternalInput").ap()
    Wv_d = nc.dram_tensor("Wv", [A, A], F32, kind="ExternalInput").ap()
    Wo_d = nc.dram_tensor("Wo", [A, A], F32, kind="ExternalInput").ap()
    bo_d = nc.dram_tensor("bo", [A], F32, kind="ExternalInput").ap()
    wg_d = nc.dram_tensor("wg", [HID, A], I32, kind="ExternalInput").ap()
    wu_d = nc.dram_tensor("wu", [HID, A], I32, kind="ExternalInput").ap()
    wd_d = nc.dram_tensor("wd", [A, HID], I32, kind="ExternalInput").ap()
    ident_d = nc.dram_tensor("ident", [128, 128], BF16, kind="ExternalInput").ap()
    esum_d = nc.dram_tensor("esum", [128, 128], BF16, kind="ExternalInput").ap()
    rows_d = nc.dram_tensor("rows", [1, 384], BF16, kind="ExternalInput").ap()
    brows_d = nc.dram_tensor("brows", [128, 256], BF16, kind="ExternalInput").ap()
    pic_d = nc.dram_tensor("pic", [64, 1536], F32, kind="ExternalInput").ap()

    spk = "ExternalOutput" if debug_spills else "Internal"
    q_sc = nc.dram_tensor("q_sc", [SB, NT, NCH, 128, 512], BF16, kind=spk).ap()
    k_sc = nc.dram_tensor("k_sc", [SB, NCH, 128, L], BF16, kind=spk).ap()
    v_sc = nc.dram_tensor("v_sc", [SB, L, A], BF16, kind=spk).ap()
    x2_sc = nc.dram_tensor("x2_sc", [SB, L, A], F32, kind=spk).ap()
    wgt_sc = nc.dram_tensor("wgt_sc", [NHC, 128, NCH, 128], BF16, kind=spk).ap()
    wut_sc = nc.dram_tensor("wut_sc", [NHC, 128, NCH, 128], BF16, kind=spk).ap()
    y_out = nc.dram_tensor("y", [SB, L, A], F32, kind="ExternalOutput").ap()
    dbg = {}
    if debug_spills:
        for nm, shape, dt in (
            ("lmq", [128, SB, NPAIR, 64], BF16), ("lmk", [128, SB, NPAIR, 64], BF16),
            ("g2", [128, SB, NPAIR, 128], BF16), ("rstd", [128, SB, 32], F32),
            ("nmr", [128, SB, 32], F32), ("xnl", [128, SB, NCH, 64], F32),
            ("zt", [128, SB, NPAIR, 64], BF16), ("s3v", [128, SB, NPAIR, 64], BF16),
        ):
            dbg[nm] = nc.dram_tensor("dbg_" + nm, shape, dt, kind="ExternalOutput").ap()

    es = contextlib.ExitStack()
    with es:
        P = Prog(nc, es)

        arena_w = es.enter_context(nc.sbuf_tensor("arena_w", [128, 49152], U8))
        cw = _carver(arena_w)
        Wq_sb = cw(0, [128, NCH, 1024], BF16)
        Wk_sb = cw(16384, [128, NCH, 1024], BF16)
        Wv_sb = cw(32768, [128, NCH, 1024], BF16)
        wdT_sb = cw(0, [128, NHC, 1024], BF16)
        Wo_sb = es.enter_context(nc.sbuf_tensor("Wo_sb", [128, NCH, 1024], BF16))

        arena_b = es.enter_context(nc.sbuf_tensor("arena_b", [128, 98304], U8))
        cb = _carver(arena_b)
        x_t = [cb(0 + i * 16384, [128, NG, 1024], F32) for i in range(2)]
        xn_bf4 = cb(32768, [128, NG, 1024], BF16)
        xnT_sb = cb(40960, [128, NCH, 512], BF16)
        qT_st = [cb(49152, [128, NCH, 512], BF16)]
        kT_st = [cb(57344, [128, NCH, 512], BF16)]
        v_st = [cb(65536, [128, NG, 1024], BF16)]
        NI = SB * NPAIR
        qlbd_all = cw(0, [128, NI, 128], BF16)
        s2e_all = cw(4096, [128, NI, 64], F32)
        s2bf_all = cw(8192, [128, NI, 64], BF16)
        s2T_all = cw(10240, [64, NI, 128], BF16)
        Z_all = cw(14336, [64, NI, 128], BF16)
        ZT_all = cw(18432, [64, NI, 128], BF16)
        XZb_all = cw(22528, [64, NI, 128], BF16)
        XZT_all = cw(26624, [64, NI, 128], BF16)
        T_all = cw(30720, [64, NI, 128], BF16)
        zscb = cw(34816, [64, 32], F32)
        csbf_all = cw(35072, [128, 16], BF16)
        ET_all = cb(0, [128, 32, 1024], BF16)
        kT_pair = [cb(73728 + i * 8192, [128, L], BF16) for i in range(2)]
        v4 = [cb(65536, [128, NG, 1024], BF16), cb(90112, [128, NG, 1024], BF16)]
        qT_t3 = [cb(0 + i * 8192, [128, NCH, 512], BF16) for i in range(2)]
        x_t3 = cb(16384, [128, NG, 1024], F32)
        attnoT = cb(32768, [128, NCH, 512], BF16)
        E1all = cb(40960, [128, NPAIR, 512], BF16)
        nlv = [cb(49152 + i * 2048, [128, 512], F32) for i in range(4)]
        invb = [cb(57344 + i * 1024, [128, 512], BF16) for i in range(4)]
        E1n2 = [cb(61440 + i * 1024, [128, 512], BF16) for i in range(4)]
        x2_sb = [cb(65536 + i * 4096, [128, 1024], F32) for i in range(4)]
        x2_t = [cb(0 + i * 16384, [128, NG, 1024], F32) for i in range(2)]
        x2n_bf = [cb(81920 + i * 2048, [128, 1024], BF16) for i in range(4)]
        x2nT = cb(34816, [128, NCH, 512], BF16)
        h_sb = cb(43008, [128, NHC, 512], BF16)
        out_sb = [cb(67584 + i * 4096, [128, 1024], F32) for i in range(2)]
        silu_g = cb(75776, [128, 512], BF16)

        wg_st = [es.enter_context(nc.sbuf_tensor(f"wg_st{i}", [128, NCH, 128], BF16)) for i in range(2)]
        wu_st = [es.enter_context(nc.sbuf_tensor(f"wu_st{i}", [128, NCH, 128], BF16)) for i in range(2)]

        ident = es.enter_context(nc.sbuf_tensor("ident_s", [128, 128], BF16))
        e_sum = es.enter_context(nc.sbuf_tensor("e_sum_s", [128, 128], BF16))
        rows_s = es.enter_context(nc.sbuf_tensor("rows_s", [1, 384], BF16))
        brows = es.enter_context(nc.sbuf_tensor("brows_s", [128, 256], BF16))
        pic_s = es.enter_context(nc.sbuf_tensor("pic_s", [64, 1536], F32))
        bo_bf = es.enter_context(nc.sbuf_tensor("bo_bf", [1, 1024], BF16))
        sq_scr = es.enter_context(nc.sbuf_tensor("sq_scr", [128, 1024], F32))
        stat4 = es.enter_context(nc.sbuf_tensor("stat4", [128, 24], F32))
        xnl = es.enter_context(nc.sbuf_tensor("xnl", [128, SB, NCH, 64], F32))
        xnl_bf = es.enter_context(nc.sbuf_tensor("xnl_bf", [128, SB, NCH, 64], BF16))
        qlT = es.enter_context(nc.sbuf_tensor("qlT", [128, SB, NPAIR, 64], BF16))
        klT = es.enter_context(nc.sbuf_tensor("klT", [128, SB, NPAIR, 64], BF16))
        klT_bd = es.enter_context(nc.sbuf_tensor("klT_bd", [128, SB, NPAIR, 128], BF16))
        qlT_bd = es.enter_context(nc.sbuf_tensor("qlT_bd", [128, 128], BF16))
        G2_bd = es.enter_context(nc.sbuf_tensor("G2_bd", [128, SB, NPAIR, 128], BF16))
        ZT_st = es.enter_context(nc.sbuf_tensor("ZT_st", [128, SB, NPAIR, 64], BF16))
        rstd2 = es.enter_context(nc.sbuf_tensor("rstd2", [128, SB, 32], F32))
        nmr2 = es.enter_context(nc.sbuf_tensor("nmr2", [128, SB, 32], F32))
        scal = es.enter_context(nc.sbuf_tensor("scal", [128, 64], F32))
        eps_c = es.enter_context(nc.sbuf_tensor("eps_c", [128, 1], F32))
        s2e = es.enter_context(nc.sbuf_tensor("s2e", [128, 64], F32))
        s2bf = es.enter_context(nc.sbuf_tensor("s2bf", [128, 64], BF16))
        cs_bf = es.enter_context(nc.sbuf_tensor("cs_bf", [128, 1], BF16))
        zsc_bf = es.enter_context(nc.sbuf_tensor("zsc_bf", [1, 2], BF16))
        s2T = es.enter_context(nc.sbuf_tensor("s2T", [64, 128], BF16))
        Z_f = es.enter_context(nc.sbuf_tensor("Z_f", [64, 128], BF16))
        XZ_bf = es.enter_context(nc.sbuf_tensor("XZ_bf", [64, 128], BF16))
        XZT_f = es.enter_context(nc.sbuf_tensor("XZT_f", [64, 128], BF16))
        ZTf = es.enter_context(nc.sbuf_tensor("ZTf", [64, 128], BF16))
        T_f = es.enter_context(nc.sbuf_tensor("T_f", [64, 128], BF16))
        tiny = es.enter_context(nc.sbuf_tensor("tiny", [64, 64], F32))
        tiny2 = es.enter_context(nc.sbuf_tensor("tiny2", [1, 64], F32))
        zsc_bf32 = es.enter_context(nc.sbuf_tensor("zsc_bf32", [1, 32], BF16))
        ones_col = es.enter_context(nc.sbuf_tensor("ones_col", [128, 1], BF16))
        s3inv = es.enter_context(nc.sbuf_tensor("s3inv", [128, 8], F32))
        s3v_sb = es.enter_context(nc.sbuf_tensor("s3v_sb", [128, SB, NPAIR, 64], BF16))

        psA = es.enter_context(nc.psum_tensor("psA", [128, 3072], F32))
        psB = es.enter_context(nc.psum_tensor("psB", [128, 2048], BF16))

        def bank(i, cols=512, off=0):
            return psA[:, i * 512 + off : i * 512 + off + cols]

        def sc(i):
            return scal[:, i : i + 1]

        bankres = [Res(P) for _ in range(7)]
        psBres = [Res(P), Res(P)]
        resources = {}

        def res(name):
            if name not in resources:
                resources[name] = Res(P)
            return resources[name]

        ns.update(locals())
        ns["ln_rel"] = {}

        P.do("vector", lambda e: e.memset(eps_c[:, :], 1e-5), inc=("m_dve", 1))
        ns["eps_gate"] = P.val("m_dve")
        P.do("gpsimd", lambda e: e.memset(ones_col[:, :], 1.0), inc=("m_gp", 1))
        ns["ones_gate"] = P.val("m_gp")
        for ap_d, sb in ((ident_d, ident), (esum_d, e_sum), (rows_d, rows_s), (pic_d, pic_s), (brows_d, brows)):
            P.do("sync", lambda e, a=ap_d, s=sb: e.dma_start(out=s[:, :], in_=a[:, :]), inc=("s_c", 16))
        ns["C_TOTAL"] = P.val("s_c")
        wsem = {}
        for nm, ap_d, sb in (("wq", Wq_d, Wq_sb), ("wk", Wk_d, Wk_sb), ("wv", Wv_d, Wv_sb), ("wo", Wo_d, Wo_sb)):
            srcw = ap_d.rearrange("(kc p) n -> p kc n", p=128)
            wsem[nm] = P.do("gpsimd", lambda e, a=srcw, s=sb: e.dma_start(out=s[:, :, :], in_=a), inc=(f"s_{nm}", 16))
        wsem["bo"] = P.do("gpsimd", lambda e: e.dma_start(out=bo_bf[:, :], in_=bo_d.rearrange("(a n) -> a n", a=1)), inc=("s_bo", 16))
        ns["wsem"] = wsem
        ns["W1_TOTAL"] = 0
        ns["ea_row"] = rows_s[:, 0:128]
        ns["eb_row"] = rows_s[:, 128:256]
        ns["ones_row"] = rows_s[:, 256:384]

        if "P1" in phases:
            p1(P, ns)
        if "P2" in phases:
            p2a(P, ns)
            p2b(P, ns)
        if "P3A" in phases:
            p3a(P, ns)
        if "P3B" in phases:
            p3b(P, ns)
        if debug_spills:
            P.wait("gpsimd", "m_dve", P.val("m_dve"))
            P.wait("gpsimd", "m_act", P.val("m_act"))
            pairs = [("lmq", qlT), ("lmk", klT), ("g2", G2_bd), ("rstd", rstd2), ("nmr", nmr2), ("xnl", xnl), ("zt", ZT_st), ("s3v", s3v_sb)]
            for nm, t in pairs:
                if nm in dbg:
                    ap = dbg[nm]
                    P.do("gpsimd", lambda e, a=ap, t=t: e.dma_start(out=a, in_=t[:]), inc=("p_dbg", 16))

        for name in ("p_q", "p_k", "p_v", "p_x2", "p_wgs", "p_out", "p_dbg"):
            P.wait("gpsimd", name, P.cnt.get(name, 0))

        block = es.enter_context(nc.Block())
        P.emit(block)
    return nc



def emit_ln_batch(P, ns, items, a5_gate=None):
    sc, sq_scr = ns["sc"], ns["sq_scr"]
    SUMSQ, NEGSUM, NEGMEAN, NM2, VAR, LNV, MH, RSTD, NMR = range(9)

    def slot(it_, k):
        return sc(it_[4] * 16 + k)

    for it_ in items:
        rel = ns["ln_rel"].get(it_[4])
        if rel:
            for eng in ("scalar", "vector", "gpsimd"):
                P.wait(eng, *rel)
    a1 = d1 = None
    sq_eng = ns.get("ln_sq_eng", "scalar")
    for it_ in items:
        if sq_eng == "scalar":
            a1 = P.do("scalar", lambda e, x=it_[0], o=slot(it_, SUMSQ): e.activation(sq_scr[:, :], x, AF.Square, accum_out=o),
                      inc=("m_act", 1) if it_ is items[-1] else None)
        else:
            a1 = P.do("vector", lambda e, x=it_[0], o=slot(it_, SUMSQ): e.scalar_tensor_tensor(sq_scr[:, :], x, 1.0, x, op0=ALU.mult, op1=ALU.mult, accum_out=o),
                      inc=("m_dve", 1) if it_ is items[-1] else None)
    a1sem = "m_act" if sq_eng == "scalar" else "m_dve"
    for it_ in items:
        d1 = P.do("vector", lambda e, x=it_[0], o=slot(it_, NEGSUM): e.reduce_sum(o, x, axis=AX.X, negate=True),
                  inc=("m_dve", 1) if it_ is items[-1] else None)
    P.wait("scalar", "m_dve", d1)
    a2 = None
    for it_ in items:
        a2 = P.do("scalar", lambda e, i=slot(it_, NEGSUM), o=slot(it_, NEGMEAN): e.activation(o, i, AF.Identity, scale=1.0 / 1024.0),
                  inc=("m_act", 1) if it_ is items[-1] else None)
    P.wait("vector", "m_act", a2)
    d2 = None
    for it_ in items:
        d2 = P.do("vector", lambda e, i=slot(it_, NEGMEAN), o=slot(it_, NM2): e.tensor_tensor(o, i, i, ALU.mult),
                  inc=("m_dve", 1) if it_ is items[-1] else None)
    P.wait("gpsimd", a1sem, a1)
    P.wait("gpsimd", "m_dve", d2)
    g1 = None
    for it_ in items:
        P.do("gpsimd", lambda e, i=slot(it_, SUMSQ), o=slot(it_, VAR): e.tensor_scalar_mul(o, i, 1.0 / 1024.0))
        g1 = P.do("gpsimd", lambda e, i=slot(it_, NM2), o=slot(it_, VAR): e.tensor_sub(o, o, i),
                  inc=("m_gp", 1) if it_ is items[-1] else None)
    P.wait("scalar", "m_gp", g1)
    a3 = None
    for it_ in items:
        a3 = P.do("scalar", lambda e, i=slot(it_, VAR), o=slot(it_, LNV): e.activation(o, i, AF.Ln, bias=ns["eps_c"][:, :]),
                  inc=("m_act", 1) if it_ is items[-1] else None)
    P.wait("vector", "m_act", a3)
    d3 = None
    for it_ in items:
        d3 = P.do("vector", lambda e, i=slot(it_, LNV), o=slot(it_, MH): e.tensor_scalar(o, i, -0.5, None, op0=ALU.mult),
                  inc=("m_dve", 1) if it_ is items[-1] else None)
    P.wait("scalar", "m_dve", d3)
    a4 = None
    for it_ in items:
        r_out = it_[2] if it_[2] is not None else slot(it_, RSTD)
        a4 = P.do("scalar", lambda e, i=slot(it_, MH), o=r_out: e.activation(o, i, AF.Exp),
                  inc=("m_act", 1) if it_ is items[-1] else None)
    P.wait("vector", "m_act", a4)
    d4 = None
    for it_ in items:
        r_out = it_[2] if it_[2] is not None else slot(it_, RSTD)
        n_out = it_[3] if it_[3] is not None else slot(it_, NMR)
        d4 = P.do("vector", lambda e, i=slot(it_, NEGMEAN), r=r_out, o=n_out: e.tensor_tensor(o, i, r, ALU.mult),
                  inc=("m_dve", 1) if it_ is items[-1] else None)
    outs = []
    if any(it_[1] is not None for it_ in items):
        P.wait("scalar", "m_dve", d4)
        if a5_gate:
            P.wait("scalar", *a5_gate)
        for it_ in items:
            r_out = it_[2] if it_[2] is not None else slot(it_, RSTD)
            n_out = it_[3] if it_[3] is not None else slot(it_, NMR)
            a5 = P.do("scalar", lambda e, x=it_[0], o=it_[1], r=r_out, n=n_out: e.activation(o, x, AF.Identity, bias=n, scale=r),
                      inc=("m_act", 1))
            outs.append(a5)
            ns["ln_rel"][it_[4]] = ("m_act", a5)
    else:
        for it_ in items:
            ns["ln_rel"][it_[4]] = ("m_dve", d4)
        outs = [d4] * len(items)
    return outs


def p1(P, ns):
    x_in, x_t, xn_bf4, xnT_sb = ns["x_in"], ns["x_t"], ns["xn_bf4"], ns["xnT_sb"]
    qT_st, kT_st, v_st = ns["qT_st"], ns["kT_st"], ns["v_st"]
    Wq_sb, Wk_sb, Wv_sb = ns["Wq_sb"], ns["Wk_sb"], ns["Wv_sb"]
    q_sc, k_sc, v_sc = ns["q_sc"], ns["k_sc"], ns["v_sc"]
    ident, xnl = ns["ident"], ns["xnl"]
    psB = ns["psB"]
    bank, bankres, psBres, res = ns["bank"], ns["bankres"], ns["psBres"], ns["res"]

    x_res = [res("p1_x0"), res("p1_x1")]
    stage_res = res("p1_st")
    xnT_res = res("p1_xnT")
    NTILES = SB * NT
    last_tr = 0

    def load_x(i):
        s, it = divmod(i, NT)
        buf = i % 2
        x_res[buf].acquire("sync")
        srcv = x_in[s, it * 512 : (it + 1) * 512, :].rearrange("(g p) d -> p g d", p=128)
        return P.do("sync", lambda e, a=srcv, b=buf: e.dma_start(out=x_t[b][:, :, :], in_=a), inc=(f"s_x{buf}", 16))

    def emit_lnb(i, xv, gate_tr):
        buf = i % 2
        P.wait("scalar", f"s_x{buf}", xv)
        P.wait("vector", f"s_x{buf}", xv)
        P.wait("scalar", "m_dve", ns["eps_gate"])
        items = [(x_t[buf][:, gi, :], xn_bf4[:, gi, :], None, None, gi) for gi in range(NG)]
        outs = emit_ln_batch(P, ns, items, a5_gate=("m_pe", gate_tr))
        x_res[buf].used_until("m_act", outs[-1])
        x_res[buf].used_until("m_dve", P.val("m_dve"))
        return outs

    xvs = {0: load_x(0), 1: load_x(1)}
    lnv = {0: emit_lnb(0, xvs[0], 0)}

    for i in range(NTILES):
        s, it = divmod(i, NT)
        t0 = it * 512
        if i + 2 < NTILES:
            xvs[i + 2] = load_x(i + 2)

        P.wait("tensor", "s_c", ns["C_TOTAL"])
        first_evac = True
        for gi in range(NG):
            P.wait("tensor", "m_act", lnv[i][gi])
            for c in range(NCH):
                half = c // 4
                if c % 4 == 0:
                    psBres[half].acquire("tensor")
                dst = psB[:, half * 1024 + (c % 4) * 128 : half * 1024 + (c % 4) * 128 + 128]
                tv = P.do("tensor", lambda e, d=dst, c=c, gi=gi: e.transpose(d, xn_bf4[:, gi, c * 128 : (c + 1) * 128], ident[:, :]),
                          inc=("m_pe", 1) if c % 4 == 3 else None)
                if c % 4 == 3:
                    last_tr = tv
                    if first_evac:
                        xnT_res.acquire("vector")
                        first_evac = False
                    P.wait("vector", "m_pe", tv)
                    pv = psB[:, half * 1024 : half * 1024 + 512].rearrange("p (c t) -> p c t", c=4)
                    ov = xnT_sb[:, c - 3 : c + 1, gi * 128 : (gi + 1) * 128]
                    evv = P.do("vector", lambda e, pv=pv, ov=ov: e.tensor_copy(ov, pv), inc=("m_dve", 1))
                    psBres[half].used_until("m_dve", evv)
        xnT_ready = P.val("m_dve")

        if i + 1 < NTILES:
            lnv[i + 1] = emit_lnb(i + 1, xvs[i + 1], last_tr)

        P.wait("vector", "m_dve", xnT_ready)
        for c in range(NCH):
            P.do("vector", lambda e, c=c, s=s, it=it: e.tensor_reduce(
                xnl[:, s, c, it * 8 : it * 8 + 8].rearrange("p (n o) -> p n o", o=1),
                xnT_sb[:, c, :].rearrange("p (n w) -> p n w", w=64),
                axis=AX.X, op=ALU.add), inc=("m_dve", 1) if c == NCH - 1 else None)

        P.wait("tensor", "s_wq", 16)
        P.wait("tensor", "s_wk", 16)
        P.wait("tensor", "s_wv", 16)
        P.wait("tensor", "m_dve", xnT_ready)
        stage = stage_res
        stage.acquire("tensor")
        for W_sb, dst_st, qscale in ((Wq_sb, qT_st[0], True), (Wk_sb, kT_st[0], False)):
            ev = None
            for c in range(NCH):
                bk = c % 4
                bankres[bk].acquire("tensor")
                mv = None
                for kc in range(NCH):
                    mv = P.do("tensor", lambda e, W=W_sb, c=c, kc=kc, bk=bk: e.matmul(
                        bank(bk), W[:, kc, c * 128 : (c + 1) * 128], xnT_sb[:, kc, :],
                        start=(kc == 0), stop=(kc == NCH - 1)),
                        inc=("m_pe", 1) if kc == NCH - 1 else None)
                if qscale:
                    P.wait("scalar", "m_pe", mv)
                    ev = P.do("scalar", lambda e, c=c, bk=bk, d=dst_st: e.activation(d[:, c, :], bank(bk), AF.Identity, scale=0.125), inc=("m_act", 1))
                    bankres[bk].used_until("m_act", ev)
                else:
                    P.wait("vector", "m_pe", mv)
                    ev = P.do("vector", lambda e, c=c, bk=bk, d=dst_st: e.tensor_copy(d[:, c, :], bank(bk)), inc=("m_dve", 1))
                    bankres[bk].used_until("m_dve", ev)
            if qscale:
                P.wait("sync", "m_act", ev)
                dq = q_sc[s, it].rearrange("c p t -> p c t")
                sp = P.do("sync", lambda e, d=dq: e.dma_start(out=d, in_=qT_st[0][:, :, :]), inc=("p_q", 16))
                stage.used_until("p_q", sp)
            else:
                P.wait("sync", "m_dve", ev)
                dk = k_sc[s, :, :, t0 : t0 + 512].rearrange("c p t -> p c t")
                sp = P.do("sync", lambda e, d=dk: e.dma_start(out=d, in_=kT_st[0][:, :, :]), inc=("p_k", 16))
                stage.used_until("p_k", sp)
        vev = None
        for gi in range(NG):
            for hf in range(2):
                bk = 4 + (gi * 2 + hf) % 2
                bankres[bk].acquire("tensor")
                mv = None
                for kc in range(NCH):
                    mv = P.do("tensor", lambda e, gi=gi, hf=hf, kc=kc, bk=bk: e.matmul(
                        bank(bk), xnT_sb[:, kc, gi * 128 : (gi + 1) * 128],
                        Wv_sb[:, kc, hf * 512 : (hf + 1) * 512],
                        start=(kc == 0), stop=(kc == NCH - 1)),
                        inc=("m_pe", 1) if kc == NCH - 1 else None)
                P.wait("scalar", "m_pe", mv)
                vev = P.do("scalar", lambda e, gi=gi, hf=hf, bk=bk, d=v_st[0]: e.activation(
                    d[:, gi, hf * 512 : (hf + 1) * 512], bank(bk), AF.Identity), inc=("m_act", 1))
                bankres[bk].used_until("m_act", vev)
        xnT_res.used_until("m_pe", P.val("m_pe"))
        P.wait("sync", "m_act", vev)
        dvv = v_sc[s, t0 : t0 + 512, :].rearrange("(g p) d -> p g d", p=128)
        sp3 = P.do("sync", lambda e, d=dvv: e.dma_start(out=d, in_=v_st[0][:, :, :]), inc=("p_v", 16))
        stage.used_until("p_v", sp3)
        if it == NT - 1:
            ns.setdefault("pk_s", {})[s] = P.val("p_k")


def p2a(P, ns):
    Wq_sb, Wk_sb = ns["Wq_sb"], ns["Wk_sb"]
    xnl, xnl_bf, qlT, klT, klT_bd = ns["xnl"], ns["xnl_bf"], ns["qlT"], ns["klT"], ns["klT_bd"]
    bank, bankres = ns["bank"], ns["bankres"]
    G2_bd = ns["G2_bd"]

    P.wait("scalar", "m_dve", P.val("m_dve"))
    cv = P.do("scalar", lambda e: e.copy(
        xnl_bf[:].rearrange("p a b c -> p (a b c)"), xnl[:].rearrange("p a b c -> p (a b c)")), inc=("m_act", 1))
    P.wait("tensor", "m_act", cv)

    for s in range(SB):
        for W_sb, dst, scale in ((Wq_sb, qlT, 0.125 / 64.0), (Wk_sb, klT, 1.0 / 64.0)):
            for c in range(NPAIR):
                bk = c % 4
                bankres[bk].acquire("tensor")
                mv = None
                for kc in range(NCH):
                    mv = P.do("tensor", lambda e, W=W_sb, s=s, c=c, kc=kc, bk=bk: e.matmul(
                        bank(bk, 64), W[:, kc, c * 128 : (c + 1) * 128], xnl_bf[:, s, kc, :],
                        start=(kc == 0), stop=(kc == NCH - 1)),
                        inc=("m_pe", 1) if kc == NCH - 1 else None)
                P.wait("scalar", "m_pe", mv)
                ev = P.do("scalar", lambda e, s=s, c=c, bk=bk, d=dst, sca=scale: e.activation(
                    d[:, s, c, :], bank(bk, 64), AF.Identity, scale=sca), inc=("m_act", 1))
                bankres[bk].used_until("m_act", ev)

    for s in range(SB):
        for c in range(NPAIR):
            P.do("gpsimd", lambda e, s=s, c=c: e.memset(klT_bd[:, s, c, :], 0.0))
            P.do("gpsimd", lambda e, s=s, c=c: e.memset(G2_bd[:, s, c, :], 0.0))
    gz = P.do("gpsimd", lambda e: e.engine_nop(), inc=("m_gp", 1))
    P.wait("vector", "m_gp", gz)
    P.wait("vector", "m_act", P.val("m_act"))
    for s in range(SB):
        for c in range(NPAIR):
            P.do("vector", lambda e, s=s, c=c: e.tensor_copy(klT_bd[0:64, s, c, 0:64], klT[0:64, s, c, :]))
            P.do("vector", lambda e, s=s, c=c: e.tensor_copy(klT_bd[64:128, s, c, 64:128], klT[64:128, s, c, :]))
    P.do("vector", lambda e: e.engine_nop(), inc=("m_dve", 1))

    import itertools
    ns["p2_gen"] = itertools.chain(pinv_gen(P, ns), factory_gen(P, ns))
    next(ns["p2_gen"])


def pinv_gen(P, ns):
    qlT, klT = ns["qlT"], ns["klT"]
    bank, bankres, psBres = ns["bank"], ns["bankres"], ns["psBres"]
    qlbd_all, s2e_all, s2bf_all, s2T_all = ns["qlbd_all"], ns["s2e_all"], ns["s2bf_all"], ns["s2T_all"]
    Z_all, ZT_all, XZb_all, XZT_all, T_all = ns["Z_all"], ns["ZT_all"], ns["XZb_all"], ns["XZT_all"], ns["T_all"]
    zscb, csbf_all = ns["zscb"], ns["csbf_all"]
    tiny, tiny2, zsc_bf32 = ns["tiny"], ns["tiny2"], ns["zsc_bf32"]
    ident, e_sum, pic_s = ns["ident"], ns["e_sum"], ns["pic_s"]
    ZT_st = ns["ZT_st"]
    psB, psA = ns["psB"], ns["psA"]
    scal, sc = ns["scal"], ns["sc"]
    ones_row = ns["ones_row"]
    NI = SB * NPAIR
    inst_sc = [(s, c) for s in range(SB) for c in range(NPAIR)]

    gates = {nm: P.val(nm) for nm in ("m_pe", "m_act", "m_dve", "p_q", "p_k", "p_v")}
    yield
    for eng in ("gpsimd", "vector", "scalar", "tensor"):
        for nm, v in gates.items():
            P.wait(eng, nm, v)

    for i in range(NI):
        P.do("gpsimd", lambda e, i=i: e.memset(qlbd_all[:, i, :], 0.0))
    gz = P.do("gpsimd", lambda e: e.engine_nop(), inc=("m_gp", 1))
    P.wait("vector", "m_gp", gz)
    P.wait("vector", "m_act", P.val("m_act"))
    d1 = None
    for i, (s, c) in enumerate(inst_sc):
        P.do("vector", lambda e, i=i, s=s, c=c: e.tensor_copy(qlbd_all[0:64, i, 0:64], qlT[0:64, s, c, :]))
        d1 = P.do("vector", lambda e, i=i, s=s, c=c: e.tensor_copy(qlbd_all[64:128, i, 64:128], qlT[64:128, s, c, :]),
                  inc=("m_dve", 1) if i == NI - 1 else None)

    yield
    for j in (2, 3):
        bankres[j].acquire("tensor")
    P.wait("tensor", "m_dve", d1)
    mv = None
    for i, (s, c) in enumerate(inst_sc):
        mv = P.do("tensor", lambda e, i=i, s=s, c=c: e.matmul(
            bank(2 + i // 8, 64, (i % 8) * 64), qlbd_all[:, i, :], klT[:, s, c, :],
            start=True, stop=True), inc=("m_pe", 1) if i == NI - 1 else None)
    P.wait("vector", "m_pe", mv)
    d2 = None
    for i in range(NI):
        d2 = P.do("vector", lambda e, i=i: e.reduce_max(scal[:, i : i + 1], bank(2 + i // 8, 64, (i % 8) * 64), axis=AX.X, negate=True),
                  inc=("m_dve", 1) if i == NI - 1 else None)
    P.wait("scalar", "m_dve", d2)
    a1 = None
    for i in range(NI):
        a1 = P.do("scalar", lambda e, i=i: e.activation(s2e_all[:, i, :], bank(2 + i // 8, 64, (i % 8) * 64), AF.Exp,
                                                        bias=scal[:, i : i + 1], accum_out=scal[:, 16 + i : 17 + i]),
                  inc=("m_act", 1) if i == NI - 1 else None)
    for j in (2, 3):
        bankres[j].used_until("m_act", a1)
    yield
    P.wait("vector", "m_act", a1)
    d3 = P.do("vector", lambda e: e.reciprocal(scal[:, 32:48], scal[:, 16:32]), inc=("m_dve", 1))
    P.wait("scalar", "m_dve", d3)
    a2 = None
    for i in range(NI):
        a2 = P.do("scalar", lambda e, i=i: e.activation(s2bf_all[:, i, :], s2e_all[:, i, :], AF.Identity, scale=scal[:, 32 + i : 33 + i]),
                  inc=("m_act", 1) if i == NI - 1 else None)
    P.wait("tensor", "m_act", a2)
    d4 = None
    for rnd in range(2):
        psBres[0].acquire("tensor")
        psBres[1].acquire("tensor")
        tv = None
        for i in range(rnd * 8, rnd * 8 + 8):
            tv = P.do("tensor", lambda e, i=i, rnd=rnd: e.transpose(psB[0:64, (i - rnd * 8) * 128 : (i - rnd * 8 + 1) * 128], s2bf_all[:, i, :], ident[:, :]),
                      inc=("m_pe", 1) if i == rnd * 8 + 7 else None)
        P.wait("vector", "m_pe", tv)
        d4 = P.do("vector", lambda e, rnd=rnd: e.tensor_copy(s2T_all[:].rearrange("p a b -> p (a b)")[:, rnd * 1024 : (rnd + 1) * 1024], psB[0:64, 0:1024]), inc=("m_dve", 1))
        psBres[0].used_until("m_dve", d4)
        psBres[1].used_until("m_dve", d4)
        yield
    P.wait("vector", "m_act", a2)
    d5 = None
    for i in range(NI):
        d5 = P.do("vector", lambda e, i=i: e.reduce_sum(scal[:, 48 + i : 49 + i], s2bf_all[:, i, :], axis=AX.X),
                  inc=("m_dve", 1) if i == NI - 1 else None)
    P.wait("scalar", "m_dve", d5)
    a3 = P.do("scalar", lambda e: e.copy(csbf_all[:, :], scal[:, 48:64]), inc=("m_act", 1))
    P.wait("tensor", "m_act", a3)
    d6 = None
    for rnd in range(2):
        psBres[0].acquire("tensor")
        psBres[1].acquire("tensor")
        tv = None
        for i in range(rnd * 8, rnd * 8 + 8):
            j = i - rnd * 8
            tv = P.do("tensor", lambda e, i=i, j=j: e.transpose(psB[0:1, j * 128 : (j + 1) * 128], csbf_all[:, i : i + 1], ident[:, :]),
                      inc=("m_pe", 1) if j == 7 else None)
        P.wait("vector", "m_pe", tv)
        for i in range(rnd * 8, rnd * 8 + 8):
            j = i - rnd * 8
            P.do("vector", lambda e, i=i, j=j: e.reduce_max(tiny[0:1, i : i + 1], psB[0:1, j * 128 : j * 128 + 64], axis=AX.X))
            d6 = P.do("vector", lambda e, i=i, j=j: e.reduce_max(tiny[0:1, 16 + i : 17 + i], psB[0:1, j * 128 + 64 : j * 128 + 128], axis=AX.X),
                      inc=("m_dve", 1) if j == 7 else None)
        psBres[0].used_until("m_dve", d6)
        psBres[1].used_until("m_dve", d6)
        yield
    yield
    for j in (2, 3):
        bankres[j].acquire("tensor")
    P.wait("tensor", "m_act", a2)
    mv = None
    for i in range(NI):
        mv = P.do("tensor", lambda e, i=i: e.matmul(bank(2 + i // 8, 64, (i % 8) * 64), e_sum[:, :], s2bf_all[:, i, :],
                                                    start=True, stop=True), inc=("m_pe", 1) if i == NI - 1 else None)
    P.wait("vector", "m_pe", mv)
    d7 = None
    for i in range(NI):
        pb = bank(2 + i // 8, 64, (i % 8) * 64)
        P.do("vector", lambda e, i=i, pb=pb: e.reduce_max(tiny[0:1, 32 + i : 33 + i], pb[0:1, :], axis=AX.X))
        d7 = P.do("vector", lambda e, i=i, pb=pb: e.reduce_max(tiny[0:1, 48 + i : 49 + i], pb[64:65, :], axis=AX.X),
                  inc=("m_dve", 1) if i == NI - 1 else None)
    for j in (2, 3):
        bankres[j].used_until("m_dve", d7)
    P.wait("gpsimd", "m_dve", d7)
    P.do("gpsimd", lambda e: e.tensor_mul(tiny2[0:1, 0:16], tiny[0:1, 0:16], tiny[0:1, 32:48]))
    g2 = P.do("gpsimd", lambda e: e.tensor_mul(tiny2[0:1, 16:32], tiny[0:1, 16:32], tiny[0:1, 48:64]), inc=("m_gp", 1))
    P.wait("vector", "m_gp", g2)
    d8 = P.do("vector", lambda e: e.reciprocal(tiny2[0:1, 32:64], tiny2[0:1, 0:32]), inc=("m_dve", 1))
    P.wait("scalar", "m_dve", d8)
    a4 = P.do("scalar", lambda e: e.copy(zsc_bf32[:, :], tiny2[0:1, 32:64]), inc=("m_act", 1))
    yield
    bankres[2].acquire("tensor")
    P.wait("tensor", "m_act", a4)
    mv = None
    for j in range(32):
        mv = P.do("tensor", lambda e, j=j: e.matmul(bank(2, 1, j)[0:64, :], ones_row[:, 0:64], zsc_bf32[:, j : j + 1],
                                                    start=True, stop=True), inc=("m_pe", 1) if j == 31 else None)
    P.wait("vector", "m_pe", mv)
    d9 = P.do("vector", lambda e: e.tensor_copy(zscb[:, :], bank(2, 32, 0)[0:64, :]), inc=("m_dve", 1))
    bankres[2].used_until("m_dve", d9)
    P.wait("scalar", "m_dve", max(d9, d4))
    a5 = None
    for i in range(NI):
        P.do("scalar", lambda e, i=i: e.activation(Z_all[:, i, 0:64], s2T_all[:, i, 0:64], AF.Identity, scale=zscb[:, i : i + 1]))
        a5 = P.do("scalar", lambda e, i=i: e.activation(Z_all[:, i, 64:128], s2T_all[:, i, 64:128], AF.Identity, scale=zscb[:, 16 + i : 17 + i]),
                  inc=("m_act", 1) if i == NI - 1 else None)

    def tr_all(src_t, dst_t, gname, gval):
        P.wait("tensor", gname, gval)
        ev = None
        for rnd in range(2):
            psBres[0].acquire("tensor")
            psBres[1].acquire("tensor")
            tv = None
            for i in range(rnd * 8, rnd * 8 + 8):
                for h in range(2):
                    j = (i - rnd * 8) * 2 + h
                    tv = P.do("tensor", lambda e, i=i, h=h, j=j: e.transpose(
                        psB[0:64, j * 64 : (j + 1) * 64],
                        src_t[:, i, h * 64 : (h + 1) * 64], ident[0:64, 0:64]),
                        inc=("m_pe", 1) if j == 15 else None)
            P.wait("vector", "m_pe", tv)
            ev = P.do("vector", lambda e, rnd=rnd: e.tensor_copy(dst_t[:].rearrange("p a b -> p (a b)")[:, rnd * 1024 : (rnd + 1) * 1024], psB[0:64, 0:1024]), inc=("m_dve", 1))
            psBres[0].used_until("m_dve", ev)
            psBres[1].used_until("m_dve", ev)
        return ev

    yield
    zt = tr_all(Z_all, ZT_all, "m_act", a5)

    z_gate = ("m_act", a5)
    lz = None
    for _ in range(6):
        def rounds(lhs_t, rhs_t, gname, gval, post):
            P.wait("tensor", gname, gval)
            out = None
            for rnd in range(2):
                for j in (2, 3):
                    bankres[j].acquire("tensor")
                mv = None
                for i in range(rnd * 8, rnd * 8 + 8):
                    for h in range(2):
                        col = (i - rnd * 8) * 128 + h * 64
                        last = (i == rnd * 8 + 7 and h == 1)
                        mv = P.do("tensor", lambda e, i=i, h=h, col=col, lhs_t=lhs_t, rhs_t=rhs_t: e.matmul(
                            psA[0:64, 1024 + col : 1024 + col + 64],
                            lhs_t[:, i, h * 64 : (h + 1) * 64], rhs_t[:, i, h * 64 : (h + 1) * 64],
                            start=True, stop=True), inc=("m_pe", 1) if last else None)
                dv, out = post(rnd, mv)
                for j in (2, 3):
                    bankres[j].used_until("m_dve", dv)
            return out

        def t_post(const_off):
            def post(rnd, mv):
                P.wait("vector", "m_pe", mv)
                ev = None
                for b in range(2):
                    pv = psA[0:64, 1024 + b * 512 : 1024 + (b + 1) * 512]
                    ev = P.do("vector", lambda e, b=b, rnd=rnd, pv=pv: e.scalar_tensor_tensor(
                        T_all[:].rearrange("p a b -> p (a b)")[:, rnd * 1024 + b * 512 : rnd * 1024 + (b + 1) * 512],
                        pv, -1.0, pic_s[:, const_off : const_off + 512],
                        op0=ALU.mult, op1=ALU.add), inc=("m_dve", 1) if b == 1 else None)
                return ev, ev
            return post

        def xz_post(rnd, mv):
            P.wait("vector", "m_pe", mv)
            for b in range(2):
                pv = psA[0:64, 1024 + b * 512 : 1024 + (b + 1) * 512]
                P.do("vector", lambda e, b=b, rnd=rnd, pv=pv: e.tensor_copy(
                    XZb_all[:].rearrange("p a b -> p (a b)")[:, rnd * 1024 + b * 512 : rnd * 1024 + (b + 1) * 512], pv))
            ev = None
            for b in range(2):
                pv = psA[0:64, 1024 + b * 512 : 1024 + (b + 1) * 512]
                ev = P.do("vector", lambda e, b=b, rnd=rnd, pv=pv: e.scalar_tensor_tensor(
                    T_all[:].rearrange("p a b -> p (a b)")[:, rnd * 1024 + b * 512 : rnd * 1024 + (b + 1) * 512],
                    pv, -1.0, pic_s[:, 0:512],
                    op0=ALU.mult, op1=ALU.add), inc=("m_dve", 1) if b == 1 else None)
            return ev, ev

        def sc_post(rnd, mv):
            P.wait("vector", "m_pe", mv)
            ev = None
            for b in range(2):
                pv = psA[0:64, 1024 + b * 512 : 1024 + (b + 1) * 512]
                ev = P.do("vector", lambda e, b=b, rnd=rnd, pv=pv: e.tensor_scalar(
                    Z_all[:].rearrange("p a b -> p (a b)")[:, rnd * 1024 + b * 512 : rnd * 1024 + (b + 1) * 512],
                    pv, 0.25, None, op0=ALU.mult), inc=("m_dve", 1) if b == 1 else None)
            return ev, ev

        yield
        t1 = rounds(s2T_all, Z_all, z_gate[0], z_gate[1], xz_post)
        yield
        xzt = tr_all(XZb_all, XZT_all, "m_dve", t1)
        yield
        t2 = rounds(XZT_all, T_all, "m_dve", max(t1, xzt), t_post(512))
        yield
        t3 = rounds(XZT_all, T_all, "m_dve", t2, t_post(1024))
        yield
        lz = rounds(ZT_all, T_all, "m_dve", t3, sc_post)
        z_gate = ("m_dve", lz)
        yield
        zt = tr_all(Z_all, ZT_all, "m_dve", lz)

    yield
    psBres[0].acquire("tensor")
    psBres[1].acquire("tensor")
    P.wait("tensor", "m_dve", lz)
    tv = None
    for i in range(NI):
        tv = P.do("tensor", lambda e, i=i: e.transpose(psB[0:128, i * 64 : (i + 1) * 64], Z_all[:, i, :], ident[0:64, 0:64]),
                  inc=("m_pe", 1) if i == NI - 1 else None)
    P.wait("vector", "m_pe", tv)
    ev = None
    for i, (s, c) in enumerate(inst_sc):
        ev = P.do("vector", lambda e, i=i, s=s, c=c: e.tensor_copy(ZT_st[:, s, c, :], psB[0:128, i * 64 : (i + 1) * 64]),
                  inc=("m_dve", 1) if i == NI - 1 else None)
    psBres[0].used_until("m_dve", ev)
    psBres[1].used_until("m_dve", ev)


def factory_gen(P, ns):
    wdT_sb, ident = ns["wdT_sb"], ns["ident"]
    psB, psBres = ns["psB"], ns["psBres"]
    wg_d, wu_d, wd_d = ns["wg_d"], ns["wu_d"], ns["wd_d"]
    wgt_sc, wut_sc = ns["wgt_sc"], ns["wut_sc"]
    res = ns["res"]
    chunk = [ns["wg_st"][i][:].rearrange("p a b -> p (a b)") for i in range(2)]
    unit = ns["wu_st"]

    gates = {nm: P.val(nm) for nm in ("m_pe", "m_act", "m_dve")}
    yield
    for nm, v in gates.items():
        P.wait("gpsimd", nm, v)

    ch_res = [res("fch0"), res("fch1")]
    wd_loads = [(wd_d, None, dc, q3) for dc in range(NCH) for q3 in range(3)]
    wgu_loads = [(wg_d, wgt_sc, hc, None) for hc in range(NHC)] + [(wu_d, wut_sc, hc, None) for hc in range(NHC)]
    all_loads = wd_loads + wgu_loads
    lvs = {}

    def issue_load(j):
        w_d2, _, a, q3 = all_loads[j]
        b = j % 2
        ch_res[b].acquire("gpsimd")
        if q3 is not None:
            lvs[j] = P.do("gpsimd", lambda e, a=a, q3=q3, b=b: e.dma_start(
                out=chunk[b], in_=w_d2[a * 128 : (a + 1) * 128, q3 * 1024 : (q3 + 1) * 1024]), inc=(f"s_wl{b}", 16))
        else:
            lvs[j] = P.do("gpsimd", lambda e, a=a, b=b, w=w_d2: e.dma_start(
                out=chunk[b], in_=w[a * 128 : (a + 1) * 128, :]), inc=(f"s_wl{b}", 16))

    issue_load(0)
    for jj in range(len(wd_loads)):
        dc, q3 = wd_loads[jj][2], wd_loads[jj][3]
        b = jj % 2
        if jj + 1 < len(all_loads):
            issue_load(jj + 1)
        P.wait("tensor", f"s_wl{b}", lvs[jj])
        last_tv = None
        for j in range(NCH):
            hc = q3 * 8 + j
            half = j // 4
            if j % 4 == 0:
                psBres[half].acquire("tensor")
            pslot = psB[:, half * 1024 + (j % 4) * 128 : half * 1024 + (j % 4) * 128 + 128]
            tv = P.do("tensor", lambda e, j=j, b=b, p=pslot: e.transpose(p, chunk[b][:, j * 128 : (j + 1) * 128], ident[:, :]),
                      inc=("m_pe", 1) if j % 4 == 3 else None)
            if j % 4 == 3:
                last_tv = tv
                P.wait("vector", "m_pe", tv)
                pv = psB[:, half * 1024 : half * 1024 + 512].rearrange("p (c t) -> p c t", c=4)
                ov = wdT_sb[:, hc - 3 : hc + 1, dc * 128 : (dc + 1) * 128]
                evv = P.do("vector", lambda e, pv=pv, ov=ov: e.tensor_copy(ov, pv), inc=("m_dve", 1))
                psBres[half].used_until("m_dve", evv)
        ch_res[b].used_until("m_pe", last_tv)
        yield

    unit_res = [res("fu0"), res("fu1")]
    for j2 in range(len(wgu_loads)):
        jj = len(wd_loads) + j2
        w_d, w_sc, hc, _ = all_loads[jj]
        b = jj % 2
        if jj + 1 < len(all_loads):
            issue_load(jj + 1)
        P.wait("tensor", f"s_wl{b}", lvs[jj])
        unit_res[b].acquire("vector")
        last_tv = None
        for kc in range(NCH):
            half = kc // 4
            if kc % 4 == 0:
                psBres[half].acquire("tensor")
            pslot = psB[:, half * 1024 + (kc % 4) * 128 : half * 1024 + (kc % 4) * 128 + 128]
            tv = P.do("tensor", lambda e, kc=kc, b=b, p=pslot: e.transpose(p, chunk[b][:, kc * 128 : (kc + 1) * 128], ident[:, :]),
                      inc=("m_pe", 1) if kc % 4 == 3 else None)
            if kc % 4 == 3:
                last_tv = tv
                P.wait("vector", "m_pe", tv)
                pv = psB[:, half * 1024 : half * 1024 + 512].rearrange("p (c t) -> p c t", c=4)
                evv = P.do("vector", lambda e, kc=kc, b=b, pv=pv: e.tensor_copy(unit[b][:, kc - 3 : kc + 1, :], pv), inc=("m_dve", 1))
                psBres[half].used_until("m_dve", evv)
        ch_res[b].used_until("m_pe", last_tv)
        dv = P.val("m_dve")
        P.wait("gpsimd", "m_dve", dv)
        sp = P.do("gpsimd", lambda e, w=w_sc, hc=hc, b=b: e.dma_start(out=w[hc], in_=unit[b][:, :, :]), inc=("p_wgs", 16))
        unit_res[b].used_until("p_wgs", sp)
        yield


def p2b(P, ns):
    kT_pair, ET_all, v4 = ns["kT_pair"], ns["ET_all"], ns["v4"]
    ones_col = ns["ones_col"]
    qlT, qlbd = ns["qlT"], ns["qlT_bd"]
    k_sc, v_sc = ns["k_sc"], ns["v_sc"]
    bank, bankres = ns["bank"], ns["bankres"]
    psA = ns["psA"]
    G2_bd, s3inv, s3v_sb, ZT_st = ns["G2_bd"], ns["s3inv"], ns["s3v_sb"], ns["ZT_st"]
    res = ns["res"]
    gen = ns["p2_gen"]

    def pump(n=1):
        if DBG["p2_nopinv"]:
            return
        for _ in range(n):
            try:
                next(gen)
            except StopIteration:
                break

    kt_res = [res("kt0"), res("kt1")]
    ve_res = [res("ve0"), res("ve1")]
    pairs = [(s_, c_) for s_ in range(SB) for c_ in range(NPAIR)]
    kvs = {}

    def load_kt(pi):
        s_, c_ = pairs[pi]
        kb_ = pi % 2
        kt_res[kb_].acquire("scalar")
        P.wait("scalar", "p_k", ns["pk_s"][s_])
        kvs[pi] = P.do("scalar", lambda e, s=s_, c=c_, kb=kb_: e.dma_start(
            out=kT_pair[kb][:, :], in_=k_sc[s, c]), inc=(f"s_kt{kb_}", 16))

    load_kt(0)
    load_kt(1)

    for eng in ("sync", "scalar", "vector"):
        P.wait(eng, "p_q", P.val("p_q"))
        P.wait(eng, "p_k", P.val("p_k"))
        P.wait(eng, "p_v", P.val("p_v"))

    if DBG["p2_nointerleave"]:
        pump(10 ** 6)
    prev_sweep = 0
    for s in range(SB):
        for c in range(NPAIR):
            kb = c % 2
            kv = kvs[s * NPAIR + c]
            prev_pe = P.val("m_pe")
            P.wait("gpsimd", "m_pe", prev_pe)
            gz = P.do("gpsimd", lambda e: e.memset(qlbd[:, :], 0.0), inc=("m_gp", 1))
            P.wait("vector", "m_gp", gz)
            P.do("vector", lambda e, s=s, c=c: e.tensor_copy(qlbd[0:64, 0:64], qlT[0:64, s, c, :]))
            qb = P.do("vector", lambda e, s=s, c=c: e.tensor_copy(qlbd[64:128, 64:128], qlT[64:128, s, c, :]), inc=("m_dve", 1))
            P.wait("tensor", f"s_kt{kb}", kv)
            P.wait("tensor", "m_dve", qb)
            P.wait("tensor", "m_gp", ns["ones_gate"])
            P.wait("scalar", "m_pe", prev_sweep)
            if c == 0:
                bankres[4].acquire("tensor")
            last = None
            exps = []
            for lt in range(32):
                bk = (lt // 4) % 2
                if lt % 4 == 0:
                    bankres[bk].acquire("tensor")
                mv = P.do("tensor", lambda e, lt=lt, bk=bk, kb=kb: e.matmul(
                    bank(bk, 128, (lt % 4) * 128), kT_pair[kb][:, lt * 128 : (lt + 1) * 128], qlbd[:, :],
                    start=True, stop=True), inc=("m_pe", 1) if lt % 4 == 3 else None)
                last = mv
                if lt % 4 == 3:
                    P.wait("scalar", "m_pe", mv)
                    ov = ET_all[:, lt - 3 : lt + 1, c * 128 : (c + 1) * 128]
                    evv = P.do("scalar", lambda e, bk=bk, ov=ov: e.activation(
                        ov, bank(bk).rearrange("p (a b) -> p a b", a=4), AF.Exp), inc=("m_act", 1))
                    bankres[bk].used_until("m_act", evv)
                    exps.append(evv)
                    pump(1)
                    if len(exps) > 1:
                        P.wait("tensor", "m_act", exps[-2])
                        for l2 in range(lt - 7, lt - 3):
                            P.do("tensor", lambda e, l2=l2, c=c: e.matmul(
                                bank(4, 1, c), ET_all[:, l2, c * 128 : (c + 1) * 128], ones_col[:, :],
                                start=(l2 == 0), stop=False))
            P.wait("tensor", "m_act", exps[-1])
            for l2 in range(28, 32):
                P.do("tensor", lambda e, l2=l2, c=c: e.matmul(
                    bank(4, 1, c), ET_all[:, l2, c * 128 : (c + 1) * 128], ones_col[:, :],
                    start=False, stop=(l2 == 31)), inc=("m_pe", 1) if l2 == 31 else None)
            kt_res[kb].used_until("m_pe", last)
            if s * NPAIR + c + 2 < len(pairs):
                load_kt(s * NPAIR + c + 2)

        P.wait("tensor", "m_act", P.val("m_act"))
        for j in range(2):
            bankres[j].acquire("tensor")
        for vt in range(8):
            b = vt % 2
            ve_res[b].acquire("sync")
            srcv = v_sc[s, vt * 512 : (vt + 1) * 512, :].rearrange("(g p) d -> p g d", p=128)
            lv = P.do("sync", lambda e, a=srcv, b=b: e.dma_start(out=v4[b][:, :, :], in_=a), inc=(f"s_v{b}", 16))
            P.wait("tensor", f"s_v{b}", lv)
            mv = None
            for g in range(NG):
                lt = vt * 4 + g
                for c in range(NPAIR):
                    mv = P.do("tensor", lambda e, lt=lt, g=g, c=c, b=b: e.matmul(
                        psA[:, (c // 4) * 512 + (c % 4) * 128 : (c // 4) * 512 + (c % 4) * 128 + 128],
                        ET_all[:, lt, c * 128 : (c + 1) * 128], v4[b][:, g, c * 128 : (c + 1) * 128],
                        start=(lt == 0), stop=(lt == 31)),
                        inc=("m_pe", 1) if (c == NPAIR - 1 and g == NG - 1) else None)
            ve_res[b].used_until("m_pe", mv)
            pump(2)
        prev_sweep = P.val("m_pe")

        P.wait("vector", "m_act", P.val("m_act"))
        sv = None
        for c in range(NPAIR):
            pb = psA[:, (c // 4) * 512 + (c % 4) * 128 : (c // 4) * 512 + (c % 4) * 128 + 128]
            P.wait("vector", "m_pe", prev_sweep)
            rv = P.do("vector", lambda e, c=c: e.reciprocal(s3inv[:, c : c + 1], bank(4, 1, c)), inc=("m_dve", 1))
            P.wait("scalar", "m_dve", rv)
            P.wait("scalar", "m_pe", prev_sweep)
            P.do("scalar", lambda e, pb=pb, s=s, c=c: e.activation(s3v_sb[0:64, s, c, :], pb[0:64, 0:64], AF.Identity, scale=s3inv[0:64, c : c + 1]))
            sv = P.do("scalar", lambda e, pb=pb, s=s, c=c: e.activation(s3v_sb[64:128, s, c, :], pb[64:128, 64:128], AF.Identity, scale=s3inv[64:128, c : c + 1]), inc=("m_act", 1))
            if c == NPAIR - 1:
                bankres[4].used_until("m_dve", P.val("m_dve"))
                bankres[4].used_until("m_act", sv)
                for j in range(2):
                    bankres[j].used_until("m_act", sv)
            pump(1)

    pump(10 ** 6)

    P.wait("tensor", "m_act", P.val("m_act"))
    P.wait("tensor", "m_dve", P.val("m_dve"))
    for s in range(SB):
        for c in range(NPAIR):
            bankres[5].acquire("tensor")
            P.do("tensor", lambda e, s=s, c=c: e.matmul(bank(5, 64, 0)[0:64, :], ZT_st[0:64, s, c, :], s3v_sb[0:64, s, c, :], start=True, stop=True))
            gm = P.do("tensor", lambda e, s=s, c=c: e.matmul(bank(5, 64, 64)[64:128, :], ZT_st[64:128, s, c, :], s3v_sb[64:128, s, c, :], start=True, stop=True), inc=("m_pe", 1))
            P.wait("vector", "m_pe", gm)
            P.do("vector", lambda e, s=s, c=c: e.tensor_copy(G2_bd[0:64, s, c, 0:64], bank(5, 64, 0)[0:64, :]))
            ge = P.do("vector", lambda e, s=s, c=c: e.tensor_copy(G2_bd[64:128, s, c, 64:128], bank(5, 64, 64)[64:128, :]), inc=("m_dve", 1))
            bankres[5].used_until("m_dve", ge)


def p3a(P, ns):
    qT_t3, x_t3, attnoT = ns["qT_t3"], ns["x_t3"], ns["attnoT"]
    E1all = ns["E1all"]
    nlv, invb, E1n2 = ns["nlv"], ns["invb"], ns["E1n2"]
    x2_sb, sq_scr = ns["x2_sb"], ns["sq_scr"]
    klT_bd, G2_bd = ns["klT_bd"], ns["G2_bd"]
    Wo_sb, bo_bf = ns["Wo_sb"], ns["bo_bf"]
    q_sc, x_in, x2_sc = ns["q_sc"], ns["x_in"], ns["x2_sc"]
    rstd2, nmr2 = ns["rstd2"], ns["nmr2"]
    e_sum, ones_row = ns["e_sum"], ns["ones_row"]
    bank, bankres, res, sc = ns["bank"], ns["bankres"], ns["res"], ns["sc"]

    P.wait("sync", "p_q", P.val("p_q"))
    P.wait("sync", "m_pe", P.val("m_pe"))
    P.wait("sync", "m_dve", P.val("m_dve"))
    P.wait("sync", "m_act", P.val("m_act"))
    stat4, eps_c = ns["stat4"], ns["eps_c"]
    st_prev = {"g4": 0, "exp": 0, "nmr": 0}
    pending_tail = None

    def stats_tail(pend):
        s_, it_, smv_ = pend
        P.wait("vector", "m_act", smv_)
        P.wait("vector", "m_gp", st_prev["g4"])
        nm2 = P.do("vector", lambda e: e.tensor_tensor(stat4[:, 8:12], stat4[:, 4:8], stat4[:, 4:8], ALU.mult), inc=("m_dve", 1))
        P.wait("gpsimd", "m_dve", nm2)
        P.do("gpsimd", lambda e: e.tensor_scalar_mul(stat4[:, 8:12], stat4[:, 8:12], 1.0 / (1024.0 * 1024.0)))
        P.do("gpsimd", lambda e: e.tensor_scalar_mul(stat4[:, 12:16], stat4[:, 0:4], 1.0 / 1024.0))
        g4v = P.do("gpsimd", lambda e: e.tensor_sub(stat4[:, 12:16], stat4[:, 12:16], stat4[:, 8:12]), inc=("m_gp", 1))
        P.wait("scalar", "m_gp", g4v)
        lnv2 = P.do("scalar", lambda e: e.activation(stat4[:, 16:20], stat4[:, 12:16], AF.Ln, bias=eps_c[:, :]), inc=("m_act", 1))
        P.wait("vector", "m_act", lnv2)
        mhv = P.do("vector", lambda e: e.tensor_scalar(stat4[:, 20:24], stat4[:, 16:20], -0.5, None, op0=ALU.mult), inc=("m_dve", 1))
        P.wait("scalar", "m_dve", mhv)
        expv = P.do("scalar", lambda e, s=s_, it=it_: e.activation(
            rstd2[:, s, it * NG : (it + 1) * NG], stat4[:, 20:24], AF.Exp), inc=("m_act", 1))
        P.wait("vector", "m_act", expv)
        nmrv = P.do("vector", lambda e, s=s_, it=it_: e.scalar_tensor_tensor(
            nmr2[:, s, it * NG : (it + 1) * NG], stat4[:, 4:8], -1.0 / 1024.0,
            rstd2[:, s, it * NG : (it + 1) * NG], op0=ALU.mult, op1=ALU.mult), inc=("m_dve", 1))
        return {"g4": g4v, "exp": expv, "nmr": nmrv}
    q_res = [res("q3_0"), res("q3_1")]
    x3_res = res("x3")
    x2_res = [res(f"x2s_{j}") for j in range(4)]
    chain_res = [res(f"p3a_ch{j}") for j in range(4)]

    NTILES = SB * NT

    def load_q(i):
        s, it = divmod(i, NT)
        buf = i % 2
        q_res[buf].acquire("sync")
        return P.do("sync", lambda e, s=s, it=it, b=buf: e.dma_start(
            out=qT_t3[b][:, :, :], in_=q_sc[s, it].rearrange("c p t -> p c t")), inc=(f"s_q{buf}", 16))

    def s1(i, qv, prev_e1ns=None):
        s, it = divmod(i, NT)
        buf = i % 2
        P.wait("tensor", f"s_q{buf}", qv)
        exps = []
        for c in range(NPAIR):
            bsc = c % 2
            bankres[bsc].acquire("tensor")
            mv = P.do("tensor", lambda e, s=s, c=c, b=buf, bk=bsc: e.matmul(
                bank(bk), klT_bd[:, s, c, :], qT_t3[b][:, c, :], start=True, stop=True), inc=("m_pe", 1))
            P.wait("scalar", "m_pe", mv)
            if prev_e1ns is not None:
                P.wait("scalar", "m_dve", prev_e1ns[c])
            ex = P.do("scalar", lambda e, c=c, bk=bsc: e.activation(E1all[:, c, :], bank(bk), AF.Exp), inc=("m_act", 1))
            bankres[bsc].used_until("m_act", ex)
            exps.append(ex)
        q_res[buf].used_until("m_pe", P.val("m_pe"))
        return exps

    def stage2(i, exps):
        s, it = divmod(i, NT)
        e1ns = []
        for c in range(NPAIR):
            bs = c % 2
            cb_ = c % 4
            bankres[bs].acquire("tensor")
            P.wait("tensor", "m_act", exps[c])
            sv = P.do("tensor", lambda e, c=c, bk=bs: e.matmul(bank(bk), e_sum[:, :], E1all[:, c, :], start=True, stop=True), inc=("m_pe", 1))
            chain_res[cb_].acquire("scalar")
            P.wait("scalar", "m_pe", sv)
            lv = P.do("scalar", lambda e, bk=bs, b=cb_: e.activation(nlv[b][:, :], bank(bk), AF.Ln), inc=("m_act", 1))
            bankres[bs].used_until("m_act", lv)
            ie = P.do("scalar", lambda e, b=cb_: e.activation(invb[b][:, :], nlv[b][:, :], AF.Exp, scale=-1.0), inc=("m_act", 1))
            P.wait("vector", "m_act", ie)
            nv = P.do("vector", lambda e, c=c, b=cb_: e.tensor_tensor(E1n2[b][:, :], E1all[:, c, :], invb[b][:, :], ALU.mult), inc=("m_dve", 1))
            chain_res[cb_].used_until("m_dve", nv)
            e1ns.append(nv)
        return e1ns

    def stage3(i, e1ns):
        s, it = divmod(i, NT)
        for c in range(NPAIR):
            bat = 4 + c % 2
            bankres[bat].acquire("tensor")
            P.wait("tensor", "m_dve", e1ns[c])
            av = P.do("tensor", lambda e, s=s, c=c, b=c % 4, bk=bat: e.matmul(bank(bk), G2_bd[:, s, c, :], E1n2[b][:, :], start=True, stop=True), inc=("m_pe", 1))
            chain_res[c % 4].used_until("m_pe", av)
            P.wait("vector", "m_pe", av)
            ae = P.do("vector", lambda e, c=c, bk=bat: e.tensor_copy(attnoT[:, c, :], bank(bk)), inc=("m_dve", 1))
            bankres[bat].used_until("m_dve", ae)
        return P.val("m_dve")

    qvs = {0: load_q(0), 1: load_q(1)}
    exps = s1(0, qvs[0])
    e1ns = stage2(0, exps)

    for i in range(NTILES):
        s, it = divmod(i, NT)
        t0 = it * 512
        if i + 2 < NTILES:
            qvs[i + 2] = load_q(i + 2)
        x3_res.acquire("sync")
        xv = P.do("sync", lambda e, s=s, t0=t0: e.dma_start(
            out=x_t3[:, :, :], in_=x_in[s, t0 : t0 + 512, :].rearrange("(g p) d -> p g d", p=128)), inc=("s_x3", 16))

        attn_done = stage3(i, e1ns)

        if i + 1 < NTILES:
            nexps = s1(i + 1, qvs[i + 1], e1ns)
            e1ns = stage2(i + 1, nexps)
        if pending_tail is not None:
            st_prev = stats_tail(pending_tail)
            pending_tail = None

        P.wait("tensor", "m_dve", attn_done)
        P.wait("tensor", "s_wo", 16)
        P.wait("tensor", "s_bo", 16)
        P.wait("tensor", "s_x3", xv)
        P.wait("vector", "s_x3", xv)
        if not DBG["p3a_wo"]:
            continue
        for gi in range(NG):
            xbuf = gi
            x2_res[xbuf].acquire("vector")
            for hf in range(2):
                bk = 2 + hf
                bankres[bk].acquire("tensor")
                for kc in range(NCH):
                    P.do("tensor", lambda e, gi=gi, hf=hf, kc=kc, bk=bk: e.matmul(
                        bank(bk), attnoT[:, kc, gi * 128 : (gi + 1) * 128],
                        Wo_sb[:, kc, hf * 512 : (hf + 1) * 512],
                        start=(kc == 0), stop=(kc == NCH - 1)))
                wv = P.do("tensor", lambda e, hf=hf, bk=bk: e.matmul(
                    bank(bk), ones_row[:, :], bo_bf[:, hf * 512 : (hf + 1) * 512],
                    start=False, stop=True), inc=("m_pe", 1))
                P.wait("vector", "m_pe", wv)
                evv = P.do("vector", lambda e, gi=gi, hf=hf, bk=bk, xb=xbuf: e.scalar_tensor_tensor(
                    x2_sb[xb][:, hf * 512 : (hf + 1) * 512], bank(bk), 1.0,
                    x_t3[:, gi, hf * 512 : (hf + 1) * 512], op0=ALU.mult, op1=ALU.add), inc=("m_dve", 1))
                bankres[bk].used_until("m_dve", evv)
            x2v = P.val("m_dve")
            P.wait("gpsimd", "m_dve", x2v)
            dst = x2_sc[s, t0 + gi * 128 : t0 + (gi + 1) * 128, :]
            sp = P.do("gpsimd", lambda e, d=dst, xb=xbuf: e.dma_start(out=d, in_=x2_sb[xb][:, :]), inc=("p_x2", 16))
            x2_res[xbuf].used_until("p_x2", sp)
            if gi == 0:
                P.wait("scalar", "m_gp", st_prev["g4"])
                P.wait("scalar", "m_dve", st_prev["nmr"])
            P.wait("scalar", "m_dve", x2v)
            P.do("scalar", lambda e, g=gi, xb=xbuf: e.activation(
                sq_scr[:, :], x2_sb[xb][:, :], AF.Square, accum_out=stat4[:, g : g + 1]), inc=("m_act", 1))
            smv = P.do("scalar", lambda e, g=gi, xb=xbuf: e.activation(
                sq_scr[:, :], x2_sb[xb][:, :], AF.Identity, accum_out=stat4[:, 4 + g : 5 + g]), inc=("m_act", 1))
            x2_res[xbuf].used_until("m_act", smv)
        pending_tail = (s, it, smv)
        x3_res.used_until("m_dve", P.val("m_dve"))
    stats_tail(pending_tail)


def p3b(P, ns):
    x2_t, x2n_bf, x2nT, h_sb, out_sb, silu_g = ns["x2_t"], ns["x2n_bf"], ns["x2nT"], ns["h_sb"], ns["out_sb"], ns["silu_g"]
    wg_st, wu_st, wdT_sb = ns["wg_st"], ns["wu_st"], ns["wdT_sb"]
    x2_sc, wgt_sc, wut_sc, y_out = ns["x2_sc"], ns["wgt_sc"], ns["wut_sc"], ns["y_out"]
    rstd2, nmr2, ident = ns["rstd2"], ns["nmr2"], ns["ident"]
    bank, bankres, psBres, psB, res = ns["bank"], ns["bankres"], ns["psBres"], ns["psB"], ns["res"]

    P.wait("sync", "p_x2", P.val("p_x2"))
    P.wait("sync", "p_wgs", P.val("p_wgs"))
    P.wait("sync", "m_pe", P.val("m_pe"))
    P.wait("sync", "m_dve", P.val("m_dve"))
    P.wait("sync", "m_act", P.val("m_act"))

    x2_res = [res("x2t_0"), res("x2t_1")]
    wg_res = [res("wgst_0"), res("wgst_1")]
    wu_res = [res("wust_0"), res("wust_1")]
    o_res = [res("osb_0"), res("osb_1")]
    silu_res = res("silu")
    xnT_res = res("x2nT")
    rstd2, nmr2 = ns["rstd2"], ns["nmr2"]
    last_tr = P.val("m_pe")
    NTILES3 = SB * NT

    def load_x2(idx):
        s, it = divmod(idx, NT)
        t0 = it * 512
        b = idx % 2
        x2_res[b].acquire("sync")
        return P.do("sync", lambda e, s=s, t0=t0, b=b: e.dma_start(
            out=x2_t[b][:, :, :], in_=x2_sc[s, t0 : t0 + 512, :].rearrange("(g p) d -> p g d", p=128)), inc=(f"s_x2{b}", 16))

    xvs = {0: load_x2(0)}
    xns = {}
    last_tr_b = [0, 0, 0, 0]

    def norms(idx2):
        s2_, it2 = divmod(idx2, NT)
        b2 = idx2 % 2
        P.wait("scalar", f"s_x2{b2}", xvs[idx2])
        vs = []
        for gi in range(NG):
            gidx = it2 * NG + gi
            P.wait("scalar", "m_pe", last_tr_b[gi])
            xn = P.do("scalar", lambda e, b=b2, gi=gi, s=s2_, gidx=gidx: e.activation(
                x2n_bf[gi][:, :], x2_t[b][:, gi, :], AF.Identity,
                bias=nmr2[:, s, gidx : gidx + 1], scale=rstd2[:, s, gidx : gidx + 1]), inc=("m_act", 1))
            vs.append(xn)
        xns[idx2] = vs

    def tr_block(idx2):
        out = None
        for gi in range(NG):
            P.wait("tensor", "m_act", xns[idx2][gi])
            for c in range(NCH):
                half = c // 4
                if c % 4 == 0:
                    psBres[half].acquire("tensor")
                pslot = psB[:, half * 1024 + (c % 4) * 128 : half * 1024 + (c % 4) * 128 + 128]
                tv = P.do("tensor", lambda e, gi=gi, c=c, p=pslot: e.transpose(p, x2n_bf[gi][:, c * 128 : (c + 1) * 128], ident[:, :]),
                          inc=("m_pe", 1) if c % 4 == 3 else None)
                if c % 4 == 3:
                    last_tr_b[gi] = tv
                    P.wait("vector", "m_pe", tv)
                    pv = psB[:, half * 1024 : half * 1024 + 512].rearrange("p (c t) -> p c t", c=4)
                    ov = x2nT[:, c - 3 : c + 1, gi * 128 : (gi + 1) * 128]
                    evv = P.do("vector", lambda e, pv=pv, ov=ov: e.tensor_copy(ov, pv), inc=("m_dve", 1))
                    psBres[half].used_until("m_dve", evv)
                    out = evv
        return out

    for s in range(SB):
        for it in range(NT):
            t0 = it * 512
            idx = s * NT + it
            buf = idx % 2

            if idx == 0:
                norms(0)
            xt_done = tr_block(idx)
            P.wait("tensor", "m_dve", xt_done)

            for hc in range(NHC):
                wb = hc % 2
                wg_res[wb].acquire("sync")
                wgv = P.do("sync", lambda e, hc=hc, b=wb: e.dma_start(out=wg_st[b][:, :, :], in_=wgt_sc[hc]), inc=(f"s_wg{wb}", 16))
                wu_res[wb].acquire("sync")
                wuv = P.do("sync", lambda e, hc=hc, b=wb: e.dma_start(out=wu_st[b][:, :, :], in_=wut_sc[hc]), inc=(f"s_wu{wb}", 16))
                if hc == 6 and idx + 1 < NTILES3:
                    xvs[idx + 1] = load_x2(idx + 1)
                P.wait("tensor", f"s_wg{wb}", wgv)
                bg, bu = hc % 2, 2 + hc % 2
                bankres[bg].acquire("tensor")
                gv = None
                for kc in range(NCH):
                    gv = P.do("tensor", lambda e, kc=kc, b=wb, bk=bg: e.matmul(
                        bank(bk), wg_st[b][:, kc, :], x2nT[:, kc, :], start=(kc == 0), stop=(kc == NCH - 1)),
                        inc=("m_pe", 1) if kc == NCH - 1 else None)
                wg_res[wb].used_until("m_pe", gv)
                P.wait("tensor", f"s_wu{wb}", wuv)
                bankres[bu].acquire("tensor")
                uv = None
                for kc in range(NCH):
                    uv = P.do("tensor", lambda e, kc=kc, b=wb, bk=bu: e.matmul(
                        bank(bk), wu_st[b][:, kc, :], x2nT[:, kc, :], start=(kc == 0), stop=(kc == NCH - 1)),
                        inc=("m_pe", 1) if kc == NCH - 1 else None)
                wu_res[wb].used_until("m_pe", uv)
                silu_res.acquire("scalar")
                P.wait("scalar", "m_pe", gv)
                sg = P.do("scalar", lambda e, bk=bg: e.activation(silu_g[:, :], bank(bk), AF.Silu, scale=S127), inc=("m_act", 1))
                bankres[bg].used_until("m_act", sg)
                P.wait("vector", "m_pe", uv)
                P.wait("vector", "m_act", sg)
                hv = P.do("vector", lambda e, hc=hc, bk=bu: e.scalar_tensor_tensor(
                    h_sb[:, hc, :], bank(bk), 1.0, silu_g[:, :], op0=ALU.mult, op1=ALU.mult), inc=("m_dve", 1))
                bankres[bu].used_until("m_dve", hv)
                silu_res.used_until("m_dve", hv)
            h_done = P.val("m_dve")

            P.wait("tensor", "m_dve", h_done)
            for gi in range(NG):
                ob = gi % 2
                o_res[ob].acquire("vector")
                for hf in range(2):
                    bk = 4 + (gi * 2 + hf) % 2
                    bankres[bk].acquire("tensor")
                    dvv = None
                    for hc in range(NHC):
                        dvv = P.do("tensor", lambda e, gi=gi, hf=hf, hc=hc, bk=bk: e.matmul(
                            bank(bk), h_sb[:, hc, gi * 128 : (gi + 1) * 128],
                            wdT_sb[:, hc, hf * 512 : (hf + 1) * 512],
                            start=(hc == 0), stop=(hc == NHC - 1)),
                            inc=("m_pe", 1) if hc == NHC - 1 else None)
                    P.wait("vector", "m_pe", dvv)
                    evv = P.do("vector", lambda e, gi=gi, hf=hf, bk=bk, b=buf, ob=ob: e.scalar_tensor_tensor(
                        out_sb[ob][:, hf * 512 : (hf + 1) * 512], bank(bk), S127SQ,
                        x2_t[b][:, gi, hf * 512 : (hf + 1) * 512], op0=ALU.mult, op1=ALU.add), inc=("m_dve", 1))
                    bankres[bk].used_until("m_dve", evv)
                ov = P.val("m_dve")
                P.wait("gpsimd", "m_dve", ov)
                dst = y_out[s, t0 + gi * 128 : t0 + (gi + 1) * 128, :]
                sp = P.do("gpsimd", lambda e, d=dst, ob=ob: e.dma_start(out=d, in_=out_sb[ob][:, :]), inc=("p_out", 16))
                o_res[ob].used_until("p_out", sp)
            if idx + 1 < NTILES3:
                norms(idx + 1)
            x2_res[buf].used_until("m_dve", P.val("m_dve"))


def _host_consts():
    ident = np.eye(128, dtype=ml_dtypes.bfloat16)
    e_sum = np.zeros((128, 128), dtype=ml_dtypes.bfloat16)
    e_sum[0:64, 0:64] = 1
    e_sum[64:128, 64:128] = 1
    rows = np.zeros((1, 384), dtype=ml_dtypes.bfloat16)
    rows[0, 0:64] = 1
    rows[0, 192:256] = 1
    rows[0, 256:384] = 1
    brows = np.zeros((128, 256), dtype=ml_dtypes.bfloat16)
    brows[0, 0:64] = 1
    brows[64, 192:256] = 1
    pic = np.zeros((64, 1536), dtype=np.float32)
    for j, v in enumerate((7.0, 15.0, 13.0)):
        blk = v * np.eye(64, dtype=np.float32)
        for r in range(8):
            pic[:, j * 512 + r * 64 : j * 512 + (r + 1) * 64] = blk
    return ident, e_sum, rows, pic, brows


def _in_maps(x, Wq, Wk, Wv, Wo, bo, wg, wu, wd):
    ident, e_sum, rows, pic, brows = _host_consts()
    maps = []
    for cidx in range(NCORES):
        maps.append({
            "x": np.ascontiguousarray(x[cidx * SB : (cidx + 1) * SB]),
            "Wq": Wq, "Wk": Wk, "Wv": Wv, "Wo": Wo, "bo": bo,
            "wg": wg, "wu": wu, "wd": wd,
            "ident": ident, "esum": e_sum, "rows": rows, "pic": pic, "brows": brows,
        })
    return maps


def kernel(L0_attn_x, L0_attn_Wk, L0_attn_Wq, L0_attn_Wv, L0_attn_Wo, L0_attn_bo,
           L0_ffn_w_gate, L0_ffn_w_up, L0_ffn_w_down):
    x = np.ascontiguousarray(np.asarray(L0_attn_x, dtype=np.float32))
    maps = _in_maps(
        x,
        np.ascontiguousarray(np.asarray(L0_attn_Wq, dtype=np.float32)),
        np.ascontiguousarray(np.asarray(L0_attn_Wk, dtype=np.float32)),
        np.ascontiguousarray(np.asarray(L0_attn_Wv, dtype=np.float32)),
        np.ascontiguousarray(np.asarray(L0_attn_Wo, dtype=np.float32)),
        np.ascontiguousarray(np.asarray(L0_attn_bo, dtype=np.float32)),
        np.ascontiguousarray(np.asarray(L0_ffn_w_gate, dtype=np.int32)),
        np.ascontiguousarray(np.asarray(L0_ffn_w_up, dtype=np.int32)),
        np.ascontiguousarray(np.asarray(L0_ffn_w_down, dtype=np.int32)),
    )
    nc = build_program()
    res = run_bass_kernel_spmd(nc, maps, core_ids=list(range(NCORES)))
    return np.concatenate([r["y"] for r in res.results], axis=0)


# version 34
# speedup vs baseline: 1.0108x; 1.0108x over previous
import math
import contextlib
import numpy as np
import ml_dtypes

import concourse.bass as bass
import concourse.mybir as mybir
from concourse.bass_utils import run_bass_kernel_spmd

F32 = mybir.dt.float32
BF16 = mybir.dt.bfloat16
I32 = mybir.dt.int32
U8 = mybir.dt.uint8
AF = mybir.ActivationFunctionType
ALU = mybir.AluOpType
AX = mybir.AxisListType

B, L, A, H, HS, M, HID = 16, 4096, 1024, 16, 64, 64, 3072
NCORES = 8
SB = B // NCORES
NT = L // 512
NG = 4
NCH = A // 128
NHC = HID // 128
NPAIR = H // 2
S127 = float(np.float32(1.0 / 127.0))
S127SQ = float(np.float32(np.float32(1.0 / 127.0) * np.float32(1.0 / 127.0)))

PHASES = ("P1", "P2", "P3A", "P3B")
DBG = {"p3a_tiles": None, "p3a_pairs": None, "p3a_wo": True, "p3a_norm": True, "p2_nointerleave": False, "p2_nopinv": False}


class Prog:

    ENGS = ("sync", "vector", "scalar", "tensor", "gpsimd")

    def __init__(self, nc, es):
        self.nc = nc
        self.es = es
        self.ops = {e: [] for e in self.ENGS}
        self.last_wait = {}
        self.sems = {}
        self.cnt = {}

    def sem(self, name):
        if name not in self.sems:
            self.sems[name] = self.es.enter_context(self.nc.semaphore(name))
            self.cnt[name] = 0
        return self.sems[name]

    def val(self, name):
        self.sem(name)
        return self.cnt[name]

    def do(self, eng, fn, inc=None):
        if inc is None:
            self.ops[eng].append(fn)
            return None
        name, n = inc
        sem = self.sem(name)
        self.cnt[name] += n
        self.ops[eng].append(lambda e: fn(e).then_inc(sem, n))
        return self.cnt[name]

    def wait(self, eng, name, v):
        if v is None or v <= 0:
            return
        assert v <= self.cnt.get(name, 0), f"wait on future value {name}={v}"
        key = (eng, name)
        if self.last_wait.get(key, 0) >= v:
            return
        self.last_wait[key] = v
        sem = self.sem(name)
        self.ops[eng].append(lambda e: e.wait_ge(sem, v))

    def emit(self, block):
        for ename in self.ENGS:
            fns = self.ops[ename]
            if not fns:
                continue

            def runner(eng, fns=fns):
                for f in fns:
                    f(eng)

            getattr(block, ename)(runner)


class Res:

    def __init__(self, prog):
        self.p = prog
        self.rel = []

    def acquire(self, eng):
        for name, v in self.rel:
            self.p.wait(eng, name, v)
        self.rel = []

    def used_until(self, name, v):
        if v:
            self.rel.append((name, v))


def _carver(arena):
    def carve(off, shape, dtype):
        per = math.prod(shape[1:]) * mybir.dt.size(dtype)
        v = arena[0 : shape[0], off : off + per].bitcast(dtype)
        if len(shape) > 2:
            names = [f"d{i}" for i in range(1, len(shape))]
            v = v.rearrange(
                "p (" + " ".join(names) + ") -> p " + " ".join(names),
                **{n: shape[i + 1] for i, n in enumerate(names)},
            )
        return v

    return carve


def build_program(phases=PHASES, debug_spills=False):
    nc = bass.Bass("TRN2", target_bir_lowering=False)
    ns = {}

    x_in = nc.dram_tensor("x", [SB, L, A], F32, kind="ExternalInput").ap()
    Wq_d = nc.dram_tensor("Wq", [A, A], F32, kind="ExternalInput").ap()
    Wk_d = nc.dram_tensor("Wk", [A, A], F32, kind="ExternalInput").ap()
    Wv_d = nc.dram_tensor("Wv", [A, A], F32, kind="ExternalInput").ap()
    Wo_d = nc.dram_tensor("Wo", [A, A], F32, kind="ExternalInput").ap()
    bo_d = nc.dram_tensor("bo", [A], F32, kind="ExternalInput").ap()
    wg_d = nc.dram_tensor("wg", [HID, A], I32, kind="ExternalInput").ap()
    wu_d = nc.dram_tensor("wu", [HID, A], I32, kind="ExternalInput").ap()
    wd_d = nc.dram_tensor("wd", [A, HID], I32, kind="ExternalInput").ap()
    ident_d = nc.dram_tensor("ident", [128, 128], BF16, kind="ExternalInput").ap()
    esum_d = nc.dram_tensor("esum", [128, 128], BF16, kind="ExternalInput").ap()
    rows_d = nc.dram_tensor("rows", [1, 384], BF16, kind="ExternalInput").ap()
    brows_d = nc.dram_tensor("brows", [128, 256], BF16, kind="ExternalInput").ap()
    pic_d = nc.dram_tensor("pic", [64, 1536], F32, kind="ExternalInput").ap()

    spk = "ExternalOutput" if debug_spills else "Internal"
    q_sc = nc.dram_tensor("q_sc", [SB, NT, NCH, 128, 512], BF16, kind=spk).ap()
    k_sc = nc.dram_tensor("k_sc", [SB, NCH, 128, L], BF16, kind=spk).ap()
    v_sc = nc.dram_tensor("v_sc", [SB, L, A], BF16, kind=spk).ap()
    x2_sc = nc.dram_tensor("x2_sc", [SB, L, A], F32, kind=spk).ap()
    wgt_sc = nc.dram_tensor("wgt_sc", [NHC, 128, NCH, 128], BF16, kind=spk).ap()
    wut_sc = nc.dram_tensor("wut_sc", [NHC, 128, NCH, 128], BF16, kind=spk).ap()
    y_out = nc.dram_tensor("y", [SB, L, A], F32, kind="ExternalOutput").ap()
    dbg = {}
    if debug_spills:
        for nm, shape, dt in (
            ("lmq", [128, SB, NPAIR, 64], BF16), ("lmk", [128, SB, NPAIR, 64], BF16),
            ("g2", [128, SB, NPAIR, 128], BF16), ("rstd", [128, SB, 32], F32),
            ("nmr", [128, SB, 32], F32), ("xnl", [128, SB, NCH, 64], F32),
            ("zt", [128, SB, NPAIR, 64], BF16), ("s3v", [128, SB, NPAIR, 64], BF16),
        ):
            dbg[nm] = nc.dram_tensor("dbg_" + nm, shape, dt, kind="ExternalOutput").ap()

    es = contextlib.ExitStack()
    with es:
        P = Prog(nc, es)

        arena_w = es.enter_context(nc.sbuf_tensor("arena_w", [128, 49152], U8))
        cw = _carver(arena_w)
        Wq_sb = cw(0, [128, NCH, 1024], BF16)
        Wk_sb = cw(16384, [128, NCH, 1024], BF16)
        Wv_sb = cw(32768, [128, NCH, 1024], BF16)
        wdT_sb = cw(0, [128, NHC, 1024], BF16)
        Wo_sb = es.enter_context(nc.sbuf_tensor("Wo_sb", [128, NCH, 1024], BF16))

        arena_b = es.enter_context(nc.sbuf_tensor("arena_b", [128, 98304], U8))
        cb = _carver(arena_b)
        x_t = [cb(0 + i * 16384, [128, NG, 1024], F32) for i in range(2)]
        xn_bf4 = cb(32768, [128, NG, 1024], BF16)
        xnT_sb = cb(40960, [128, NCH, 512], BF16)
        qT_st = [cb(49152, [128, NCH, 512], BF16)]
        kT_st = [cb(57344, [128, NCH, 512], BF16)]
        v_st = [cb(65536, [128, NG, 1024], BF16)]
        NI = SB * NPAIR
        qlbd_all = cw(0, [128, NI, 128], BF16)
        s2e_all = cw(4096, [128, NI, 64], F32)
        s2bf_all = cw(8192, [128, NI, 64], BF16)
        s2T_all = cw(10240, [64, NI, 128], BF16)
        Z_all = cw(14336, [64, NI, 128], BF16)
        ZT_all = cw(18432, [64, NI, 128], BF16)
        XZb_all = cw(22528, [64, NI, 128], BF16)
        XZT_all = cw(26624, [64, NI, 128], BF16)
        T_all = cw(30720, [64, NI, 128], BF16)
        zscb = cw(34816, [64, 32], F32)
        csbf_all = cw(35072, [128, 16], BF16)
        ET_all = cb(0, [128, 32, 1024], BF16)
        kT_pair = [cb(73728 + i * 8192, [128, L], BF16) for i in range(2)]
        v4 = [cb(65536, [128, NG, 1024], BF16), cb(90112, [128, NG, 1024], BF16)]
        qT_t3 = [cb(0 + i * 8192, [128, NCH, 512], BF16) for i in range(2)]
        x_t3 = cb(16384, [128, NG, 1024], F32)
        attnoT = cb(32768, [128, NCH, 512], BF16)
        E1all = cb(40960, [128, NPAIR, 512], BF16)
        nlv = [cb(49152 + i * 2048, [128, 512], F32) for i in range(4)]
        invb = [cb(57344 + i * 1024, [128, 512], BF16) for i in range(4)]
        E1n2 = [cb(61440 + i * 1024, [128, 512], BF16) for i in range(4)]
        x2_sb = [cb(65536 + i * 4096, [128, 1024], F32) for i in range(4)]
        x2_t = [cb(0 + i * 16384, [128, NG, 1024], F32) for i in range(2)]
        x2n_bf = [cb(81920 + i * 2048, [128, 1024], BF16) for i in range(4)]
        x2nT = cb(34816, [128, NCH, 512], BF16)
        h_sb = cb(43008, [128, NHC, 512], BF16)
        out_sb = [cb(67584 + i * 4096, [128, 1024], F32) for i in range(2)]
        silu_g = cb(75776, [128, 512], BF16)

        wg_st = [es.enter_context(nc.sbuf_tensor(f"wg_st{i}", [128, NCH, 128], BF16)) for i in range(2)]
        wu_st = [es.enter_context(nc.sbuf_tensor(f"wu_st{i}", [128, NCH, 128], BF16)) for i in range(2)]

        ident = es.enter_context(nc.sbuf_tensor("ident_s", [128, 128], BF16))
        e_sum = es.enter_context(nc.sbuf_tensor("e_sum_s", [128, 128], BF16))
        rows_s = es.enter_context(nc.sbuf_tensor("rows_s", [1, 384], BF16))
        brows = es.enter_context(nc.sbuf_tensor("brows_s", [128, 256], BF16))
        pic_s = es.enter_context(nc.sbuf_tensor("pic_s", [64, 1536], F32))
        bo_bf = es.enter_context(nc.sbuf_tensor("bo_bf", [1, 1024], BF16))
        sq_scr = es.enter_context(nc.sbuf_tensor("sq_scr", [128, 1024], F32))
        stat4 = es.enter_context(nc.sbuf_tensor("stat4", [128, 24], F32))
        xnl = es.enter_context(nc.sbuf_tensor("xnl", [128, SB, NCH, 64], F32))
        xnl_bf = es.enter_context(nc.sbuf_tensor("xnl_bf", [128, SB, NCH, 64], BF16))
        qlT = es.enter_context(nc.sbuf_tensor("qlT", [128, SB, NPAIR, 64], BF16))
        klT = es.enter_context(nc.sbuf_tensor("klT", [128, SB, NPAIR, 64], BF16))
        klT_bd = es.enter_context(nc.sbuf_tensor("klT_bd", [128, SB, NPAIR, 128], BF16))
        qlT_bd = es.enter_context(nc.sbuf_tensor("qlT_bd", [128, 128], BF16))
        G2_bd = es.enter_context(nc.sbuf_tensor("G2_bd", [128, SB, NPAIR, 128], BF16))
        ZT_st = es.enter_context(nc.sbuf_tensor("ZT_st", [128, SB, NPAIR, 64], BF16))
        rstd2 = es.enter_context(nc.sbuf_tensor("rstd2", [128, SB, 32], F32))
        nmr2 = es.enter_context(nc.sbuf_tensor("nmr2", [128, SB, 32], F32))
        scal = es.enter_context(nc.sbuf_tensor("scal", [128, 64], F32))
        eps_c = es.enter_context(nc.sbuf_tensor("eps_c", [128, 1], F32))
        s2e = es.enter_context(nc.sbuf_tensor("s2e", [128, 64], F32))
        s2bf = es.enter_context(nc.sbuf_tensor("s2bf", [128, 64], BF16))
        cs_bf = es.enter_context(nc.sbuf_tensor("cs_bf", [128, 1], BF16))
        zsc_bf = es.enter_context(nc.sbuf_tensor("zsc_bf", [1, 2], BF16))
        s2T = es.enter_context(nc.sbuf_tensor("s2T", [64, 128], BF16))
        Z_f = es.enter_context(nc.sbuf_tensor("Z_f", [64, 128], BF16))
        XZ_bf = es.enter_context(nc.sbuf_tensor("XZ_bf", [64, 128], BF16))
        XZT_f = es.enter_context(nc.sbuf_tensor("XZT_f", [64, 128], BF16))
        ZTf = es.enter_context(nc.sbuf_tensor("ZTf", [64, 128], BF16))
        T_f = es.enter_context(nc.sbuf_tensor("T_f", [64, 128], BF16))
        tiny = es.enter_context(nc.sbuf_tensor("tiny", [64, 64], F32))
        tiny2 = es.enter_context(nc.sbuf_tensor("tiny2", [1, 64], F32))
        zsc_bf32 = es.enter_context(nc.sbuf_tensor("zsc_bf32", [1, 32], BF16))
        ones_col = es.enter_context(nc.sbuf_tensor("ones_col", [128, 1], BF16))
        s3inv = es.enter_context(nc.sbuf_tensor("s3inv", [128, 8], F32))
        s3v_sb = es.enter_context(nc.sbuf_tensor("s3v_sb", [128, SB, NPAIR, 64], BF16))

        psA = es.enter_context(nc.psum_tensor("psA", [128, 3072], F32))
        psB = es.enter_context(nc.psum_tensor("psB", [128, 2048], BF16))

        def bank(i, cols=512, off=0):
            return psA[:, i * 512 + off : i * 512 + off + cols]

        def sc(i):
            return scal[:, i : i + 1]

        bankres = [Res(P) for _ in range(7)]
        psBres = [Res(P), Res(P)]
        resources = {}

        def res(name):
            if name not in resources:
                resources[name] = Res(P)
            return resources[name]

        ns.update(locals())
        ns["ln_rel"] = {}

        P.do("vector", lambda e: e.memset(eps_c[:, :], 1e-5), inc=("m_dve", 1))
        ns["eps_gate"] = P.val("m_dve")
        P.do("gpsimd", lambda e: e.memset(ones_col[:, :], 1.0), inc=("m_gp", 1))
        ns["ones_gate"] = P.val("m_gp")
        for ap_d, sb in ((ident_d, ident), (esum_d, e_sum), (rows_d, rows_s), (pic_d, pic_s), (brows_d, brows)):
            P.do("sync", lambda e, a=ap_d, s=sb: e.dma_start(out=s[:, :], in_=a[:, :]), inc=("s_c", 16))
        ns["C_TOTAL"] = P.val("s_c")
        wsem = {}
        for nm, ap_d, sb in (("wq", Wq_d, Wq_sb), ("wk", Wk_d, Wk_sb), ("wv", Wv_d, Wv_sb), ("wo", Wo_d, Wo_sb)):
            srcw = ap_d.rearrange("(kc p) n -> p kc n", p=128)
            wsem[nm] = P.do("gpsimd", lambda e, a=srcw, s=sb: e.dma_start(out=s[:, :, :], in_=a), inc=(f"s_{nm}", 16))
        wsem["bo"] = P.do("gpsimd", lambda e: e.dma_start(out=bo_bf[:, :], in_=bo_d.rearrange("(a n) -> a n", a=1)), inc=("s_bo", 16))
        ns["wsem"] = wsem
        ns["W1_TOTAL"] = 0
        ns["ea_row"] = rows_s[:, 0:128]
        ns["eb_row"] = rows_s[:, 128:256]
        ns["ones_row"] = rows_s[:, 256:384]

        if "P1" in phases:
            p1(P, ns)
        if "P2" in phases:
            p2a(P, ns)
            p2b(P, ns)
        if "P3A" in phases:
            p3a(P, ns)
        if "P3B" in phases:
            p3b(P, ns)
        if debug_spills:
            P.wait("gpsimd", "m_dve", P.val("m_dve"))
            P.wait("gpsimd", "m_act", P.val("m_act"))
            pairs = [("lmq", qlT), ("lmk", klT), ("g2", G2_bd), ("rstd", rstd2), ("nmr", nmr2), ("xnl", xnl), ("zt", ZT_st), ("s3v", s3v_sb)]
            for nm, t in pairs:
                if nm in dbg:
                    ap = dbg[nm]
                    P.do("gpsimd", lambda e, a=ap, t=t: e.dma_start(out=a, in_=t[:]), inc=("p_dbg", 16))

        for name in ("p_q", "p_k", "p_v", "p_x2", "p_wgs", "p_out", "p_dbg"):
            P.wait("gpsimd", name, P.cnt.get(name, 0))

        block = es.enter_context(nc.Block())
        P.emit(block)
    return nc



def emit_ln_batch(P, ns, items, a5_gate=None):
    sc, sq_scr = ns["sc"], ns["sq_scr"]
    SUMSQ, NEGSUM, NEGMEAN, NM2, VAR, LNV, MH, RSTD, NMR = range(9)

    def slot(it_, k):
        return sc(it_[4] * 16 + k)

    for it_ in items:
        rel = ns["ln_rel"].get(it_[4])
        if rel:
            for eng in ("scalar", "vector", "gpsimd"):
                P.wait(eng, *rel)
    a1 = d1 = None
    sq_eng = ns.get("ln_sq_eng", "scalar")
    for it_ in items:
        if sq_eng == "scalar":
            a1 = P.do("scalar", lambda e, x=it_[0], o=slot(it_, SUMSQ): e.activation(sq_scr[:, :], x, AF.Square, accum_out=o),
                      inc=("m_act", 1) if it_ is items[-1] else None)
        else:
            a1 = P.do("vector", lambda e, x=it_[0], o=slot(it_, SUMSQ): e.scalar_tensor_tensor(sq_scr[:, :], x, 1.0, x, op0=ALU.mult, op1=ALU.mult, accum_out=o),
                      inc=("m_dve", 1) if it_ is items[-1] else None)
    a1sem = "m_act" if sq_eng == "scalar" else "m_dve"
    for it_ in items:
        d1 = P.do("vector", lambda e, x=it_[0], o=slot(it_, NEGSUM): e.reduce_sum(o, x, axis=AX.X, negate=True),
                  inc=("m_dve", 1) if it_ is items[-1] else None)
    P.wait("scalar", "m_dve", d1)
    a2 = None
    for it_ in items:
        a2 = P.do("scalar", lambda e, i=slot(it_, NEGSUM), o=slot(it_, NEGMEAN): e.activation(o, i, AF.Identity, scale=1.0 / 1024.0),
                  inc=("m_act", 1) if it_ is items[-1] else None)
    P.wait("vector", "m_act", a2)
    d2 = None
    for it_ in items:
        d2 = P.do("vector", lambda e, i=slot(it_, NEGMEAN), o=slot(it_, NM2): e.tensor_tensor(o, i, i, ALU.mult),
                  inc=("m_dve", 1) if it_ is items[-1] else None)
    P.wait("gpsimd", a1sem, a1)
    P.wait("gpsimd", "m_dve", d2)
    g1 = None
    for it_ in items:
        P.do("gpsimd", lambda e, i=slot(it_, SUMSQ), o=slot(it_, VAR): e.tensor_scalar_mul(o, i, 1.0 / 1024.0))
        g1 = P.do("gpsimd", lambda e, i=slot(it_, NM2), o=slot(it_, VAR): e.tensor_sub(o, o, i),
                  inc=("m_gp", 1) if it_ is items[-1] else None)
    P.wait("scalar", "m_gp", g1)
    a3 = None
    for it_ in items:
        a3 = P.do("scalar", lambda e, i=slot(it_, VAR), o=slot(it_, LNV): e.activation(o, i, AF.Ln, bias=ns["eps_c"][:, :]),
                  inc=("m_act", 1) if it_ is items[-1] else None)
    P.wait("vector", "m_act", a3)
    d3 = None
    for it_ in items:
        d3 = P.do("vector", lambda e, i=slot(it_, LNV), o=slot(it_, MH): e.tensor_scalar(o, i, -0.5, None, op0=ALU.mult),
                  inc=("m_dve", 1) if it_ is items[-1] else None)
    P.wait("scalar", "m_dve", d3)
    a4 = None
    for it_ in items:
        r_out = it_[2] if it_[2] is not None else slot(it_, RSTD)
        a4 = P.do("scalar", lambda e, i=slot(it_, MH), o=r_out: e.activation(o, i, AF.Exp),
                  inc=("m_act", 1) if it_ is items[-1] else None)
    P.wait("vector", "m_act", a4)
    d4 = None
    for it_ in items:
        r_out = it_[2] if it_[2] is not None else slot(it_, RSTD)
        n_out = it_[3] if it_[3] is not None else slot(it_, NMR)
        d4 = P.do("vector", lambda e, i=slot(it_, NEGMEAN), r=r_out, o=n_out: e.tensor_tensor(o, i, r, ALU.mult),
                  inc=("m_dve", 1) if it_ is items[-1] else None)
    outs = []
    if any(it_[1] is not None for it_ in items):
        P.wait("scalar", "m_dve", d4)
        if a5_gate:
            P.wait("scalar", *a5_gate)
        for it_ in items:
            r_out = it_[2] if it_[2] is not None else slot(it_, RSTD)
            n_out = it_[3] if it_[3] is not None else slot(it_, NMR)
            a5 = P.do("scalar", lambda e, x=it_[0], o=it_[1], r=r_out, n=n_out: e.activation(o, x, AF.Identity, bias=n, scale=r),
                      inc=("m_act", 1))
            outs.append(a5)
            ns["ln_rel"][it_[4]] = ("m_act", a5)
    else:
        for it_ in items:
            ns["ln_rel"][it_[4]] = ("m_dve", d4)
        outs = [d4] * len(items)
    return outs


def p1(P, ns):
    x_in, x_t, xn_bf4, xnT_sb = ns["x_in"], ns["x_t"], ns["xn_bf4"], ns["xnT_sb"]
    qT_st, kT_st, v_st = ns["qT_st"], ns["kT_st"], ns["v_st"]
    Wq_sb, Wk_sb, Wv_sb = ns["Wq_sb"], ns["Wk_sb"], ns["Wv_sb"]
    q_sc, k_sc, v_sc = ns["q_sc"], ns["k_sc"], ns["v_sc"]
    ident, xnl = ns["ident"], ns["xnl"]
    psB = ns["psB"]
    bank, bankres, psBres, res = ns["bank"], ns["bankres"], ns["psBres"], ns["res"]

    x_res = [res("p1_x0"), res("p1_x1")]
    stage_res = res("p1_st")
    xnT_res = res("p1_xnT")
    NTILES = SB * NT
    last_tr = 0

    def load_x(i):
        s, it = divmod(i, NT)
        buf = i % 2
        x_res[buf].acquire("sync")
        srcv = x_in[s, it * 512 : (it + 1) * 512, :].rearrange("(g p) d -> p g d", p=128)
        return P.do("sync", lambda e, a=srcv, b=buf: e.dma_start(out=x_t[b][:, :, :], in_=a), inc=(f"s_x{buf}", 16))

    def emit_lnb(i, xv, gate_tr):
        buf = i % 2
        P.wait("scalar", f"s_x{buf}", xv)
        P.wait("vector", f"s_x{buf}", xv)
        P.wait("scalar", "m_dve", ns["eps_gate"])
        items = [(x_t[buf][:, gi, :], xn_bf4[:, gi, :], None, None, gi) for gi in range(NG)]
        outs = emit_ln_batch(P, ns, items, a5_gate=("m_pe", gate_tr))
        x_res[buf].used_until("m_act", outs[-1])
        x_res[buf].used_until("m_dve", P.val("m_dve"))
        return outs

    xvs = {0: load_x(0), 1: load_x(1)}
    lnv = {0: emit_lnb(0, xvs[0], 0)}

    for i in range(NTILES):
        s, it = divmod(i, NT)
        t0 = it * 512
        if i + 2 < NTILES:
            xvs[i + 2] = load_x(i + 2)

        P.wait("tensor", "s_c", ns["C_TOTAL"])
        first_evac = True
        for gi in range(NG):
            P.wait("tensor", "m_act", lnv[i][gi])
            for c in range(NCH):
                half = c // 4
                if c % 4 == 0:
                    psBres[half].acquire("tensor")
                dst = psB[:, half * 1024 + (c % 4) * 128 : half * 1024 + (c % 4) * 128 + 128]
                tv = P.do("tensor", lambda e, d=dst, c=c, gi=gi: e.transpose(d, xn_bf4[:, gi, c * 128 : (c + 1) * 128], ident[:, :]),
                          inc=("m_pe", 1) if c % 4 == 3 else None)
                if c % 4 == 3:
                    last_tr = tv
                    if first_evac:
                        xnT_res.acquire("vector")
                        first_evac = False
                    P.wait("vector", "m_pe", tv)
                    pv = psB[:, half * 1024 : half * 1024 + 512].rearrange("p (c t) -> p c t", c=4)
                    ov = xnT_sb[:, c - 3 : c + 1, gi * 128 : (gi + 1) * 128]
                    evv = P.do("vector", lambda e, pv=pv, ov=ov: e.tensor_copy(ov, pv), inc=("m_dve", 1))
                    psBres[half].used_until("m_dve", evv)
        xnT_ready = P.val("m_dve")

        if i + 1 < NTILES:
            lnv[i + 1] = emit_lnb(i + 1, xvs[i + 1], last_tr)

        P.wait("vector", "m_dve", xnT_ready)
        for c in range(NCH):
            P.do("vector", lambda e, c=c, s=s, it=it: e.tensor_reduce(
                xnl[:, s, c, it * 8 : it * 8 + 8].rearrange("p (n o) -> p n o", o=1),
                xnT_sb[:, c, :].rearrange("p (n w) -> p n w", w=64),
                axis=AX.X, op=ALU.add), inc=("m_dve", 1) if c == NCH - 1 else None)

        P.wait("tensor", "s_wq", 16)
        P.wait("tensor", "s_wk", 16)
        P.wait("tensor", "s_wv", 16)
        P.wait("tensor", "m_dve", xnT_ready)
        stage = stage_res
        stage.acquire("tensor")
        for W_sb, dst_st, qscale in ((Wq_sb, qT_st[0], True), (Wk_sb, kT_st[0], False)):
            ev = None
            for c in range(NCH):
                bk = c % 4
                bankres[bk].acquire("tensor")
                mv = None
                for kc in range(NCH):
                    mv = P.do("tensor", lambda e, W=W_sb, c=c, kc=kc, bk=bk: e.matmul(
                        bank(bk), W[:, kc, c * 128 : (c + 1) * 128], xnT_sb[:, kc, :],
                        start=(kc == 0), stop=(kc == NCH - 1)),
                        inc=("m_pe", 1) if kc == NCH - 1 else None)
                if qscale:
                    P.wait("scalar", "m_pe", mv)
                    ev = P.do("scalar", lambda e, c=c, bk=bk, d=dst_st: e.activation(d[:, c, :], bank(bk), AF.Identity, scale=0.125), inc=("m_act", 1))
                    bankres[bk].used_until("m_act", ev)
                else:
                    P.wait("vector", "m_pe", mv)
                    ev = P.do("vector", lambda e, c=c, bk=bk, d=dst_st: e.tensor_copy(d[:, c, :], bank(bk)), inc=("m_dve", 1))
                    bankres[bk].used_until("m_dve", ev)
            if qscale:
                P.wait("sync", "m_act", ev)
                dq = q_sc[s, it].rearrange("c p t -> p c t")
                sp = P.do("sync", lambda e, d=dq: e.dma_start(out=d, in_=qT_st[0][:, :, :]), inc=("p_q", 16))
                stage.used_until("p_q", sp)
            else:
                P.wait("sync", "m_dve", ev)
                dk = k_sc[s, :, :, t0 : t0 + 512].rearrange("c p t -> p c t")
                sp = P.do("sync", lambda e, d=dk: e.dma_start(out=d, in_=kT_st[0][:, :, :]), inc=("p_k", 16))
                stage.used_until("p_k", sp)
        vev = None
        for gi in range(NG):
            for hf in range(2):
                bk = 4 + (gi * 2 + hf) % 2
                bankres[bk].acquire("tensor")
                mv = None
                for kc in range(NCH):
                    mv = P.do("tensor", lambda e, gi=gi, hf=hf, kc=kc, bk=bk: e.matmul(
                        bank(bk), xnT_sb[:, kc, gi * 128 : (gi + 1) * 128],
                        Wv_sb[:, kc, hf * 512 : (hf + 1) * 512],
                        start=(kc == 0), stop=(kc == NCH - 1)),
                        inc=("m_pe", 1) if kc == NCH - 1 else None)
                P.wait("scalar", "m_pe", mv)
                vev = P.do("scalar", lambda e, gi=gi, hf=hf, bk=bk, d=v_st[0]: e.activation(
                    d[:, gi, hf * 512 : (hf + 1) * 512], bank(bk), AF.Identity), inc=("m_act", 1))
                bankres[bk].used_until("m_act", vev)
        xnT_res.used_until("m_pe", P.val("m_pe"))
        P.wait("sync", "m_act", vev)
        dvv = v_sc[s, t0 : t0 + 512, :].rearrange("(g p) d -> p g d", p=128)
        sp3 = P.do("sync", lambda e, d=dvv: e.dma_start(out=d, in_=v_st[0][:, :, :]), inc=("p_v", 16))
        stage.used_until("p_v", sp3)
        if it == NT - 1:
            ns.setdefault("pk_s", {})[s] = P.val("p_k")


def p2a(P, ns):
    Wq_sb, Wk_sb = ns["Wq_sb"], ns["Wk_sb"]
    xnl, xnl_bf, qlT, klT, klT_bd = ns["xnl"], ns["xnl_bf"], ns["qlT"], ns["klT"], ns["klT_bd"]
    bank, bankres = ns["bank"], ns["bankres"]
    G2_bd = ns["G2_bd"]

    P.wait("scalar", "m_dve", P.val("m_dve"))
    cv = P.do("scalar", lambda e: e.copy(
        xnl_bf[:].rearrange("p a b c -> p (a b c)"), xnl[:].rearrange("p a b c -> p (a b c)")), inc=("m_act", 1))
    P.wait("tensor", "m_act", cv)

    for s in range(SB):
        for W_sb, dst, scale in ((Wq_sb, qlT, 0.125 / 64.0), (Wk_sb, klT, 1.0 / 64.0)):
            for c in range(NPAIR):
                bk = c % 4
                bankres[bk].acquire("tensor")
                mv = None
                for kc in range(NCH):
                    mv = P.do("tensor", lambda e, W=W_sb, s=s, c=c, kc=kc, bk=bk: e.matmul(
                        bank(bk, 64), W[:, kc, c * 128 : (c + 1) * 128], xnl_bf[:, s, kc, :],
                        start=(kc == 0), stop=(kc == NCH - 1)),
                        inc=("m_pe", 1) if kc == NCH - 1 else None)
                P.wait("scalar", "m_pe", mv)
                ev = P.do("scalar", lambda e, s=s, c=c, bk=bk, d=dst, sca=scale: e.activation(
                    d[:, s, c, :], bank(bk, 64), AF.Identity, scale=sca), inc=("m_act", 1))
                bankres[bk].used_until("m_act", ev)

    for s in range(SB):
        for c in range(NPAIR):
            P.do("gpsimd", lambda e, s=s, c=c: e.memset(klT_bd[:, s, c, :], 0.0))
            P.do("gpsimd", lambda e, s=s, c=c: e.memset(G2_bd[:, s, c, :], 0.0))
    gz = P.do("gpsimd", lambda e: e.engine_nop(), inc=("m_gp", 1))
    P.wait("vector", "m_gp", gz)
    P.wait("vector", "m_act", P.val("m_act"))
    for s in range(SB):
        for c in range(NPAIR):
            P.do("vector", lambda e, s=s, c=c: e.tensor_copy(klT_bd[0:64, s, c, 0:64], klT[0:64, s, c, :]))
            P.do("vector", lambda e, s=s, c=c: e.tensor_copy(klT_bd[64:128, s, c, 64:128], klT[64:128, s, c, :]))
    P.do("vector", lambda e: e.engine_nop(), inc=("m_dve", 1))

    import itertools
    ns["p2_gen"] = itertools.chain(pinv_gen(P, ns), factory_gen(P, ns))
    next(ns["p2_gen"])


def pinv_gen(P, ns):
    qlT, klT = ns["qlT"], ns["klT"]
    bank, bankres, psBres = ns["bank"], ns["bankres"], ns["psBres"]
    qlbd_all, s2e_all, s2bf_all, s2T_all = ns["qlbd_all"], ns["s2e_all"], ns["s2bf_all"], ns["s2T_all"]
    Z_all, ZT_all, XZb_all, XZT_all, T_all = ns["Z_all"], ns["ZT_all"], ns["XZb_all"], ns["XZT_all"], ns["T_all"]
    zscb, csbf_all = ns["zscb"], ns["csbf_all"]
    tiny, tiny2, zsc_bf32 = ns["tiny"], ns["tiny2"], ns["zsc_bf32"]
    ident, e_sum, pic_s = ns["ident"], ns["e_sum"], ns["pic_s"]
    ZT_st = ns["ZT_st"]
    psB, psA = ns["psB"], ns["psA"]
    scal, sc = ns["scal"], ns["sc"]
    ones_row = ns["ones_row"]
    NI = SB * NPAIR
    inst_sc = [(s, c) for s in range(SB) for c in range(NPAIR)]

    gates = {nm: P.val(nm) for nm in ("m_pe", "m_act", "m_dve", "p_q", "p_k", "p_v")}
    yield
    for eng in ("gpsimd", "vector", "scalar", "tensor"):
        for nm, v in gates.items():
            P.wait(eng, nm, v)

    for i in range(NI):
        P.do("gpsimd", lambda e, i=i: e.memset(qlbd_all[:, i, :], 0.0))
    gz = P.do("gpsimd", lambda e: e.engine_nop(), inc=("m_gp", 1))
    P.wait("vector", "m_gp", gz)
    P.wait("vector", "m_act", P.val("m_act"))
    d1 = None
    for i, (s, c) in enumerate(inst_sc):
        P.do("vector", lambda e, i=i, s=s, c=c: e.tensor_copy(qlbd_all[0:64, i, 0:64], qlT[0:64, s, c, :]))
        d1 = P.do("vector", lambda e, i=i, s=s, c=c: e.tensor_copy(qlbd_all[64:128, i, 64:128], qlT[64:128, s, c, :]),
                  inc=("m_dve", 1) if i == NI - 1 else None)

    yield
    for j in (2, 3):
        bankres[j].acquire("tensor")
    P.wait("tensor", "m_dve", d1)
    mv = None
    for i, (s, c) in enumerate(inst_sc):
        mv = P.do("tensor", lambda e, i=i, s=s, c=c: e.matmul(
            bank(2 + i // 8, 64, (i % 8) * 64), qlbd_all[:, i, :], klT[:, s, c, :],
            start=True, stop=True), inc=("m_pe", 1) if i == NI - 1 else None)
    P.wait("vector", "m_pe", mv)
    d2 = None
    for i in range(NI):
        d2 = P.do("vector", lambda e, i=i: e.reduce_max(scal[:, i : i + 1], bank(2 + i // 8, 64, (i % 8) * 64), axis=AX.X, negate=True),
                  inc=("m_dve", 1) if i == NI - 1 else None)
    P.wait("scalar", "m_dve", d2)
    a1 = None
    for i in range(NI):
        a1 = P.do("scalar", lambda e, i=i: e.activation(s2e_all[:, i, :], bank(2 + i // 8, 64, (i % 8) * 64), AF.Exp,
                                                        bias=scal[:, i : i + 1], accum_out=scal[:, 16 + i : 17 + i]),
                  inc=("m_act", 1) if i == NI - 1 else None)
    for j in (2, 3):
        bankres[j].used_until("m_act", a1)
    yield
    P.wait("vector", "m_act", a1)
    d3 = P.do("vector", lambda e: e.reciprocal(scal[:, 32:48], scal[:, 16:32]), inc=("m_dve", 1))
    P.wait("scalar", "m_dve", d3)
    a2 = None
    for i in range(NI):
        a2 = P.do("scalar", lambda e, i=i: e.activation(s2bf_all[:, i, :], s2e_all[:, i, :], AF.Identity, scale=scal[:, 32 + i : 33 + i]),
                  inc=("m_act", 1) if i == NI - 1 else None)
    P.wait("tensor", "m_act", a2)
    d4 = None
    for rnd in range(2):
        psBres[0].acquire("tensor")
        psBres[1].acquire("tensor")
        tv = None
        for i in range(rnd * 8, rnd * 8 + 8):
            tv = P.do("tensor", lambda e, i=i, rnd=rnd: e.transpose(psB[0:64, (i - rnd * 8) * 128 : (i - rnd * 8 + 1) * 128], s2bf_all[:, i, :], ident[:, :]),
                      inc=("m_pe", 1) if i == rnd * 8 + 7 else None)
        P.wait("vector", "m_pe", tv)
        d4 = P.do("vector", lambda e, rnd=rnd: e.tensor_copy(s2T_all[:].rearrange("p a b -> p (a b)")[:, rnd * 1024 : (rnd + 1) * 1024], psB[0:64, 0:1024]), inc=("m_dve", 1))
        psBres[0].used_until("m_dve", d4)
        psBres[1].used_until("m_dve", d4)
        yield
    P.wait("vector", "m_act", a2)
    d5 = None
    for i in range(NI):
        d5 = P.do("vector", lambda e, i=i: e.reduce_sum(scal[:, 48 + i : 49 + i], s2bf_all[:, i, :], axis=AX.X),
                  inc=("m_dve", 1) if i == NI - 1 else None)
    P.wait("scalar", "m_dve", d5)
    a3 = P.do("scalar", lambda e: e.copy(csbf_all[:, :], scal[:, 48:64]), inc=("m_act", 1))
    P.wait("tensor", "m_act", a3)
    d6 = None
    for rnd in range(2):
        psBres[0].acquire("tensor")
        psBres[1].acquire("tensor")
        tv = None
        for i in range(rnd * 8, rnd * 8 + 8):
            j = i - rnd * 8
            tv = P.do("tensor", lambda e, i=i, j=j: e.transpose(psB[0:1, j * 128 : (j + 1) * 128], csbf_all[:, i : i + 1], ident[:, :]),
                      inc=("m_pe", 1) if j == 7 else None)
        P.wait("vector", "m_pe", tv)
        for i in range(rnd * 8, rnd * 8 + 8):
            j = i - rnd * 8
            P.do("vector", lambda e, i=i, j=j: e.reduce_max(tiny[0:1, i : i + 1], psB[0:1, j * 128 : j * 128 + 64], axis=AX.X))
            d6 = P.do("vector", lambda e, i=i, j=j: e.reduce_max(tiny[0:1, 16 + i : 17 + i], psB[0:1, j * 128 + 64 : j * 128 + 128], axis=AX.X),
                      inc=("m_dve", 1) if j == 7 else None)
        psBres[0].used_until("m_dve", d6)
        psBres[1].used_until("m_dve", d6)
        yield
    yield
    for j in (2, 3):
        bankres[j].acquire("tensor")
    P.wait("tensor", "m_act", a2)
    mv = None
    for i in range(NI):
        mv = P.do("tensor", lambda e, i=i: e.matmul(bank(2 + i // 8, 64, (i % 8) * 64), e_sum[:, :], s2bf_all[:, i, :],
                                                    start=True, stop=True), inc=("m_pe", 1) if i == NI - 1 else None)
    P.wait("vector", "m_pe", mv)
    d7 = None
    for i in range(NI):
        pb = bank(2 + i // 8, 64, (i % 8) * 64)
        P.do("vector", lambda e, i=i, pb=pb: e.reduce_max(tiny[0:1, 32 + i : 33 + i], pb[0:1, :], axis=AX.X))
        d7 = P.do("vector", lambda e, i=i, pb=pb: e.reduce_max(tiny[0:1, 48 + i : 49 + i], pb[64:65, :], axis=AX.X),
                  inc=("m_dve", 1) if i == NI - 1 else None)
    for j in (2, 3):
        bankres[j].used_until("m_dve", d7)
    P.wait("gpsimd", "m_dve", d7)
    P.do("gpsimd", lambda e: e.tensor_mul(tiny2[0:1, 0:16], tiny[0:1, 0:16], tiny[0:1, 32:48]))
    g2 = P.do("gpsimd", lambda e: e.tensor_mul(tiny2[0:1, 16:32], tiny[0:1, 16:32], tiny[0:1, 48:64]), inc=("m_gp", 1))
    P.wait("vector", "m_gp", g2)
    d8 = P.do("vector", lambda e: e.reciprocal(tiny2[0:1, 32:64], tiny2[0:1, 0:32]), inc=("m_dve", 1))
    P.wait("scalar", "m_dve", d8)
    a4 = P.do("scalar", lambda e: e.copy(zsc_bf32[:, :], tiny2[0:1, 32:64]), inc=("m_act", 1))
    yield
    bankres[2].acquire("tensor")
    P.wait("tensor", "m_act", a4)
    mv = None
    for j in range(32):
        mv = P.do("tensor", lambda e, j=j: e.matmul(bank(2, 1, j)[0:64, :], ones_row[:, 0:64], zsc_bf32[:, j : j + 1],
                                                    start=True, stop=True), inc=("m_pe", 1) if j == 31 else None)
    P.wait("vector", "m_pe", mv)
    d9 = P.do("vector", lambda e: e.tensor_copy(zscb[:, :], bank(2, 32, 0)[0:64, :]), inc=("m_dve", 1))
    bankres[2].used_until("m_dve", d9)
    P.wait("scalar", "m_dve", max(d9, d4))
    a5 = None
    for i in range(NI):
        P.do("scalar", lambda e, i=i: e.activation(Z_all[:, i, 0:64], s2T_all[:, i, 0:64], AF.Identity, scale=zscb[:, i : i + 1]))
        a5 = P.do("scalar", lambda e, i=i: e.activation(Z_all[:, i, 64:128], s2T_all[:, i, 64:128], AF.Identity, scale=zscb[:, 16 + i : 17 + i]),
                  inc=("m_act", 1) if i == NI - 1 else None)

    def tr_all(src_t, dst_t, gname, gval):
        P.wait("tensor", gname, gval)
        ev = None
        for rnd in range(2):
            psBres[0].acquire("tensor")
            psBres[1].acquire("tensor")
            tv = None
            for i in range(rnd * 8, rnd * 8 + 8):
                for h in range(2):
                    j = (i - rnd * 8) * 2 + h
                    tv = P.do("tensor", lambda e, i=i, h=h, j=j: e.transpose(
                        psB[0:64, j * 64 : (j + 1) * 64],
                        src_t[:, i, h * 64 : (h + 1) * 64], ident[0:64, 0:64]),
                        inc=("m_pe", 1) if j == 15 else None)
            P.wait("vector", "m_pe", tv)
            ev = P.do("vector", lambda e, rnd=rnd: e.tensor_copy(dst_t[:].rearrange("p a b -> p (a b)")[:, rnd * 1024 : (rnd + 1) * 1024], psB[0:64, 0:1024]), inc=("m_dve", 1))
            psBres[0].used_until("m_dve", ev)
            psBres[1].used_until("m_dve", ev)
        return ev

    yield
    zt = tr_all(Z_all, ZT_all, "m_act", a5)

    z_gate = ("m_act", a5)
    lz = None
    for _ in range(6):
        def rounds(lhs_t, rhs_t, gname, gval, post):
            P.wait("tensor", gname, gval)
            out = None
            for rnd in range(2):
                for j in (2, 3):
                    bankres[j].acquire("tensor")
                mv = None
                for i in range(rnd * 8, rnd * 8 + 8):
                    for h in range(2):
                        col = (i - rnd * 8) * 128 + h * 64
                        last = (i == rnd * 8 + 7 and h == 1)
                        mv = P.do("tensor", lambda e, i=i, h=h, col=col, lhs_t=lhs_t, rhs_t=rhs_t: e.matmul(
                            psA[0:64, 1024 + col : 1024 + col + 64],
                            lhs_t[:, i, h * 64 : (h + 1) * 64], rhs_t[:, i, h * 64 : (h + 1) * 64],
                            start=True, stop=True), inc=("m_pe", 1) if last else None)
                dv, out = post(rnd, mv)
                for j in (2, 3):
                    bankres[j].used_until("m_dve", dv)
            return out

        def t_post(const_off):
            def post(rnd, mv):
                P.wait("vector", "m_pe", mv)
                ev = None
                for b in range(2):
                    pv = psA[0:64, 1024 + b * 512 : 1024 + (b + 1) * 512]
                    ev = P.do("vector", lambda e, b=b, rnd=rnd, pv=pv: e.scalar_tensor_tensor(
                        T_all[:].rearrange("p a b -> p (a b)")[:, rnd * 1024 + b * 512 : rnd * 1024 + (b + 1) * 512],
                        pv, -1.0, pic_s[:, const_off : const_off + 512],
                        op0=ALU.mult, op1=ALU.add), inc=("m_dve", 1) if b == 1 else None)
                return ev, ev
            return post

        def xz_post(rnd, mv):
            P.wait("vector", "m_pe", mv)
            for b in range(2):
                pv = psA[0:64, 1024 + b * 512 : 1024 + (b + 1) * 512]
                P.do("vector", lambda e, b=b, rnd=rnd, pv=pv: e.tensor_copy(
                    XZb_all[:].rearrange("p a b -> p (a b)")[:, rnd * 1024 + b * 512 : rnd * 1024 + (b + 1) * 512], pv))
            ev = None
            for b in range(2):
                pv = psA[0:64, 1024 + b * 512 : 1024 + (b + 1) * 512]
                ev = P.do("vector", lambda e, b=b, rnd=rnd, pv=pv: e.scalar_tensor_tensor(
                    T_all[:].rearrange("p a b -> p (a b)")[:, rnd * 1024 + b * 512 : rnd * 1024 + (b + 1) * 512],
                    pv, -1.0, pic_s[:, 0:512],
                    op0=ALU.mult, op1=ALU.add), inc=("m_dve", 1) if b == 1 else None)
            return ev, ev

        def sc_post(rnd, mv):
            P.wait("vector", "m_pe", mv)
            ev = None
            for b in range(2):
                pv = psA[0:64, 1024 + b * 512 : 1024 + (b + 1) * 512]
                ev = P.do("vector", lambda e, b=b, rnd=rnd, pv=pv: e.tensor_scalar(
                    Z_all[:].rearrange("p a b -> p (a b)")[:, rnd * 1024 + b * 512 : rnd * 1024 + (b + 1) * 512],
                    pv, 0.25, None, op0=ALU.mult), inc=("m_dve", 1) if b == 1 else None)
            return ev, ev

        yield
        t1 = rounds(s2T_all, Z_all, z_gate[0], z_gate[1], xz_post)
        yield
        xzt = tr_all(XZb_all, XZT_all, "m_dve", t1)
        yield
        t2 = rounds(XZT_all, T_all, "m_dve", max(t1, xzt), t_post(512))
        yield
        t3 = rounds(XZT_all, T_all, "m_dve", t2, t_post(1024))
        yield
        lz = rounds(ZT_all, T_all, "m_dve", t3, sc_post)
        z_gate = ("m_dve", lz)
        yield
        zt = tr_all(Z_all, ZT_all, "m_dve", lz)

    yield
    psBres[0].acquire("tensor")
    psBres[1].acquire("tensor")
    P.wait("tensor", "m_dve", lz)
    tv = None
    for i in range(NI):
        tv = P.do("tensor", lambda e, i=i: e.transpose(psB[0:128, i * 64 : (i + 1) * 64], Z_all[:, i, :], ident[0:64, 0:64]),
                  inc=("m_pe", 1) if i == NI - 1 else None)
    P.wait("vector", "m_pe", tv)
    ev = None
    for i, (s, c) in enumerate(inst_sc):
        ev = P.do("vector", lambda e, i=i, s=s, c=c: e.tensor_copy(ZT_st[:, s, c, :], psB[0:128, i * 64 : (i + 1) * 64]),
                  inc=("m_dve", 1) if i == NI - 1 else None)
    psBres[0].used_until("m_dve", ev)
    psBres[1].used_until("m_dve", ev)


def factory_gen(P, ns):
    wdT_sb, ident = ns["wdT_sb"], ns["ident"]
    psB, psBres = ns["psB"], ns["psBres"]
    wg_d, wu_d, wd_d = ns["wg_d"], ns["wu_d"], ns["wd_d"]
    wgt_sc, wut_sc = ns["wgt_sc"], ns["wut_sc"]
    res = ns["res"]
    chunk = [ns["wg_st"][i][:].rearrange("p a b -> p (a b)") for i in range(2)]
    unit = ns["wu_st"]

    gates = {nm: P.val(nm) for nm in ("m_pe", "m_act", "m_dve")}
    yield
    for nm, v in gates.items():
        P.wait("gpsimd", nm, v)

    ch_res = [res("fch0"), res("fch1")]
    wd_loads = [(wd_d, None, dc, q3) for dc in range(NCH) for q3 in range(3)]
    wgu_loads = [(wg_d, wgt_sc, hc, None) for hc in range(NHC)] + [(wu_d, wut_sc, hc, None) for hc in range(NHC)]
    all_loads = wd_loads + wgu_loads
    lvs = {}

    def issue_load(j):
        w_d2, _, a, q3 = all_loads[j]
        b = j % 2
        ch_res[b].acquire("gpsimd")
        if q3 is not None:
            lvs[j] = P.do("gpsimd", lambda e, a=a, q3=q3, b=b: e.dma_start(
                out=chunk[b], in_=w_d2[a * 128 : (a + 1) * 128, q3 * 1024 : (q3 + 1) * 1024]), inc=(f"s_wl{b}", 16))
        else:
            lvs[j] = P.do("gpsimd", lambda e, a=a, b=b, w=w_d2: e.dma_start(
                out=chunk[b], in_=w[a * 128 : (a + 1) * 128, :]), inc=(f"s_wl{b}", 16))

    issue_load(0)
    for jj in range(len(wd_loads)):
        dc, q3 = wd_loads[jj][2], wd_loads[jj][3]
        b = jj % 2
        if jj + 1 < len(all_loads):
            issue_load(jj + 1)
        P.wait("tensor", f"s_wl{b}", lvs[jj])
        last_tv = None
        for j in range(NCH):
            hc = q3 * 8 + j
            half = j // 4
            if j % 4 == 0:
                psBres[half].acquire("tensor")
            pslot = psB[:, half * 1024 + (j % 4) * 128 : half * 1024 + (j % 4) * 128 + 128]
            tv = P.do("tensor", lambda e, j=j, b=b, p=pslot: e.transpose(p, chunk[b][:, j * 128 : (j + 1) * 128], ident[:, :]),
                      inc=("m_pe", 1) if j % 4 == 3 else None)
            if j % 4 == 3:
                last_tv = tv
                P.wait("vector", "m_pe", tv)
                pv = psB[:, half * 1024 : half * 1024 + 512].rearrange("p (c t) -> p c t", c=4)
                ov = wdT_sb[:, hc - 3 : hc + 1, dc * 128 : (dc + 1) * 128]
                evv = P.do("vector", lambda e, pv=pv, ov=ov: e.tensor_copy(ov, pv), inc=("m_dve", 1))
                psBres[half].used_until("m_dve", evv)
        ch_res[b].used_until("m_pe", last_tv)
        yield

    unit_res = [res("fu0"), res("fu1")]
    for j2 in range(len(wgu_loads)):
        jj = len(wd_loads) + j2
        w_d, w_sc, hc, _ = all_loads[jj]
        b = jj % 2
        if jj + 1 < len(all_loads):
            issue_load(jj + 1)
        P.wait("tensor", f"s_wl{b}", lvs[jj])
        unit_res[b].acquire("vector")
        last_tv = None
        for kc in range(NCH):
            half = kc // 4
            if kc % 4 == 0:
                psBres[half].acquire("tensor")
            pslot = psB[:, half * 1024 + (kc % 4) * 128 : half * 1024 + (kc % 4) * 128 + 128]
            tv = P.do("tensor", lambda e, kc=kc, b=b, p=pslot: e.transpose(p, chunk[b][:, kc * 128 : (kc + 1) * 128], ident[:, :]),
                      inc=("m_pe", 1) if kc % 4 == 3 else None)
            if kc % 4 == 3:
                last_tv = tv
                P.wait("vector", "m_pe", tv)
                pv = psB[:, half * 1024 : half * 1024 + 512].rearrange("p (c t) -> p c t", c=4)
                evv = P.do("vector", lambda e, kc=kc, b=b, pv=pv: e.tensor_copy(unit[b][:, kc - 3 : kc + 1, :], pv), inc=("m_dve", 1))
                psBres[half].used_until("m_dve", evv)
        ch_res[b].used_until("m_pe", last_tv)
        dv = P.val("m_dve")
        P.wait("gpsimd", "m_dve", dv)
        sp = P.do("gpsimd", lambda e, w=w_sc, hc=hc, b=b: e.dma_start(out=w[hc], in_=unit[b][:, :, :]), inc=("p_wgs", 16))
        unit_res[b].used_until("p_wgs", sp)
        yield


def p2b(P, ns):
    kT_pair, ET_all, v4 = ns["kT_pair"], ns["ET_all"], ns["v4"]
    ones_col = ns["ones_col"]
    qlT, qlbd = ns["qlT"], ns["qlT_bd"]
    k_sc, v_sc = ns["k_sc"], ns["v_sc"]
    bank, bankres = ns["bank"], ns["bankres"]
    psA = ns["psA"]
    G2_bd, s3inv, s3v_sb, ZT_st = ns["G2_bd"], ns["s3inv"], ns["s3v_sb"], ns["ZT_st"]
    res = ns["res"]
    gen = ns["p2_gen"]

    def pump(n=1):
        if DBG["p2_nopinv"]:
            return
        for _ in range(n):
            try:
                next(gen)
            except StopIteration:
                break

    kt_res = [res("kt0"), res("kt1")]
    ve_res = [res("ve0"), res("ve1")]
    pairs = [(s_, c_) for s_ in range(SB) for c_ in range(NPAIR)]
    kvs = {}

    def load_kt(pi):
        s_, c_ = pairs[pi]
        kb_ = pi % 2
        kt_res[kb_].acquire("scalar")
        P.wait("scalar", "p_k", ns["pk_s"][s_])
        kvs[pi] = P.do("scalar", lambda e, s=s_, c=c_, kb=kb_: e.dma_start(
            out=kT_pair[kb][:, :], in_=k_sc[s, c]), inc=(f"s_kt{kb_}", 16))

    load_kt(0)
    load_kt(1)

    for eng in ("sync", "scalar", "vector"):
        P.wait(eng, "p_q", P.val("p_q"))
        P.wait(eng, "p_k", P.val("p_k"))
        P.wait(eng, "p_v", P.val("p_v"))

    if DBG["p2_nointerleave"]:
        pump(10 ** 6)
    prev_sweep = 0
    for s in range(SB):
        for c in range(NPAIR):
            kb = c % 2
            kv = kvs[s * NPAIR + c]
            prev_pe = P.val("m_pe")
            P.wait("gpsimd", "m_pe", prev_pe)
            gz = P.do("gpsimd", lambda e: e.memset(qlbd[:, :], 0.0), inc=("m_gp", 1))
            P.wait("vector", "m_gp", gz)
            P.do("vector", lambda e, s=s, c=c: e.tensor_copy(qlbd[0:64, 0:64], qlT[0:64, s, c, :]))
            qb = P.do("vector", lambda e, s=s, c=c: e.tensor_copy(qlbd[64:128, 64:128], qlT[64:128, s, c, :]), inc=("m_dve", 1))
            P.wait("tensor", f"s_kt{kb}", kv)
            P.wait("tensor", "m_dve", qb)
            P.wait("tensor", "m_gp", ns["ones_gate"])
            P.wait("scalar", "m_pe", prev_sweep)
            if c == 0:
                bankres[4].acquire("tensor")
            last = None
            exps = []
            for lt in range(32):
                bk = (lt // 4) % 2
                if lt % 4 == 0:
                    bankres[bk].acquire("tensor")
                mv = P.do("tensor", lambda e, lt=lt, bk=bk, kb=kb: e.matmul(
                    bank(bk, 128, (lt % 4) * 128), kT_pair[kb][:, lt * 128 : (lt + 1) * 128], qlbd[:, :],
                    start=True, stop=True), inc=("m_pe", 1) if lt % 4 == 3 else None)
                last = mv
                if lt % 4 == 3:
                    P.wait("scalar", "m_pe", mv)
                    ov = ET_all[:, lt - 3 : lt + 1, c * 128 : (c + 1) * 128]
                    evv = P.do("scalar", lambda e, bk=bk, ov=ov: e.activation(
                        ov, bank(bk).rearrange("p (a b) -> p a b", a=4), AF.Exp), inc=("m_act", 1))
                    bankres[bk].used_until("m_act", evv)
                    exps.append(evv)
                    pump(1)
                    if len(exps) > 1:
                        P.wait("tensor", "m_act", exps[-2])
                        for l2 in range(lt - 7, lt - 3):
                            P.do("tensor", lambda e, l2=l2, c=c: e.matmul(
                                bank(4, 1, c), ET_all[:, l2, c * 128 : (c + 1) * 128], ones_col[:, :],
                                start=(l2 == 0), stop=False))
            P.wait("tensor", "m_act", exps[-1])
            for l2 in range(28, 32):
                P.do("tensor", lambda e, l2=l2, c=c: e.matmul(
                    bank(4, 1, c), ET_all[:, l2, c * 128 : (c + 1) * 128], ones_col[:, :],
                    start=False, stop=(l2 == 31)), inc=("m_pe", 1) if l2 == 31 else None)
            kt_res[kb].used_until("m_pe", last)
            if s * NPAIR + c + 2 < len(pairs):
                load_kt(s * NPAIR + c + 2)

        P.wait("tensor", "m_act", P.val("m_act"))
        for j in range(2):
            bankres[j].acquire("tensor")
        for vt in range(8):
            b = vt % 2
            ve_res[b].acquire("sync")
            srcv = v_sc[s, vt * 512 : (vt + 1) * 512, :].rearrange("(g p) d -> p g d", p=128)
            lv = P.do("sync", lambda e, a=srcv, b=b: e.dma_start(out=v4[b][:, :, :], in_=a), inc=(f"s_v{b}", 16))
            P.wait("tensor", f"s_v{b}", lv)
            mv = None
            for g in range(NG):
                lt = vt * 4 + g
                for c in range(NPAIR):
                    mv = P.do("tensor", lambda e, lt=lt, g=g, c=c, b=b: e.matmul(
                        psA[:, (c // 4) * 512 + (c % 4) * 128 : (c // 4) * 512 + (c % 4) * 128 + 128],
                        ET_all[:, lt, c * 128 : (c + 1) * 128], v4[b][:, g, c * 128 : (c + 1) * 128],
                        start=(lt == 0), stop=(lt == 31)),
                        inc=("m_pe", 1) if (c == NPAIR - 1 and g == NG - 1) else None)
            ve_res[b].used_until("m_pe", mv)
            pump(2)
        prev_sweep = P.val("m_pe")

        P.wait("vector", "m_act", P.val("m_act"))
        sv = None
        for c in range(NPAIR):
            pb = psA[:, (c // 4) * 512 + (c % 4) * 128 : (c // 4) * 512 + (c % 4) * 128 + 128]
            P.wait("vector", "m_pe", prev_sweep)
            rv = P.do("vector", lambda e, c=c: e.reciprocal(s3inv[:, c : c + 1], bank(4, 1, c)), inc=("m_dve", 1))
            P.wait("scalar", "m_dve", rv)
            P.wait("scalar", "m_pe", prev_sweep)
            P.do("scalar", lambda e, pb=pb, s=s, c=c: e.activation(s3v_sb[0:64, s, c, :], pb[0:64, 0:64], AF.Identity, scale=s3inv[0:64, c : c + 1]))
            sv = P.do("scalar", lambda e, pb=pb, s=s, c=c: e.activation(s3v_sb[64:128, s, c, :], pb[64:128, 64:128], AF.Identity, scale=s3inv[64:128, c : c + 1]), inc=("m_act", 1))
            if c == NPAIR - 1:
                bankres[4].used_until("m_dve", P.val("m_dve"))
                bankres[4].used_until("m_act", sv)
                for j in range(2):
                    bankres[j].used_until("m_act", sv)
            pump(1)

    pump(10 ** 6)

    P.wait("tensor", "m_act", P.val("m_act"))
    P.wait("tensor", "m_dve", P.val("m_dve"))
    for s in range(SB):
        for c in range(NPAIR):
            bankres[5].acquire("tensor")
            P.do("tensor", lambda e, s=s, c=c: e.matmul(bank(5, 64, 0)[0:64, :], ZT_st[0:64, s, c, :], s3v_sb[0:64, s, c, :], start=True, stop=True))
            gm = P.do("tensor", lambda e, s=s, c=c: e.matmul(bank(5, 64, 64)[64:128, :], ZT_st[64:128, s, c, :], s3v_sb[64:128, s, c, :], start=True, stop=True), inc=("m_pe", 1))
            P.wait("vector", "m_pe", gm)
            P.do("vector", lambda e, s=s, c=c: e.tensor_copy(G2_bd[0:64, s, c, 0:64], bank(5, 64, 0)[0:64, :]))
            ge = P.do("vector", lambda e, s=s, c=c: e.tensor_copy(G2_bd[64:128, s, c, 64:128], bank(5, 64, 64)[64:128, :]), inc=("m_dve", 1))
            bankres[5].used_until("m_dve", ge)


def p3a(P, ns):
    qT_t3, x_t3, attnoT = ns["qT_t3"], ns["x_t3"], ns["attnoT"]
    E1all = ns["E1all"]
    nlv, invb, E1n2 = ns["nlv"], ns["invb"], ns["E1n2"]
    x2_sb, sq_scr = ns["x2_sb"], ns["sq_scr"]
    klT_bd, G2_bd = ns["klT_bd"], ns["G2_bd"]
    Wo_sb, bo_bf = ns["Wo_sb"], ns["bo_bf"]
    q_sc, x_in, x2_sc = ns["q_sc"], ns["x_in"], ns["x2_sc"]
    rstd2, nmr2 = ns["rstd2"], ns["nmr2"]
    e_sum, ones_row = ns["e_sum"], ns["ones_row"]
    bank, bankres, res, sc = ns["bank"], ns["bankres"], ns["res"], ns["sc"]

    P.wait("sync", "p_q", P.val("p_q"))
    P.wait("sync", "m_pe", P.val("m_pe"))
    P.wait("sync", "m_dve", P.val("m_dve"))
    P.wait("sync", "m_act", P.val("m_act"))
    stat4, eps_c = ns["stat4"], ns["eps_c"]
    st_prev = {"g4": 0, "exp": 0, "nmr": 0}
    pending_tail = None

    def stats_tail(pend):
        s_, it_, smv_ = pend
        P.wait("vector", "m_act", smv_)
        P.wait("vector", "m_gp", st_prev["g4"])
        nm2 = P.do("vector", lambda e: e.tensor_tensor(stat4[:, 8:12], stat4[:, 4:8], stat4[:, 4:8], ALU.mult), inc=("m_dve", 1))
        P.wait("gpsimd", "m_dve", nm2)
        P.do("gpsimd", lambda e: e.tensor_scalar_mul(stat4[:, 8:12], stat4[:, 8:12], 1.0 / (1024.0 * 1024.0)))
        P.do("gpsimd", lambda e: e.tensor_scalar_mul(stat4[:, 12:16], stat4[:, 0:4], 1.0 / 1024.0))
        g4v = P.do("gpsimd", lambda e: e.tensor_sub(stat4[:, 12:16], stat4[:, 12:16], stat4[:, 8:12]), inc=("m_gp", 1))
        P.wait("scalar", "m_gp", g4v)
        lnv2 = P.do("scalar", lambda e: e.activation(stat4[:, 16:20], stat4[:, 12:16], AF.Ln, bias=eps_c[:, :]), inc=("m_act", 1))
        P.wait("vector", "m_act", lnv2)
        mhv = P.do("vector", lambda e: e.tensor_scalar(stat4[:, 20:24], stat4[:, 16:20], -0.5, None, op0=ALU.mult), inc=("m_dve", 1))
        P.wait("scalar", "m_dve", mhv)
        expv = P.do("scalar", lambda e, s=s_, it=it_: e.activation(
            rstd2[:, s, it * NG : (it + 1) * NG], stat4[:, 20:24], AF.Exp), inc=("m_act", 1))
        P.wait("vector", "m_act", expv)
        nmrv = P.do("vector", lambda e, s=s_, it=it_: e.scalar_tensor_tensor(
            nmr2[:, s, it * NG : (it + 1) * NG], stat4[:, 4:8], -1.0 / 1024.0,
            rstd2[:, s, it * NG : (it + 1) * NG], op0=ALU.mult, op1=ALU.mult), inc=("m_dve", 1))
        return {"g4": g4v, "exp": expv, "nmr": nmrv}
    q_res = [res("q3_0"), res("q3_1")]
    x3_res = res("x3")
    x2_res = [res(f"x2s_{j}") for j in range(4)]
    chain_res = [res(f"p3a_ch{j}") for j in range(4)]

    NTILES = SB * NT

    def load_q(i):
        s, it = divmod(i, NT)
        buf = i % 2
        q_res[buf].acquire("sync")
        return P.do("sync", lambda e, s=s, it=it, b=buf: e.dma_start(
            out=qT_t3[b][:, :, :], in_=q_sc[s, it].rearrange("c p t -> p c t")), inc=(f"s_q{buf}", 16))

    def s1(i, qv, prev_e1ns=None):
        s, it = divmod(i, NT)
        buf = i % 2
        P.wait("tensor", f"s_q{buf}", qv)
        exps = []
        for c in range(NPAIR):
            bsc = c % 2
            bankres[bsc].acquire("tensor")
            mv = P.do("tensor", lambda e, s=s, c=c, b=buf, bk=bsc: e.matmul(
                bank(bk), klT_bd[:, s, c, :], qT_t3[b][:, c, :], start=True, stop=True), inc=("m_pe", 1))
            P.wait("scalar", "m_pe", mv)
            if prev_e1ns is not None:
                P.wait("scalar", "m_dve", prev_e1ns[c])
            ex = P.do("scalar", lambda e, c=c, bk=bsc: e.activation(E1all[:, c, :], bank(bk), AF.Exp), inc=("m_act", 1))
            bankres[bsc].used_until("m_act", ex)
            exps.append(ex)
        q_res[buf].used_until("m_pe", P.val("m_pe"))
        return exps

    def stage2(i, exps):
        s, it = divmod(i, NT)
        e1ns = []
        for c in range(NPAIR):
            bs = c % 2
            cb_ = c % 4
            bankres[bs].acquire("tensor")
            P.wait("tensor", "m_act", exps[c])
            sv = P.do("tensor", lambda e, c=c, bk=bs: e.matmul(bank(bk), e_sum[:, :], E1all[:, c, :], start=True, stop=True), inc=("m_pe", 1))
            chain_res[cb_].acquire("scalar")
            P.wait("scalar", "m_pe", sv)
            lv = P.do("scalar", lambda e, bk=bs, b=cb_: e.activation(nlv[b][:, :], bank(bk), AF.Ln), inc=("m_act", 1))
            bankres[bs].used_until("m_act", lv)
            ie = P.do("scalar", lambda e, b=cb_: e.activation(invb[b][:, :], nlv[b][:, :], AF.Exp, scale=-1.0), inc=("m_act", 1))
            P.wait("vector", "m_act", ie)
            nv = P.do("vector", lambda e, c=c, b=cb_: e.tensor_tensor(E1n2[b][:, :], E1all[:, c, :], invb[b][:, :], ALU.mult), inc=("m_dve", 1))
            chain_res[cb_].used_until("m_dve", nv)
            e1ns.append(nv)
        return e1ns

    def stage3(i, e1ns):
        s, it = divmod(i, NT)
        for c in range(NPAIR):
            bat = 4 + c % 2
            bankres[bat].acquire("tensor")
            P.wait("tensor", "m_dve", e1ns[c])
            av = P.do("tensor", lambda e, s=s, c=c, b=c % 4, bk=bat: e.matmul(bank(bk), G2_bd[:, s, c, :], E1n2[b][:, :], start=True, stop=True), inc=("m_pe", 1))
            chain_res[c % 4].used_until("m_pe", av)
            P.wait("vector", "m_pe", av)
            ae = P.do("vector", lambda e, c=c, bk=bat: e.tensor_copy(attnoT[:, c, :], bank(bk)), inc=("m_dve", 1))
            bankres[bat].used_until("m_dve", ae)
        return P.val("m_dve")

    qvs = {0: load_q(0), 1: load_q(1)}
    exps = s1(0, qvs[0])
    e1ns = stage2(0, exps)

    for i in range(NTILES):
        s, it = divmod(i, NT)
        t0 = it * 512
        if i + 2 < NTILES:
            qvs[i + 2] = load_q(i + 2)
        x3_res.acquire("sync")
        xv = P.do("sync", lambda e, s=s, t0=t0: e.dma_start(
            out=x_t3[:, :, :], in_=x_in[s, t0 : t0 + 512, :].rearrange("(g p) d -> p g d", p=128)), inc=("s_x3", 16))

        attn_done = stage3(i, e1ns)

        if i + 1 < NTILES:
            nexps = s1(i + 1, qvs[i + 1], e1ns)
            e1ns = stage2(i + 1, nexps)
        if pending_tail is not None:
            st_prev = stats_tail(pending_tail)
            pending_tail = None

        P.wait("tensor", "m_dve", attn_done)
        P.wait("tensor", "s_wo", 16)
        P.wait("tensor", "s_bo", 16)
        P.wait("tensor", "s_x3", xv)
        P.wait("vector", "s_x3", xv)
        if not DBG["p3a_wo"]:
            continue
        for gi in range(NG):
            xbuf = gi
            x2_res[xbuf].acquire("vector")
            for hf in range(2):
                bk = 2 + hf
                bankres[bk].acquire("tensor")
                for kc in range(NCH):
                    P.do("tensor", lambda e, gi=gi, hf=hf, kc=kc, bk=bk: e.matmul(
                        bank(bk), attnoT[:, kc, gi * 128 : (gi + 1) * 128],
                        Wo_sb[:, kc, hf * 512 : (hf + 1) * 512],
                        start=(kc == 0), stop=(kc == NCH - 1)))
                wv = P.do("tensor", lambda e, hf=hf, bk=bk: e.matmul(
                    bank(bk), ones_row[:, :], bo_bf[:, hf * 512 : (hf + 1) * 512],
                    start=False, stop=True), inc=("m_pe", 1))
                P.wait("vector", "m_pe", wv)
                evv = P.do("vector", lambda e, gi=gi, hf=hf, bk=bk, xb=xbuf: e.scalar_tensor_tensor(
                    x2_sb[xb][:, hf * 512 : (hf + 1) * 512], bank(bk), 1.0,
                    x_t3[:, gi, hf * 512 : (hf + 1) * 512], op0=ALU.mult, op1=ALU.add), inc=("m_dve", 1))
                bankres[bk].used_until("m_dve", evv)
            x2v = P.val("m_dve")
            P.wait("gpsimd", "m_dve", x2v)
            dst = x2_sc[s, t0 + gi * 128 : t0 + (gi + 1) * 128, :]
            sp = P.do("gpsimd", lambda e, d=dst, xb=xbuf: e.dma_start(out=d, in_=x2_sb[xb][:, :]), inc=("p_x2", 16))
            x2_res[xbuf].used_until("p_x2", sp)
            if gi == 0:
                P.wait("scalar", "m_gp", st_prev["g4"])
                P.wait("scalar", "m_dve", st_prev["nmr"])
            P.wait("scalar", "m_dve", x2v)
            P.do("scalar", lambda e, g=gi, xb=xbuf: e.activation(
                sq_scr[:, :], x2_sb[xb][:, :], AF.Square, accum_out=stat4[:, g : g + 1]), inc=("m_act", 1))
            smv = P.do("scalar", lambda e, g=gi, xb=xbuf: e.activation(
                sq_scr[:, :], x2_sb[xb][:, :], AF.Identity, accum_out=stat4[:, 4 + g : 5 + g]), inc=("m_act", 1))
            x2_res[xbuf].used_until("m_act", smv)
        pending_tail = (s, it, smv)
        x3_res.used_until("m_dve", P.val("m_dve"))
    stats_tail(pending_tail)


def p3b(P, ns):
    x2_t, x2n_bf, x2nT, h_sb, out_sb, silu_g = ns["x2_t"], ns["x2n_bf"], ns["x2nT"], ns["h_sb"], ns["out_sb"], ns["silu_g"]
    wg_st, wu_st, wdT_sb = ns["wg_st"], ns["wu_st"], ns["wdT_sb"]
    x2_sc, wgt_sc, wut_sc, y_out = ns["x2_sc"], ns["wgt_sc"], ns["wut_sc"], ns["y_out"]
    rstd2, nmr2, ident = ns["rstd2"], ns["nmr2"], ns["ident"]
    bank, bankres, psBres, psB, res = ns["bank"], ns["bankres"], ns["psBres"], ns["psB"], ns["res"]

    P.wait("sync", "p_x2", P.val("p_x2"))
    P.wait("sync", "p_wgs", P.val("p_wgs"))
    P.wait("sync", "m_pe", P.val("m_pe"))
    P.wait("sync", "m_dve", P.val("m_dve"))
    P.wait("sync", "m_act", P.val("m_act"))

    x2_res = [res("x2t_0"), res("x2t_1")]
    wg_res = [res("wgst_0"), res("wgst_1")]
    wu_res = [res("wust_0"), res("wust_1")]
    o_res = [res("osb_0"), res("osb_1")]
    silu_res = res("silu")
    xnT_res = res("x2nT")
    rstd2, nmr2 = ns["rstd2"], ns["nmr2"]
    last_tr = P.val("m_pe")
    NTILES3 = SB * NT

    def load_x2(idx):
        s, it = divmod(idx, NT)
        t0 = it * 512
        b = idx % 2
        x2_res[b].acquire("scalar")
        return P.do("scalar", lambda e, s=s, t0=t0, b=b: e.dma_start(
            out=x2_t[b][:, :, :], in_=x2_sc[s, t0 : t0 + 512, :].rearrange("(g p) d -> p g d", p=128)), inc=(f"s_x2{b}", 16))

    xvs = {0: load_x2(0)}
    xns = {}
    last_tr_b = [0, 0, 0, 0]

    def norms(idx2):
        s2_, it2 = divmod(idx2, NT)
        b2 = idx2 % 2
        P.wait("scalar", f"s_x2{b2}", xvs[idx2])
        vs = []
        for gi in range(NG):
            gidx = it2 * NG + gi
            P.wait("scalar", "m_pe", last_tr_b[gi])
            xn = P.do("scalar", lambda e, b=b2, gi=gi, s=s2_, gidx=gidx: e.activation(
                x2n_bf[gi][:, :], x2_t[b][:, gi, :], AF.Identity,
                bias=nmr2[:, s, gidx : gidx + 1], scale=rstd2[:, s, gidx : gidx + 1]), inc=("m_act", 1))
            vs.append(xn)
        xns[idx2] = vs

    def tr_block(idx2):
        out = None
        for gi in range(NG):
            P.wait("tensor", "m_act", xns[idx2][gi])
            for c in range(NCH):
                half = c // 4
                if c % 4 == 0:
                    psBres[half].acquire("tensor")
                pslot = psB[:, half * 1024 + (c % 4) * 128 : half * 1024 + (c % 4) * 128 + 128]
                tv = P.do("tensor", lambda e, gi=gi, c=c, p=pslot: e.transpose(p, x2n_bf[gi][:, c * 128 : (c + 1) * 128], ident[:, :]),
                          inc=("m_pe", 1) if c % 4 == 3 else None)
                if c % 4 == 3:
                    last_tr_b[gi] = tv
                    P.wait("vector", "m_pe", tv)
                    pv = psB[:, half * 1024 : half * 1024 + 512].rearrange("p (c t) -> p c t", c=4)
                    ov = x2nT[:, c - 3 : c + 1, gi * 128 : (gi + 1) * 128]
                    evv = P.do("vector", lambda e, pv=pv, ov=ov: e.tensor_copy(ov, pv), inc=("m_dve", 1))
                    psBres[half].used_until("m_dve", evv)
                    out = evv
        return out

    for s in range(SB):
        for it in range(NT):
            t0 = it * 512
            idx = s * NT + it
            buf = idx % 2
            if idx + 1 < NTILES3:
                xvs[idx + 1] = load_x2(idx + 1)

            if idx == 0:
                norms(0)
            xt_done = tr_block(idx)
            P.wait("tensor", "m_dve", xt_done)

            for hc in range(NHC):
                wb = hc % 2
                wg_res[wb].acquire("sync")
                wgv = P.do("sync", lambda e, hc=hc, b=wb: e.dma_start(out=wg_st[b][:, :, :], in_=wgt_sc[hc]), inc=(f"s_wg{wb}", 16))
                wu_res[wb].acquire("sync")
                wuv = P.do("sync", lambda e, hc=hc, b=wb: e.dma_start(out=wu_st[b][:, :, :], in_=wut_sc[hc]), inc=(f"s_wu{wb}", 16))
                P.wait("tensor", f"s_wg{wb}", wgv)
                bg, bu = hc % 2, 2 + hc % 2
                bankres[bg].acquire("tensor")
                gv = None
                for kc in range(NCH):
                    gv = P.do("tensor", lambda e, kc=kc, b=wb, bk=bg: e.matmul(
                        bank(bk), wg_st[b][:, kc, :], x2nT[:, kc, :], start=(kc == 0), stop=(kc == NCH - 1)),
                        inc=("m_pe", 1) if kc == NCH - 1 else None)
                wg_res[wb].used_until("m_pe", gv)
                P.wait("tensor", f"s_wu{wb}", wuv)
                bankres[bu].acquire("tensor")
                uv = None
                for kc in range(NCH):
                    uv = P.do("tensor", lambda e, kc=kc, b=wb, bk=bu: e.matmul(
                        bank(bk), wu_st[b][:, kc, :], x2nT[:, kc, :], start=(kc == 0), stop=(kc == NCH - 1)),
                        inc=("m_pe", 1) if kc == NCH - 1 else None)
                wu_res[wb].used_until("m_pe", uv)
                silu_res.acquire("scalar")
                P.wait("scalar", "m_pe", gv)
                sg = P.do("scalar", lambda e, bk=bg: e.activation(silu_g[:, :], bank(bk), AF.Silu, scale=S127), inc=("m_act", 1))
                bankres[bg].used_until("m_act", sg)
                P.wait("vector", "m_pe", uv)
                P.wait("vector", "m_act", sg)
                hv = P.do("vector", lambda e, hc=hc, bk=bu: e.scalar_tensor_tensor(
                    h_sb[:, hc, :], bank(bk), 1.0, silu_g[:, :], op0=ALU.mult, op1=ALU.mult), inc=("m_dve", 1))
                bankres[bu].used_until("m_dve", hv)
                silu_res.used_until("m_dve", hv)
            h_done = P.val("m_dve")

            P.wait("tensor", "m_dve", h_done)
            for gi in range(NG):
                ob = gi % 2
                o_res[ob].acquire("vector")
                for hf in range(2):
                    bk = 4 + (gi * 2 + hf) % 2
                    bankres[bk].acquire("tensor")
                    dvv = None
                    for hc in range(NHC):
                        dvv = P.do("tensor", lambda e, gi=gi, hf=hf, hc=hc, bk=bk: e.matmul(
                            bank(bk), h_sb[:, hc, gi * 128 : (gi + 1) * 128],
                            wdT_sb[:, hc, hf * 512 : (hf + 1) * 512],
                            start=(hc == 0), stop=(hc == NHC - 1)),
                            inc=("m_pe", 1) if hc == NHC - 1 else None)
                    P.wait("vector", "m_pe", dvv)
                    evv = P.do("vector", lambda e, gi=gi, hf=hf, bk=bk, b=buf, ob=ob: e.scalar_tensor_tensor(
                        out_sb[ob][:, hf * 512 : (hf + 1) * 512], bank(bk), S127SQ,
                        x2_t[b][:, gi, hf * 512 : (hf + 1) * 512], op0=ALU.mult, op1=ALU.add), inc=("m_dve", 1))
                    bankres[bk].used_until("m_dve", evv)
                ov = P.val("m_dve")
                P.wait("gpsimd", "m_dve", ov)
                dst = y_out[s, t0 + gi * 128 : t0 + (gi + 1) * 128, :]
                sp = P.do("gpsimd", lambda e, d=dst, ob=ob: e.dma_start(out=d, in_=out_sb[ob][:, :]), inc=("p_out", 16))
                o_res[ob].used_until("p_out", sp)
            if idx + 1 < NTILES3:
                norms(idx + 1)
            x2_res[buf].used_until("m_dve", P.val("m_dve"))


def _host_consts():
    ident = np.eye(128, dtype=ml_dtypes.bfloat16)
    e_sum = np.zeros((128, 128), dtype=ml_dtypes.bfloat16)
    e_sum[0:64, 0:64] = 1
    e_sum[64:128, 64:128] = 1
    rows = np.zeros((1, 384), dtype=ml_dtypes.bfloat16)
    rows[0, 0:64] = 1
    rows[0, 192:256] = 1
    rows[0, 256:384] = 1
    brows = np.zeros((128, 256), dtype=ml_dtypes.bfloat16)
    brows[0, 0:64] = 1
    brows[64, 192:256] = 1
    pic = np.zeros((64, 1536), dtype=np.float32)
    for j, v in enumerate((7.0, 15.0, 13.0)):
        blk = v * np.eye(64, dtype=np.float32)
        for r in range(8):
            pic[:, j * 512 + r * 64 : j * 512 + (r + 1) * 64] = blk
    return ident, e_sum, rows, pic, brows


def _in_maps(x, Wq, Wk, Wv, Wo, bo, wg, wu, wd):
    ident, e_sum, rows, pic, brows = _host_consts()
    maps = []
    for cidx in range(NCORES):
        maps.append({
            "x": np.ascontiguousarray(x[cidx * SB : (cidx + 1) * SB]),
            "Wq": Wq, "Wk": Wk, "Wv": Wv, "Wo": Wo, "bo": bo,
            "wg": wg, "wu": wu, "wd": wd,
            "ident": ident, "esum": e_sum, "rows": rows, "pic": pic, "brows": brows,
        })
    return maps


def kernel(L0_attn_x, L0_attn_Wk, L0_attn_Wq, L0_attn_Wv, L0_attn_Wo, L0_attn_bo,
           L0_ffn_w_gate, L0_ffn_w_up, L0_ffn_w_down):
    x = np.ascontiguousarray(np.asarray(L0_attn_x, dtype=np.float32))
    maps = _in_maps(
        x,
        np.ascontiguousarray(np.asarray(L0_attn_Wq, dtype=np.float32)),
        np.ascontiguousarray(np.asarray(L0_attn_Wk, dtype=np.float32)),
        np.ascontiguousarray(np.asarray(L0_attn_Wv, dtype=np.float32)),
        np.ascontiguousarray(np.asarray(L0_attn_Wo, dtype=np.float32)),
        np.ascontiguousarray(np.asarray(L0_attn_bo, dtype=np.float32)),
        np.ascontiguousarray(np.asarray(L0_ffn_w_gate, dtype=np.int32)),
        np.ascontiguousarray(np.asarray(L0_ffn_w_up, dtype=np.int32)),
        np.ascontiguousarray(np.asarray(L0_ffn_w_down, dtype=np.int32)),
    )
    nc = build_program()
    res = run_bass_kernel_spmd(nc, maps, core_ids=list(range(NCORES)))
    return np.concatenate([r["y"] for r in res.results], axis=0)


# version 35
# speedup vs baseline: 1.0432x; 1.0321x over previous
import math
import contextlib
import numpy as np
import ml_dtypes

import concourse.bass as bass
import concourse.mybir as mybir
from concourse.bass_utils import run_bass_kernel_spmd

F32 = mybir.dt.float32
BF16 = mybir.dt.bfloat16
I32 = mybir.dt.int32
U8 = mybir.dt.uint8
AF = mybir.ActivationFunctionType
ALU = mybir.AluOpType
AX = mybir.AxisListType

B, L, A, H, HS, M, HID = 16, 4096, 1024, 16, 64, 64, 3072
NCORES = 8
SB = B // NCORES
NT = L // 512
NG = 4
NCH = A // 128
NHC = HID // 128
NPAIR = H // 2
S127 = float(np.float32(1.0 / 127.0))
S127SQ = float(np.float32(np.float32(1.0 / 127.0) * np.float32(1.0 / 127.0)))

PHASES = ("P1", "P2", "P3A", "P3B")
DBG = {"p3a_tiles": None, "p3a_pairs": None, "p3a_wo": True, "p3a_norm": True, "p2_nointerleave": False, "p2_nopinv": False}


class Prog:

    ENGS = ("sync", "vector", "scalar", "tensor", "gpsimd")

    def __init__(self, nc, es):
        self.nc = nc
        self.es = es
        self.ops = {e: [] for e in self.ENGS}
        self.last_wait = {}
        self.sems = {}
        self.cnt = {}

    def sem(self, name):
        if name not in self.sems:
            self.sems[name] = self.es.enter_context(self.nc.semaphore(name))
            self.cnt[name] = 0
        return self.sems[name]

    def val(self, name):
        self.sem(name)
        return self.cnt[name]

    def do(self, eng, fn, inc=None):
        if inc is None:
            self.ops[eng].append(fn)
            return None
        name, n = inc
        sem = self.sem(name)
        self.cnt[name] += n
        self.ops[eng].append(lambda e: fn(e).then_inc(sem, n))
        return self.cnt[name]

    def wait(self, eng, name, v):
        if v is None or v <= 0:
            return
        assert v <= self.cnt.get(name, 0), f"wait on future value {name}={v}"
        key = (eng, name)
        if self.last_wait.get(key, 0) >= v:
            return
        self.last_wait[key] = v
        sem = self.sem(name)
        self.ops[eng].append(lambda e: e.wait_ge(sem, v))

    def emit(self, block):
        for ename in self.ENGS:
            fns = self.ops[ename]
            if not fns:
                continue

            def runner(eng, fns=fns):
                for f in fns:
                    f(eng)

            getattr(block, ename)(runner)


class Res:

    def __init__(self, prog):
        self.p = prog
        self.rel = []

    def acquire(self, eng):
        for name, v in self.rel:
            self.p.wait(eng, name, v)
        self.rel = []

    def used_until(self, name, v):
        if v:
            self.rel.append((name, v))


def _carver(arena):
    def carve(off, shape, dtype):
        per = math.prod(shape[1:]) * mybir.dt.size(dtype)
        v = arena[0 : shape[0], off : off + per].bitcast(dtype)
        if len(shape) > 2:
            names = [f"d{i}" for i in range(1, len(shape))]
            v = v.rearrange(
                "p (" + " ".join(names) + ") -> p " + " ".join(names),
                **{n: shape[i + 1] for i, n in enumerate(names)},
            )
        return v

    return carve


def build_program(phases=PHASES, debug_spills=False):
    nc = bass.Bass("TRN2", target_bir_lowering=False)
    ns = {}

    x_in = nc.dram_tensor("x", [SB, L, A], F32, kind="ExternalInput").ap()
    Wq_d = nc.dram_tensor("Wq", [A, A], F32, kind="ExternalInput").ap()
    Wk_d = nc.dram_tensor("Wk", [A, A], F32, kind="ExternalInput").ap()
    Wv_d = nc.dram_tensor("Wv", [A, A], F32, kind="ExternalInput").ap()
    Wo_d = nc.dram_tensor("Wo", [A, A], F32, kind="ExternalInput").ap()
    bo_d = nc.dram_tensor("bo", [A], F32, kind="ExternalInput").ap()
    wg_d = nc.dram_tensor("wg", [HID, A], I32, kind="ExternalInput").ap()
    wu_d = nc.dram_tensor("wu", [HID, A], I32, kind="ExternalInput").ap()
    wd_d = nc.dram_tensor("wd", [A, HID], I32, kind="ExternalInput").ap()
    ident_d = nc.dram_tensor("ident", [128, 128], BF16, kind="ExternalInput").ap()
    esum_d = nc.dram_tensor("esum", [128, 128], BF16, kind="ExternalInput").ap()
    rows_d = nc.dram_tensor("rows", [1, 384], BF16, kind="ExternalInput").ap()
    brows_d = nc.dram_tensor("brows", [128, 256], BF16, kind="ExternalInput").ap()
    pic_d = nc.dram_tensor("pic", [64, 1536], F32, kind="ExternalInput").ap()

    spk = "ExternalOutput" if debug_spills else "Internal"
    q_sc = nc.dram_tensor("q_sc", [SB, NT, NCH, 128, 512], BF16, kind=spk).ap()
    k_sc = nc.dram_tensor("k_sc", [SB, NCH, 128, L], BF16, kind=spk).ap()
    v_sc = nc.dram_tensor("v_sc", [SB, L, A], BF16, kind=spk).ap()
    x2_sc = nc.dram_tensor("x2_sc", [SB, L, A], F32, kind=spk).ap()
    wgt_sc = nc.dram_tensor("wgt_sc", [NHC, 128, NCH, 128], BF16, kind=spk).ap()
    wut_sc = nc.dram_tensor("wut_sc", [NHC, 128, NCH, 128], BF16, kind=spk).ap()
    y_out = nc.dram_tensor("y", [SB, L, A], F32, kind="ExternalOutput").ap()
    dbg = {}
    if debug_spills:
        for nm, shape, dt in (
            ("lmq", [128, SB, NPAIR, 64], BF16), ("lmk", [128, SB, NPAIR, 64], BF16),
            ("g2", [128, SB, NPAIR, 128], BF16), ("rstd", [128, SB, 32], F32),
            ("nmr", [128, SB, 32], F32), ("xnl", [128, SB, NCH, 64], F32),
            ("zt", [128, SB, NPAIR, 64], BF16), ("s3v", [128, SB, NPAIR, 64], BF16),
        ):
            dbg[nm] = nc.dram_tensor("dbg_" + nm, shape, dt, kind="ExternalOutput").ap()

    es = contextlib.ExitStack()
    with es:
        P = Prog(nc, es)

        arena_w = es.enter_context(nc.sbuf_tensor("arena_w", [128, 49152], U8))
        cw = _carver(arena_w)
        Wq_sb = cw(0, [128, NCH, 1024], BF16)
        Wk_sb = cw(16384, [128, NCH, 1024], BF16)
        Wv_sb = cw(32768, [128, NCH, 1024], BF16)
        wdT_sb = cw(0, [128, NHC, 1024], BF16)
        Wo_sb = es.enter_context(nc.sbuf_tensor("Wo_sb", [128, NCH, 1024], BF16))

        arena_b = es.enter_context(nc.sbuf_tensor("arena_b", [128, 98304], U8))
        cb = _carver(arena_b)
        x_t = [cb(0 + i * 16384, [128, NG, 1024], F32) for i in range(2)]
        xn_bf4 = cb(32768, [128, NG, 1024], BF16)
        xnT_sb = cb(40960, [128, NCH, 512], BF16)
        qT_st = [cb(49152, [128, NCH, 512], BF16)]
        kT_st = [cb(57344, [128, NCH, 512], BF16)]
        v_st = [cb(65536, [128, NG, 1024], BF16)]
        NI = SB * NPAIR
        qlbd_all = cw(0, [128, NI, 128], BF16)
        s2e_all = cw(4096, [128, NI, 64], F32)
        s2bf_all = cw(8192, [128, NI, 64], BF16)
        s2T_all = cw(10240, [64, NI, 128], BF16)
        Z_all = cw(14336, [64, NI, 128], BF16)
        ZT_all = cw(18432, [64, NI, 128], BF16)
        XZb_all = cw(22528, [64, NI, 128], BF16)
        XZT_all = cw(26624, [64, NI, 128], BF16)
        T_all = cw(30720, [64, NI, 128], BF16)
        zscb = cw(34816, [64, 32], F32)
        csbf_all = cw(35072, [128, 16], BF16)
        ET_all = cb(0, [128, 32, 1024], BF16)
        kT_pair = [cb(73728 + i * 8192, [128, L], BF16) for i in range(2)]
        v4 = [cb(65536, [128, NG, 1024], BF16), cb(90112, [128, NG, 1024], BF16)]
        qT_t3 = [cb(0 + i * 8192, [128, NCH, 512], BF16) for i in range(2)]
        x_t3 = cb(16384, [128, NG, 1024], F32)
        attnoT = cb(32768, [128, NCH, 512], BF16)
        E1all = cb(40960, [128, NPAIR, 512], BF16)
        nlv = [cb(49152 + i * 2048, [128, 512], F32) for i in range(4)]
        invb = [cb(57344 + i * 1024, [128, 512], BF16) for i in range(4)]
        E1n2 = [cb(61440 + i * 1024, [128, 512], BF16) for i in range(4)]
        x2_sb = [cb(65536 + i * 4096, [128, 1024], F32) for i in range(4)]
        x2_t = [cb(0 + i * 16384, [128, NG, 1024], F32) for i in range(2)]
        x2n_bf = [cb(81920 + i * 2048, [128, 1024], BF16) for i in range(4)]
        x2nT = cb(34816, [128, NCH, 512], BF16)
        h_sb = cb(43008, [128, NHC, 512], BF16)
        out_sb = [cb(67584 + i * 4096, [128, 1024], F32) for i in range(2)]
        silu_g = cb(75776, [128, 512], BF16)

        wg_st = [es.enter_context(nc.sbuf_tensor(f"wg_st{i}", [128, NCH, 128], BF16)) for i in range(2)]
        wu_st = [es.enter_context(nc.sbuf_tensor(f"wu_st{i}", [128, NCH, 128], BF16)) for i in range(2)]

        ident = es.enter_context(nc.sbuf_tensor("ident_s", [128, 128], BF16))
        e_sum = es.enter_context(nc.sbuf_tensor("e_sum_s", [128, 128], BF16))
        rows_s = es.enter_context(nc.sbuf_tensor("rows_s", [1, 384], BF16))
        brows = es.enter_context(nc.sbuf_tensor("brows_s", [128, 256], BF16))
        pic_s = es.enter_context(nc.sbuf_tensor("pic_s", [64, 1536], F32))
        bo_bf = es.enter_context(nc.sbuf_tensor("bo_bf", [1, 1024], BF16))
        sq_scr = es.enter_context(nc.sbuf_tensor("sq_scr", [128, 1024], F32))
        stat4 = es.enter_context(nc.sbuf_tensor("stat4", [128, 24], F32))
        xnl = es.enter_context(nc.sbuf_tensor("xnl", [128, SB, NCH, 64], F32))
        xnl_bf = es.enter_context(nc.sbuf_tensor("xnl_bf", [128, SB, NCH, 64], BF16))
        qlT = es.enter_context(nc.sbuf_tensor("qlT", [128, SB, NPAIR, 64], BF16))
        klT = es.enter_context(nc.sbuf_tensor("klT", [128, SB, NPAIR, 64], BF16))
        klT_bd = es.enter_context(nc.sbuf_tensor("klT_bd", [128, SB, NPAIR, 128], BF16))
        qlT_bd = es.enter_context(nc.sbuf_tensor("qlT_bd", [128, 128], BF16))
        G2_bd = es.enter_context(nc.sbuf_tensor("G2_bd", [128, SB, NPAIR, 128], BF16))
        ZT_st = es.enter_context(nc.sbuf_tensor("ZT_st", [128, SB, NPAIR, 64], BF16))
        rstd2 = es.enter_context(nc.sbuf_tensor("rstd2", [128, SB, 32], F32))
        nmr2 = es.enter_context(nc.sbuf_tensor("nmr2", [128, SB, 32], F32))
        scal = es.enter_context(nc.sbuf_tensor("scal", [128, 64], F32))
        eps_c = es.enter_context(nc.sbuf_tensor("eps_c", [128, 1], F32))
        s2e = es.enter_context(nc.sbuf_tensor("s2e", [128, 64], F32))
        s2bf = es.enter_context(nc.sbuf_tensor("s2bf", [128, 64], BF16))
        cs_bf = es.enter_context(nc.sbuf_tensor("cs_bf", [128, 1], BF16))
        zsc_bf = es.enter_context(nc.sbuf_tensor("zsc_bf", [1, 2], BF16))
        s2T = es.enter_context(nc.sbuf_tensor("s2T", [64, 128], BF16))
        Z_f = es.enter_context(nc.sbuf_tensor("Z_f", [64, 128], BF16))
        XZ_bf = es.enter_context(nc.sbuf_tensor("XZ_bf", [64, 128], BF16))
        XZT_f = es.enter_context(nc.sbuf_tensor("XZT_f", [64, 128], BF16))
        ZTf = es.enter_context(nc.sbuf_tensor("ZTf", [64, 128], BF16))
        T_f = es.enter_context(nc.sbuf_tensor("T_f", [64, 128], BF16))
        tiny = es.enter_context(nc.sbuf_tensor("tiny", [64, 64], F32))
        tiny2 = es.enter_context(nc.sbuf_tensor("tiny2", [1, 64], F32))
        zsc_bf32 = es.enter_context(nc.sbuf_tensor("zsc_bf32", [1, 32], BF16))
        ones_col = es.enter_context(nc.sbuf_tensor("ones_col", [128, 1], BF16))
        s3inv = es.enter_context(nc.sbuf_tensor("s3inv", [128, 8], F32))
        s3v_sb = es.enter_context(nc.sbuf_tensor("s3v_sb", [128, SB, NPAIR, 64], BF16))

        psA = es.enter_context(nc.psum_tensor("psA", [128, 3072], F32))
        psB = es.enter_context(nc.psum_tensor("psB", [128, 2048], BF16))

        def bank(i, cols=512, off=0):
            return psA[:, i * 512 + off : i * 512 + off + cols]

        def sc(i):
            return scal[:, i : i + 1]

        bankres = [Res(P) for _ in range(7)]
        psBres = [Res(P), Res(P)]
        resources = {}

        def res(name):
            if name not in resources:
                resources[name] = Res(P)
            return resources[name]

        ns.update(locals())
        ns["ln_rel"] = {}

        P.do("vector", lambda e: e.memset(eps_c[:, :], 1e-5), inc=("m_dve", 1))
        ns["eps_gate"] = P.val("m_dve")
        P.do("gpsimd", lambda e: e.memset(ones_col[:, :], 1.0), inc=("m_gp", 1))
        ns["ones_gate"] = P.val("m_gp")
        for ap_d, sb in ((ident_d, ident), (esum_d, e_sum), (rows_d, rows_s), (pic_d, pic_s), (brows_d, brows)):
            P.do("sync", lambda e, a=ap_d, s=sb: e.dma_start(out=s[:, :], in_=a[:, :]), inc=("s_c", 16))
        ns["C_TOTAL"] = P.val("s_c")
        wsem = {}
        for nm, ap_d, sb in (("wq", Wq_d, Wq_sb), ("wk", Wk_d, Wk_sb), ("wv", Wv_d, Wv_sb), ("wo", Wo_d, Wo_sb)):
            srcw = ap_d.rearrange("(kc p) n -> p kc n", p=128)
            wsem[nm] = P.do("gpsimd", lambda e, a=srcw, s=sb: e.dma_start(out=s[:, :, :], in_=a), inc=(f"s_{nm}", 16))
        wsem["bo"] = P.do("gpsimd", lambda e: e.dma_start(out=bo_bf[:, :], in_=bo_d.rearrange("(a n) -> a n", a=1)), inc=("s_bo", 16))
        ns["wsem"] = wsem
        ns["W1_TOTAL"] = 0
        ns["ea_row"] = rows_s[:, 0:128]
        ns["eb_row"] = rows_s[:, 128:256]
        ns["ones_row"] = rows_s[:, 256:384]

        if "P1" in phases:
            p1(P, ns)
        if "P2" in phases:
            p2a(P, ns)
            p2b(P, ns)
        if "P3A" in phases:
            p3a(P, ns)
        if "P3B" in phases:
            p3b(P, ns)
        if debug_spills:
            P.wait("gpsimd", "m_dve", P.val("m_dve"))
            P.wait("gpsimd", "m_act", P.val("m_act"))
            pairs = [("lmq", qlT), ("lmk", klT), ("g2", G2_bd), ("rstd", rstd2), ("nmr", nmr2), ("xnl", xnl), ("zt", ZT_st), ("s3v", s3v_sb)]
            for nm, t in pairs:
                if nm in dbg:
                    ap = dbg[nm]
                    P.do("gpsimd", lambda e, a=ap, t=t: e.dma_start(out=a, in_=t[:]), inc=("p_dbg", 16))

        for name in ("p_q", "p_k", "p_v", "p_x2", "p_wgs", "p_out", "p_dbg"):
            P.wait("gpsimd", name, P.cnt.get(name, 0))

        block = es.enter_context(nc.Block())
        P.emit(block)
    return nc



def emit_ln_batch(P, ns, items, a5_gate=None):
    sc, sq_scr = ns["sc"], ns["sq_scr"]
    SUMSQ, NEGSUM, NEGMEAN, NM2, VAR, LNV, MH, RSTD, NMR = range(9)

    def slot(it_, k):
        return sc(it_[4] * 16 + k)

    for it_ in items:
        rel = ns["ln_rel"].get(it_[4])
        if rel:
            for eng in ("scalar", "vector", "gpsimd"):
                P.wait(eng, *rel)
    a1 = d1 = None
    sq_eng = ns.get("ln_sq_eng", "scalar")
    for it_ in items:
        if sq_eng == "scalar":
            a1 = P.do("scalar", lambda e, x=it_[0], o=slot(it_, SUMSQ): e.activation(sq_scr[:, :], x, AF.Square, accum_out=o),
                      inc=("m_act", 1) if it_ is items[-1] else None)
        else:
            a1 = P.do("vector", lambda e, x=it_[0], o=slot(it_, SUMSQ): e.scalar_tensor_tensor(sq_scr[:, :], x, 1.0, x, op0=ALU.mult, op1=ALU.mult, accum_out=o),
                      inc=("m_dve", 1) if it_ is items[-1] else None)
    a1sem = "m_act" if sq_eng == "scalar" else "m_dve"
    for it_ in items:
        d1 = P.do("vector", lambda e, x=it_[0], o=slot(it_, NEGSUM): e.reduce_sum(o, x, axis=AX.X, negate=True),
                  inc=("m_dve", 1) if it_ is items[-1] else None)
    P.wait("scalar", "m_dve", d1)
    a2 = None
    for it_ in items:
        a2 = P.do("scalar", lambda e, i=slot(it_, NEGSUM), o=slot(it_, NEGMEAN): e.activation(o, i, AF.Identity, scale=1.0 / 1024.0),
                  inc=("m_act", 1) if it_ is items[-1] else None)
    P.wait("vector", "m_act", a2)
    d2 = None
    for it_ in items:
        d2 = P.do("vector", lambda e, i=slot(it_, NEGMEAN), o=slot(it_, NM2): e.tensor_tensor(o, i, i, ALU.mult),
                  inc=("m_dve", 1) if it_ is items[-1] else None)
    P.wait("gpsimd", a1sem, a1)
    P.wait("gpsimd", "m_dve", d2)
    g1 = None
    for it_ in items:
        P.do("gpsimd", lambda e, i=slot(it_, SUMSQ), o=slot(it_, VAR): e.tensor_scalar_mul(o, i, 1.0 / 1024.0))
        g1 = P.do("gpsimd", lambda e, i=slot(it_, NM2), o=slot(it_, VAR): e.tensor_sub(o, o, i),
                  inc=("m_gp", 1) if it_ is items[-1] else None)
    P.wait("scalar", "m_gp", g1)
    a3 = None
    for it_ in items:
        a3 = P.do("scalar", lambda e, i=slot(it_, VAR), o=slot(it_, LNV): e.activation(o, i, AF.Ln, bias=ns["eps_c"][:, :]),
                  inc=("m_act", 1) if it_ is items[-1] else None)
    P.wait("vector", "m_act", a3)
    d3 = None
    for it_ in items:
        d3 = P.do("vector", lambda e, i=slot(it_, LNV), o=slot(it_, MH): e.tensor_scalar(o, i, -0.5, None, op0=ALU.mult),
                  inc=("m_dve", 1) if it_ is items[-1] else None)
    P.wait("scalar", "m_dve", d3)
    a4 = None
    for it_ in items:
        r_out = it_[2] if it_[2] is not None else slot(it_, RSTD)
        a4 = P.do("scalar", lambda e, i=slot(it_, MH), o=r_out: e.activation(o, i, AF.Exp),
                  inc=("m_act", 1) if it_ is items[-1] else None)
    P.wait("vector", "m_act", a4)
    d4 = None
    for it_ in items:
        r_out = it_[2] if it_[2] is not None else slot(it_, RSTD)
        n_out = it_[3] if it_[3] is not None else slot(it_, NMR)
        d4 = P.do("vector", lambda e, i=slot(it_, NEGMEAN), r=r_out, o=n_out: e.tensor_tensor(o, i, r, ALU.mult),
                  inc=("m_dve", 1) if it_ is items[-1] else None)
    outs = []
    if any(it_[1] is not None for it_ in items):
        P.wait("scalar", "m_dve", d4)
        if a5_gate:
            P.wait("scalar", *a5_gate)
        for it_ in items:
            r_out = it_[2] if it_[2] is not None else slot(it_, RSTD)
            n_out = it_[3] if it_[3] is not None else slot(it_, NMR)
            a5 = P.do("scalar", lambda e, x=it_[0], o=it_[1], r=r_out, n=n_out: e.activation(o, x, AF.Identity, bias=n, scale=r),
                      inc=("m_act", 1))
            outs.append(a5)
            ns["ln_rel"][it_[4]] = ("m_act", a5)
    else:
        for it_ in items:
            ns["ln_rel"][it_[4]] = ("m_dve", d4)
        outs = [d4] * len(items)
    return outs


def p1(P, ns):
    x_in, x_t, xn_bf4, xnT_sb = ns["x_in"], ns["x_t"], ns["xn_bf4"], ns["xnT_sb"]
    qT_st, kT_st, v_st = ns["qT_st"], ns["kT_st"], ns["v_st"]
    Wq_sb, Wk_sb, Wv_sb = ns["Wq_sb"], ns["Wk_sb"], ns["Wv_sb"]
    q_sc, k_sc, v_sc = ns["q_sc"], ns["k_sc"], ns["v_sc"]
    ident, xnl = ns["ident"], ns["xnl"]
    psB = ns["psB"]
    bank, bankres, psBres, res = ns["bank"], ns["bankres"], ns["psBres"], ns["res"]

    x_res = [res("p1_x0"), res("p1_x1")]
    stage_res = res("p1_st")
    xnT_res = res("p1_xnT")
    NTILES = SB * NT
    last_tr = 0

    def load_x(i):
        s, it = divmod(i, NT)
        buf = i % 2
        x_res[buf].acquire("sync")
        srcv = x_in[s, it * 512 : (it + 1) * 512, :].rearrange("(g p) d -> p g d", p=128)
        return P.do("sync", lambda e, a=srcv, b=buf: e.dma_start(out=x_t[b][:, :, :], in_=a), inc=(f"s_x{buf}", 16))

    def emit_lnb(i, xv, gate_tr):
        buf = i % 2
        P.wait("scalar", f"s_x{buf}", xv)
        P.wait("vector", f"s_x{buf}", xv)
        P.wait("scalar", "m_dve", ns["eps_gate"])
        items = [(x_t[buf][:, gi, :], xn_bf4[:, gi, :], None, None, gi) for gi in range(NG)]
        outs = emit_ln_batch(P, ns, items, a5_gate=("m_pe", gate_tr))
        x_res[buf].used_until("m_act", outs[-1])
        x_res[buf].used_until("m_dve", P.val("m_dve"))
        return outs

    xvs = {0: load_x(0), 1: load_x(1)}
    lnv = {0: emit_lnb(0, xvs[0], 0)}

    for i in range(NTILES):
        s, it = divmod(i, NT)
        t0 = it * 512
        if i + 2 < NTILES:
            xvs[i + 2] = load_x(i + 2)

        P.wait("tensor", "s_c", ns["C_TOTAL"])
        first_evac = True
        for gi in range(NG):
            P.wait("tensor", "m_act", lnv[i][gi])
            for c in range(NCH):
                half = c // 4
                if c % 4 == 0:
                    psBres[half].acquire("tensor")
                dst = psB[:, half * 1024 + (c % 4) * 128 : half * 1024 + (c % 4) * 128 + 128]
                tv = P.do("tensor", lambda e, d=dst, c=c, gi=gi: e.transpose(d, xn_bf4[:, gi, c * 128 : (c + 1) * 128], ident[:, :]),
                          inc=("m_pe", 1) if c % 4 == 3 else None)
                if c % 4 == 3:
                    last_tr = tv
                    if first_evac:
                        xnT_res.acquire("vector")
                        first_evac = False
                    P.wait("vector", "m_pe", tv)
                    pv = psB[:, half * 1024 : half * 1024 + 512].rearrange("p (c t) -> p c t", c=4)
                    ov = xnT_sb[:, c - 3 : c + 1, gi * 128 : (gi + 1) * 128]
                    evv = P.do("vector", lambda e, pv=pv, ov=ov: e.tensor_copy(ov, pv), inc=("m_dve", 1))
                    psBres[half].used_until("m_dve", evv)
        xnT_ready = P.val("m_dve")

        if i + 1 < NTILES:
            lnv[i + 1] = emit_lnb(i + 1, xvs[i + 1], last_tr)

        P.wait("vector", "m_dve", xnT_ready)
        for c in range(NCH):
            P.do("vector", lambda e, c=c, s=s, it=it: e.tensor_reduce(
                xnl[:, s, c, it * 8 : it * 8 + 8].rearrange("p (n o) -> p n o", o=1),
                xnT_sb[:, c, :].rearrange("p (n w) -> p n w", w=64),
                axis=AX.X, op=ALU.add), inc=("m_dve", 1) if c == NCH - 1 else None)

        P.wait("tensor", "s_wq", 16)
        P.wait("tensor", "s_wk", 16)
        P.wait("tensor", "s_wv", 16)
        P.wait("tensor", "m_dve", xnT_ready)
        stage = stage_res
        stage.acquire("tensor")
        for W_sb, dst_st, qscale in ((Wq_sb, qT_st[0], True), (Wk_sb, kT_st[0], False)):
            ev = None
            for c in range(NCH):
                bk = c % 4
                bankres[bk].acquire("tensor")
                mv = None
                for kc in range(NCH):
                    mv = P.do("tensor", lambda e, W=W_sb, c=c, kc=kc, bk=bk: e.matmul(
                        bank(bk), W[:, kc, c * 128 : (c + 1) * 128], xnT_sb[:, kc, :],
                        start=(kc == 0), stop=(kc == NCH - 1)),
                        inc=("m_pe", 1) if kc == NCH - 1 else None)
                if qscale:
                    P.wait("scalar", "m_pe", mv)
                    ev = P.do("scalar", lambda e, c=c, bk=bk, d=dst_st: e.activation(d[:, c, :], bank(bk), AF.Identity, scale=0.125), inc=("m_act", 1))
                    bankres[bk].used_until("m_act", ev)
                else:
                    P.wait("vector", "m_pe", mv)
                    ev = P.do("vector", lambda e, c=c, bk=bk, d=dst_st: e.tensor_copy(d[:, c, :], bank(bk)), inc=("m_dve", 1))
                    bankres[bk].used_until("m_dve", ev)
            if qscale:
                P.wait("scalar", "m_act", ev)
                dq = q_sc[s, it].rearrange("c p t -> p c t")
                sp = P.do("scalar", lambda e, d=dq: e.dma_start(out=d, in_=qT_st[0][:, :, :]), inc=("p_q", 16))
                stage.used_until("p_q", sp)
            else:
                P.wait("scalar", "m_dve", ev)
                dk = k_sc[s, :, :, t0 : t0 + 512].rearrange("c p t -> p c t")
                sp = P.do("scalar", lambda e, d=dk: e.dma_start(out=d, in_=kT_st[0][:, :, :]), inc=("p_k", 16))
                stage.used_until("p_k", sp)
        vev = None
        for gi in range(NG):
            for hf in range(2):
                bk = 4 + (gi * 2 + hf) % 2
                bankres[bk].acquire("tensor")
                mv = None
                for kc in range(NCH):
                    mv = P.do("tensor", lambda e, gi=gi, hf=hf, kc=kc, bk=bk: e.matmul(
                        bank(bk), xnT_sb[:, kc, gi * 128 : (gi + 1) * 128],
                        Wv_sb[:, kc, hf * 512 : (hf + 1) * 512],
                        start=(kc == 0), stop=(kc == NCH - 1)),
                        inc=("m_pe", 1) if kc == NCH - 1 else None)
                P.wait("scalar", "m_pe", mv)
                vev = P.do("scalar", lambda e, gi=gi, hf=hf, bk=bk, d=v_st[0]: e.activation(
                    d[:, gi, hf * 512 : (hf + 1) * 512], bank(bk), AF.Identity), inc=("m_act", 1))
                bankres[bk].used_until("m_act", vev)
        xnT_res.used_until("m_pe", P.val("m_pe"))
        P.wait("sync", "m_act", vev)
        dvv = v_sc[s, t0 : t0 + 512, :].rearrange("(g p) d -> p g d", p=128)
        sp3 = P.do("sync", lambda e, d=dvv: e.dma_start(out=d, in_=v_st[0][:, :, :]), inc=("p_v", 16))
        stage.used_until("p_v", sp3)
        if it == NT - 1:
            ns.setdefault("pk_s", {})[s] = P.val("p_k")


def p2a(P, ns):
    Wq_sb, Wk_sb = ns["Wq_sb"], ns["Wk_sb"]
    xnl, xnl_bf, qlT, klT, klT_bd = ns["xnl"], ns["xnl_bf"], ns["qlT"], ns["klT"], ns["klT_bd"]
    bank, bankres = ns["bank"], ns["bankres"]
    G2_bd = ns["G2_bd"]

    P.wait("scalar", "m_dve", P.val("m_dve"))
    cv = P.do("scalar", lambda e: e.copy(
        xnl_bf[:].rearrange("p a b c -> p (a b c)"), xnl[:].rearrange("p a b c -> p (a b c)")), inc=("m_act", 1))
    P.wait("tensor", "m_act", cv)

    for s in range(SB):
        for W_sb, dst, scale in ((Wq_sb, qlT, 0.125 / 64.0), (Wk_sb, klT, 1.0 / 64.0)):
            for c in range(NPAIR):
                bk = c % 4
                bankres[bk].acquire("tensor")
                mv = None
                for kc in range(NCH):
                    mv = P.do("tensor", lambda e, W=W_sb, s=s, c=c, kc=kc, bk=bk: e.matmul(
                        bank(bk, 64), W[:, kc, c * 128 : (c + 1) * 128], xnl_bf[:, s, kc, :],
                        start=(kc == 0), stop=(kc == NCH - 1)),
                        inc=("m_pe", 1) if kc == NCH - 1 else None)
                P.wait("scalar", "m_pe", mv)
                ev = P.do("scalar", lambda e, s=s, c=c, bk=bk, d=dst, sca=scale: e.activation(
                    d[:, s, c, :], bank(bk, 64), AF.Identity, scale=sca), inc=("m_act", 1))
                bankres[bk].used_until("m_act", ev)

    for s in range(SB):
        for c in range(NPAIR):
            P.do("gpsimd", lambda e, s=s, c=c: e.memset(klT_bd[:, s, c, :], 0.0))
            P.do("gpsimd", lambda e, s=s, c=c: e.memset(G2_bd[:, s, c, :], 0.0))
    gz = P.do("gpsimd", lambda e: e.engine_nop(), inc=("m_gp", 1))
    P.wait("vector", "m_gp", gz)
    P.wait("vector", "m_act", P.val("m_act"))
    for s in range(SB):
        for c in range(NPAIR):
            P.do("vector", lambda e, s=s, c=c: e.tensor_copy(klT_bd[0:64, s, c, 0:64], klT[0:64, s, c, :]))
            P.do("vector", lambda e, s=s, c=c: e.tensor_copy(klT_bd[64:128, s, c, 64:128], klT[64:128, s, c, :]))
    P.do("vector", lambda e: e.engine_nop(), inc=("m_dve", 1))

    import itertools
    ns["p2_gen"] = itertools.chain(pinv_gen(P, ns), factory_gen(P, ns))
    next(ns["p2_gen"])


def pinv_gen(P, ns):
    qlT, klT = ns["qlT"], ns["klT"]
    bank, bankres, psBres = ns["bank"], ns["bankres"], ns["psBres"]
    qlbd_all, s2e_all, s2bf_all, s2T_all = ns["qlbd_all"], ns["s2e_all"], ns["s2bf_all"], ns["s2T_all"]
    Z_all, ZT_all, XZb_all, XZT_all, T_all = ns["Z_all"], ns["ZT_all"], ns["XZb_all"], ns["XZT_all"], ns["T_all"]
    zscb, csbf_all = ns["zscb"], ns["csbf_all"]
    tiny, tiny2, zsc_bf32 = ns["tiny"], ns["tiny2"], ns["zsc_bf32"]
    ident, e_sum, pic_s = ns["ident"], ns["e_sum"], ns["pic_s"]
    ZT_st = ns["ZT_st"]
    psB, psA = ns["psB"], ns["psA"]
    scal, sc = ns["scal"], ns["sc"]
    ones_row = ns["ones_row"]
    NI = SB * NPAIR
    inst_sc = [(s, c) for s in range(SB) for c in range(NPAIR)]

    gates = {nm: P.val(nm) for nm in ("m_pe", "m_act", "m_dve", "p_q", "p_k", "p_v")}
    yield
    for eng in ("gpsimd", "vector", "scalar", "tensor"):
        for nm, v in gates.items():
            P.wait(eng, nm, v)

    for i in range(NI):
        P.do("gpsimd", lambda e, i=i: e.memset(qlbd_all[:, i, :], 0.0))
    gz = P.do("gpsimd", lambda e: e.engine_nop(), inc=("m_gp", 1))
    P.wait("vector", "m_gp", gz)
    P.wait("vector", "m_act", P.val("m_act"))
    d1 = None
    for i, (s, c) in enumerate(inst_sc):
        P.do("vector", lambda e, i=i, s=s, c=c: e.tensor_copy(qlbd_all[0:64, i, 0:64], qlT[0:64, s, c, :]))
        d1 = P.do("vector", lambda e, i=i, s=s, c=c: e.tensor_copy(qlbd_all[64:128, i, 64:128], qlT[64:128, s, c, :]),
                  inc=("m_dve", 1) if i == NI - 1 else None)

    yield
    for j in (2, 3):
        bankres[j].acquire("tensor")
    P.wait("tensor", "m_dve", d1)
    mv = None
    for i, (s, c) in enumerate(inst_sc):
        mv = P.do("tensor", lambda e, i=i, s=s, c=c: e.matmul(
            bank(2 + i // 8, 64, (i % 8) * 64), qlbd_all[:, i, :], klT[:, s, c, :],
            start=True, stop=True), inc=("m_pe", 1) if i == NI - 1 else None)
    P.wait("vector", "m_pe", mv)
    d2 = None
    for i in range(NI):
        d2 = P.do("vector", lambda e, i=i: e.reduce_max(scal[:, i : i + 1], bank(2 + i // 8, 64, (i % 8) * 64), axis=AX.X, negate=True),
                  inc=("m_dve", 1) if i == NI - 1 else None)
    P.wait("scalar", "m_dve", d2)
    a1 = None
    for i in range(NI):
        a1 = P.do("scalar", lambda e, i=i: e.activation(s2e_all[:, i, :], bank(2 + i // 8, 64, (i % 8) * 64), AF.Exp,
                                                        bias=scal[:, i : i + 1], accum_out=scal[:, 16 + i : 17 + i]),
                  inc=("m_act", 1) if i == NI - 1 else None)
    for j in (2, 3):
        bankres[j].used_until("m_act", a1)
    yield
    P.wait("vector", "m_act", a1)
    d3 = P.do("vector", lambda e: e.reciprocal(scal[:, 32:48], scal[:, 16:32]), inc=("m_dve", 1))
    P.wait("scalar", "m_dve", d3)
    a2 = None
    for i in range(NI):
        a2 = P.do("scalar", lambda e, i=i: e.activation(s2bf_all[:, i, :], s2e_all[:, i, :], AF.Identity, scale=scal[:, 32 + i : 33 + i]),
                  inc=("m_act", 1) if i == NI - 1 else None)
    P.wait("tensor", "m_act", a2)
    d4 = None
    for rnd in range(2):
        psBres[0].acquire("tensor")
        psBres[1].acquire("tensor")
        tv = None
        for i in range(rnd * 8, rnd * 8 + 8):
            tv = P.do("tensor", lambda e, i=i, rnd=rnd: e.transpose(psB[0:64, (i - rnd * 8) * 128 : (i - rnd * 8 + 1) * 128], s2bf_all[:, i, :], ident[:, :]),
                      inc=("m_pe", 1) if i == rnd * 8 + 7 else None)
        P.wait("vector", "m_pe", tv)
        d4 = P.do("vector", lambda e, rnd=rnd: e.tensor_copy(s2T_all[:].rearrange("p a b -> p (a b)")[:, rnd * 1024 : (rnd + 1) * 1024], psB[0:64, 0:1024]), inc=("m_dve", 1))
        psBres[0].used_until("m_dve", d4)
        psBres[1].used_until("m_dve", d4)
        yield
    P.wait("vector", "m_act", a2)
    d5 = None
    for i in range(NI):
        d5 = P.do("vector", lambda e, i=i: e.reduce_sum(scal[:, 48 + i : 49 + i], s2bf_all[:, i, :], axis=AX.X),
                  inc=("m_dve", 1) if i == NI - 1 else None)
    P.wait("scalar", "m_dve", d5)
    a3 = P.do("scalar", lambda e: e.copy(csbf_all[:, :], scal[:, 48:64]), inc=("m_act", 1))
    P.wait("tensor", "m_act", a3)
    d6 = None
    for rnd in range(2):
        psBres[0].acquire("tensor")
        psBres[1].acquire("tensor")
        tv = None
        for i in range(rnd * 8, rnd * 8 + 8):
            j = i - rnd * 8
            tv = P.do("tensor", lambda e, i=i, j=j: e.transpose(psB[0:1, j * 128 : (j + 1) * 128], csbf_all[:, i : i + 1], ident[:, :]),
                      inc=("m_pe", 1) if j == 7 else None)
        P.wait("vector", "m_pe", tv)
        for i in range(rnd * 8, rnd * 8 + 8):
            j = i - rnd * 8
            P.do("vector", lambda e, i=i, j=j: e.reduce_max(tiny[0:1, i : i + 1], psB[0:1, j * 128 : j * 128 + 64], axis=AX.X))
            d6 = P.do("vector", lambda e, i=i, j=j: e.reduce_max(tiny[0:1, 16 + i : 17 + i], psB[0:1, j * 128 + 64 : j * 128 + 128], axis=AX.X),
                      inc=("m_dve", 1) if j == 7 else None)
        psBres[0].used_until("m_dve", d6)
        psBres[1].used_until("m_dve", d6)
        yield
    yield
    for j in (2, 3):
        bankres[j].acquire("tensor")
    P.wait("tensor", "m_act", a2)
    mv = None
    for i in range(NI):
        mv = P.do("tensor", lambda e, i=i: e.matmul(bank(2 + i // 8, 64, (i % 8) * 64), e_sum[:, :], s2bf_all[:, i, :],
                                                    start=True, stop=True), inc=("m_pe", 1) if i == NI - 1 else None)
    P.wait("vector", "m_pe", mv)
    d7 = None
    for i in range(NI):
        pb = bank(2 + i // 8, 64, (i % 8) * 64)
        P.do("vector", lambda e, i=i, pb=pb: e.reduce_max(tiny[0:1, 32 + i : 33 + i], pb[0:1, :], axis=AX.X))
        d7 = P.do("vector", lambda e, i=i, pb=pb: e.reduce_max(tiny[0:1, 48 + i : 49 + i], pb[64:65, :], axis=AX.X),
                  inc=("m_dve", 1) if i == NI - 1 else None)
    for j in (2, 3):
        bankres[j].used_until("m_dve", d7)
    P.wait("gpsimd", "m_dve", d7)
    P.do("gpsimd", lambda e: e.tensor_mul(tiny2[0:1, 0:16], tiny[0:1, 0:16], tiny[0:1, 32:48]))
    g2 = P.do("gpsimd", lambda e: e.tensor_mul(tiny2[0:1, 16:32], tiny[0:1, 16:32], tiny[0:1, 48:64]), inc=("m_gp", 1))
    P.wait("vector", "m_gp", g2)
    d8 = P.do("vector", lambda e: e.reciprocal(tiny2[0:1, 32:64], tiny2[0:1, 0:32]), inc=("m_dve", 1))
    P.wait("scalar", "m_dve", d8)
    a4 = P.do("scalar", lambda e: e.copy(zsc_bf32[:, :], tiny2[0:1, 32:64]), inc=("m_act", 1))
    yield
    bankres[2].acquire("tensor")
    P.wait("tensor", "m_act", a4)
    mv = None
    for j in range(32):
        mv = P.do("tensor", lambda e, j=j: e.matmul(bank(2, 1, j)[0:64, :], ones_row[:, 0:64], zsc_bf32[:, j : j + 1],
                                                    start=True, stop=True), inc=("m_pe", 1) if j == 31 else None)
    P.wait("vector", "m_pe", mv)
    d9 = P.do("vector", lambda e: e.tensor_copy(zscb[:, :], bank(2, 32, 0)[0:64, :]), inc=("m_dve", 1))
    bankres[2].used_until("m_dve", d9)
    P.wait("scalar", "m_dve", max(d9, d4))
    a5 = None
    for i in range(NI):
        P.do("scalar", lambda e, i=i: e.activation(Z_all[:, i, 0:64], s2T_all[:, i, 0:64], AF.Identity, scale=zscb[:, i : i + 1]))
        a5 = P.do("scalar", lambda e, i=i: e.activation(Z_all[:, i, 64:128], s2T_all[:, i, 64:128], AF.Identity, scale=zscb[:, 16 + i : 17 + i]),
                  inc=("m_act", 1) if i == NI - 1 else None)

    def tr_all(src_t, dst_t, gname, gval):
        P.wait("tensor", gname, gval)
        ev = None
        for rnd in range(2):
            psBres[0].acquire("tensor")
            psBres[1].acquire("tensor")
            tv = None
            for i in range(rnd * 8, rnd * 8 + 8):
                for h in range(2):
                    j = (i - rnd * 8) * 2 + h
                    tv = P.do("tensor", lambda e, i=i, h=h, j=j: e.transpose(
                        psB[0:64, j * 64 : (j + 1) * 64],
                        src_t[:, i, h * 64 : (h + 1) * 64], ident[0:64, 0:64]),
                        inc=("m_pe", 1) if j == 15 else None)
            P.wait("vector", "m_pe", tv)
            ev = P.do("vector", lambda e, rnd=rnd: e.tensor_copy(dst_t[:].rearrange("p a b -> p (a b)")[:, rnd * 1024 : (rnd + 1) * 1024], psB[0:64, 0:1024]), inc=("m_dve", 1))
            psBres[0].used_until("m_dve", ev)
            psBres[1].used_until("m_dve", ev)
        return ev

    yield
    zt = tr_all(Z_all, ZT_all, "m_act", a5)

    z_gate = ("m_act", a5)
    lz = None
    for _ in range(6):
        def rounds(lhs_t, rhs_t, gname, gval, post):
            P.wait("tensor", gname, gval)
            out = None
            for rnd in range(2):
                for j in (2, 3):
                    bankres[j].acquire("tensor")
                mv = None
                for i in range(rnd * 8, rnd * 8 + 8):
                    for h in range(2):
                        col = (i - rnd * 8) * 128 + h * 64
                        last = (i == rnd * 8 + 7 and h == 1)
                        mv = P.do("tensor", lambda e, i=i, h=h, col=col, lhs_t=lhs_t, rhs_t=rhs_t: e.matmul(
                            psA[0:64, 1024 + col : 1024 + col + 64],
                            lhs_t[:, i, h * 64 : (h + 1) * 64], rhs_t[:, i, h * 64 : (h + 1) * 64],
                            start=True, stop=True), inc=("m_pe", 1) if last else None)
                dv, out = post(rnd, mv)
                for j in (2, 3):
                    bankres[j].used_until("m_dve", dv)
            return out

        def t_post(const_off):
            def post(rnd, mv):
                P.wait("vector", "m_pe", mv)
                ev = None
                for b in range(2):
                    pv = psA[0:64, 1024 + b * 512 : 1024 + (b + 1) * 512]
                    ev = P.do("vector", lambda e, b=b, rnd=rnd, pv=pv: e.scalar_tensor_tensor(
                        T_all[:].rearrange("p a b -> p (a b)")[:, rnd * 1024 + b * 512 : rnd * 1024 + (b + 1) * 512],
                        pv, -1.0, pic_s[:, const_off : const_off + 512],
                        op0=ALU.mult, op1=ALU.add), inc=("m_dve", 1) if b == 1 else None)
                return ev, ev
            return post

        def xz_post(rnd, mv):
            P.wait("vector", "m_pe", mv)
            for b in range(2):
                pv = psA[0:64, 1024 + b * 512 : 1024 + (b + 1) * 512]
                P.do("vector", lambda e, b=b, rnd=rnd, pv=pv: e.tensor_copy(
                    XZb_all[:].rearrange("p a b -> p (a b)")[:, rnd * 1024 + b * 512 : rnd * 1024 + (b + 1) * 512], pv))
            ev = None
            for b in range(2):
                pv = psA[0:64, 1024 + b * 512 : 1024 + (b + 1) * 512]
                ev = P.do("vector", lambda e, b=b, rnd=rnd, pv=pv: e.scalar_tensor_tensor(
                    T_all[:].rearrange("p a b -> p (a b)")[:, rnd * 1024 + b * 512 : rnd * 1024 + (b + 1) * 512],
                    pv, -1.0, pic_s[:, 0:512],
                    op0=ALU.mult, op1=ALU.add), inc=("m_dve", 1) if b == 1 else None)
            return ev, ev

        def sc_post(rnd, mv):
            P.wait("vector", "m_pe", mv)
            ev = None
            for b in range(2):
                pv = psA[0:64, 1024 + b * 512 : 1024 + (b + 1) * 512]
                ev = P.do("vector", lambda e, b=b, rnd=rnd, pv=pv: e.tensor_scalar(
                    Z_all[:].rearrange("p a b -> p (a b)")[:, rnd * 1024 + b * 512 : rnd * 1024 + (b + 1) * 512],
                    pv, 0.25, None, op0=ALU.mult), inc=("m_dve", 1) if b == 1 else None)
            return ev, ev

        yield
        t1 = rounds(s2T_all, Z_all, z_gate[0], z_gate[1], xz_post)
        yield
        xzt = tr_all(XZb_all, XZT_all, "m_dve", t1)
        yield
        t2 = rounds(XZT_all, T_all, "m_dve", max(t1, xzt), t_post(512))
        yield
        t3 = rounds(XZT_all, T_all, "m_dve", t2, t_post(1024))
        yield
        lz = rounds(ZT_all, T_all, "m_dve", t3, sc_post)
        z_gate = ("m_dve", lz)
        yield
        zt = tr_all(Z_all, ZT_all, "m_dve", lz)

    yield
    psBres[0].acquire("tensor")
    psBres[1].acquire("tensor")
    P.wait("tensor", "m_dve", lz)
    tv = None
    for i in range(NI):
        tv = P.do("tensor", lambda e, i=i: e.transpose(psB[0:128, i * 64 : (i + 1) * 64], Z_all[:, i, :], ident[0:64, 0:64]),
                  inc=("m_pe", 1) if i == NI - 1 else None)
    P.wait("vector", "m_pe", tv)
    ev = None
    for i, (s, c) in enumerate(inst_sc):
        ev = P.do("vector", lambda e, i=i, s=s, c=c: e.tensor_copy(ZT_st[:, s, c, :], psB[0:128, i * 64 : (i + 1) * 64]),
                  inc=("m_dve", 1) if i == NI - 1 else None)
    psBres[0].used_until("m_dve", ev)
    psBres[1].used_until("m_dve", ev)


def factory_gen(P, ns):
    wdT_sb, ident = ns["wdT_sb"], ns["ident"]
    psB, psBres = ns["psB"], ns["psBres"]
    wg_d, wu_d, wd_d = ns["wg_d"], ns["wu_d"], ns["wd_d"]
    wgt_sc, wut_sc = ns["wgt_sc"], ns["wut_sc"]
    res = ns["res"]
    chunk = [ns["wg_st"][i][:].rearrange("p a b -> p (a b)") for i in range(2)]
    unit = ns["wu_st"]

    gates = {nm: P.val(nm) for nm in ("m_pe", "m_act", "m_dve")}
    yield
    for nm, v in gates.items():
        P.wait("gpsimd", nm, v)

    ch_res = [res("fch0"), res("fch1")]
    wd_loads = [(wd_d, None, dc, q3) for dc in range(NCH) for q3 in range(3)]
    wgu_loads = [(wg_d, wgt_sc, hc, None) for hc in range(NHC)] + [(wu_d, wut_sc, hc, None) for hc in range(NHC)]
    all_loads = wd_loads + wgu_loads
    lvs = {}

    def issue_load(j):
        w_d2, _, a, q3 = all_loads[j]
        b = j % 2
        ch_res[b].acquire("gpsimd")
        if q3 is not None:
            lvs[j] = P.do("gpsimd", lambda e, a=a, q3=q3, b=b: e.dma_start(
                out=chunk[b], in_=w_d2[a * 128 : (a + 1) * 128, q3 * 1024 : (q3 + 1) * 1024]), inc=(f"s_wl{b}", 16))
        else:
            lvs[j] = P.do("gpsimd", lambda e, a=a, b=b, w=w_d2: e.dma_start(
                out=chunk[b], in_=w[a * 128 : (a + 1) * 128, :]), inc=(f"s_wl{b}", 16))

    issue_load(0)
    for jj in range(len(wd_loads)):
        dc, q3 = wd_loads[jj][2], wd_loads[jj][3]
        b = jj % 2
        if jj + 1 < len(all_loads):
            issue_load(jj + 1)
        P.wait("tensor", f"s_wl{b}", lvs[jj])
        last_tv = None
        for j in range(NCH):
            hc = q3 * 8 + j
            half = j // 4
            if j % 4 == 0:
                psBres[half].acquire("tensor")
            pslot = psB[:, half * 1024 + (j % 4) * 128 : half * 1024 + (j % 4) * 128 + 128]
            tv = P.do("tensor", lambda e, j=j, b=b, p=pslot: e.transpose(p, chunk[b][:, j * 128 : (j + 1) * 128], ident[:, :]),
                      inc=("m_pe", 1) if j % 4 == 3 else None)
            if j % 4 == 3:
                last_tv = tv
                P.wait("vector", "m_pe", tv)
                pv = psB[:, half * 1024 : half * 1024 + 512].rearrange("p (c t) -> p c t", c=4)
                ov = wdT_sb[:, hc - 3 : hc + 1, dc * 128 : (dc + 1) * 128]
                evv = P.do("vector", lambda e, pv=pv, ov=ov: e.tensor_copy(ov, pv), inc=("m_dve", 1))
                psBres[half].used_until("m_dve", evv)
        ch_res[b].used_until("m_pe", last_tv)
        yield

    unit_res = [res("fu0"), res("fu1")]
    for j2 in range(len(wgu_loads)):
        jj = len(wd_loads) + j2
        w_d, w_sc, hc, _ = all_loads[jj]
        b = jj % 2
        if jj + 1 < len(all_loads):
            issue_load(jj + 1)
        P.wait("tensor", f"s_wl{b}", lvs[jj])
        unit_res[b].acquire("vector")
        last_tv = None
        for kc in range(NCH):
            half = kc // 4
            if kc % 4 == 0:
                psBres[half].acquire("tensor")
            pslot = psB[:, half * 1024 + (kc % 4) * 128 : half * 1024 + (kc % 4) * 128 + 128]
            tv = P.do("tensor", lambda e, kc=kc, b=b, p=pslot: e.transpose(p, chunk[b][:, kc * 128 : (kc + 1) * 128], ident[:, :]),
                      inc=("m_pe", 1) if kc % 4 == 3 else None)
            if kc % 4 == 3:
                last_tv = tv
                P.wait("vector", "m_pe", tv)
                pv = psB[:, half * 1024 : half * 1024 + 512].rearrange("p (c t) -> p c t", c=4)
                evv = P.do("vector", lambda e, kc=kc, b=b, pv=pv: e.tensor_copy(unit[b][:, kc - 3 : kc + 1, :], pv), inc=("m_dve", 1))
                psBres[half].used_until("m_dve", evv)
        ch_res[b].used_until("m_pe", last_tv)
        dv = P.val("m_dve")
        P.wait("gpsimd", "m_dve", dv)
        sp = P.do("gpsimd", lambda e, w=w_sc, hc=hc, b=b: e.dma_start(out=w[hc], in_=unit[b][:, :, :]), inc=("p_wgs", 16))
        unit_res[b].used_until("p_wgs", sp)
        yield


def p2b(P, ns):
    kT_pair, ET_all, v4 = ns["kT_pair"], ns["ET_all"], ns["v4"]
    ones_col = ns["ones_col"]
    qlT, qlbd = ns["qlT"], ns["qlT_bd"]
    k_sc, v_sc = ns["k_sc"], ns["v_sc"]
    bank, bankres = ns["bank"], ns["bankres"]
    psA = ns["psA"]
    G2_bd, s3inv, s3v_sb, ZT_st = ns["G2_bd"], ns["s3inv"], ns["s3v_sb"], ns["ZT_st"]
    res = ns["res"]
    gen = ns["p2_gen"]

    def pump(n=1):
        if DBG["p2_nopinv"]:
            return
        for _ in range(n):
            try:
                next(gen)
            except StopIteration:
                break

    kt_res = [res("kt0"), res("kt1")]
    ve_res = [res("ve0"), res("ve1")]
    pairs = [(s_, c_) for s_ in range(SB) for c_ in range(NPAIR)]
    kvs = {}

    def load_kt(pi):
        s_, c_ = pairs[pi]
        kb_ = pi % 2
        kt_res[kb_].acquire("scalar")
        P.wait("scalar", "p_k", ns["pk_s"][s_])
        kvs[pi] = P.do("scalar", lambda e, s=s_, c=c_, kb=kb_: e.dma_start(
            out=kT_pair[kb][:, :], in_=k_sc[s, c]), inc=(f"s_kt{kb_}", 16))

    load_kt(0)
    load_kt(1)

    for eng in ("sync", "scalar", "vector"):
        P.wait(eng, "p_q", P.val("p_q"))
        P.wait(eng, "p_k", P.val("p_k"))
        P.wait(eng, "p_v", P.val("p_v"))

    if DBG["p2_nointerleave"]:
        pump(10 ** 6)
    prev_sweep = 0
    for s in range(SB):
        for c in range(NPAIR):
            kb = c % 2
            kv = kvs[s * NPAIR + c]
            prev_pe = P.val("m_pe")
            P.wait("gpsimd", "m_pe", prev_pe)
            gz = P.do("gpsimd", lambda e: e.memset(qlbd[:, :], 0.0), inc=("m_gp", 1))
            P.wait("vector", "m_gp", gz)
            P.do("vector", lambda e, s=s, c=c: e.tensor_copy(qlbd[0:64, 0:64], qlT[0:64, s, c, :]))
            qb = P.do("vector", lambda e, s=s, c=c: e.tensor_copy(qlbd[64:128, 64:128], qlT[64:128, s, c, :]), inc=("m_dve", 1))
            P.wait("tensor", f"s_kt{kb}", kv)
            P.wait("tensor", "m_dve", qb)
            P.wait("tensor", "m_gp", ns["ones_gate"])
            P.wait("scalar", "m_pe", prev_sweep)
            if c == 0:
                bankres[4].acquire("tensor")
            last = None
            exps = []
            for lt in range(32):
                bk = (lt // 4) % 2
                if lt % 4 == 0:
                    bankres[bk].acquire("tensor")
                mv = P.do("tensor", lambda e, lt=lt, bk=bk, kb=kb: e.matmul(
                    bank(bk, 128, (lt % 4) * 128), kT_pair[kb][:, lt * 128 : (lt + 1) * 128], qlbd[:, :],
                    start=True, stop=True), inc=("m_pe", 1) if lt % 4 == 3 else None)
                last = mv
                if lt % 4 == 3:
                    P.wait("scalar", "m_pe", mv)
                    ov = ET_all[:, lt - 3 : lt + 1, c * 128 : (c + 1) * 128]
                    evv = P.do("scalar", lambda e, bk=bk, ov=ov: e.activation(
                        ov, bank(bk).rearrange("p (a b) -> p a b", a=4), AF.Exp), inc=("m_act", 1))
                    bankres[bk].used_until("m_act", evv)
                    exps.append(evv)
                    pump(1)
                    if len(exps) > 1:
                        P.wait("tensor", "m_act", exps[-2])
                        for l2 in range(lt - 7, lt - 3):
                            P.do("tensor", lambda e, l2=l2, c=c: e.matmul(
                                bank(4, 1, c), ET_all[:, l2, c * 128 : (c + 1) * 128], ones_col[:, :],
                                start=(l2 == 0), stop=False))
            P.wait("tensor", "m_act", exps[-1])
            for l2 in range(28, 32):
                P.do("tensor", lambda e, l2=l2, c=c: e.matmul(
                    bank(4, 1, c), ET_all[:, l2, c * 128 : (c + 1) * 128], ones_col[:, :],
                    start=False, stop=(l2 == 31)), inc=("m_pe", 1) if l2 == 31 else None)
            kt_res[kb].used_until("m_pe", last)
            if s * NPAIR + c + 2 < len(pairs):
                load_kt(s * NPAIR + c + 2)

        P.wait("tensor", "m_act", P.val("m_act"))
        for j in range(2):
            bankres[j].acquire("tensor")
        for vt in range(8):
            b = vt % 2
            ve_res[b].acquire("sync")
            srcv = v_sc[s, vt * 512 : (vt + 1) * 512, :].rearrange("(g p) d -> p g d", p=128)
            lv = P.do("sync", lambda e, a=srcv, b=b: e.dma_start(out=v4[b][:, :, :], in_=a), inc=(f"s_v{b}", 16))
            P.wait("tensor", f"s_v{b}", lv)
            mv = None
            for g in range(NG):
                lt = vt * 4 + g
                for c in range(NPAIR):
                    mv = P.do("tensor", lambda e, lt=lt, g=g, c=c, b=b: e.matmul(
                        psA[:, (c // 4) * 512 + (c % 4) * 128 : (c // 4) * 512 + (c % 4) * 128 + 128],
                        ET_all[:, lt, c * 128 : (c + 1) * 128], v4[b][:, g, c * 128 : (c + 1) * 128],
                        start=(lt == 0), stop=(lt == 31)),
                        inc=("m_pe", 1) if (c == NPAIR - 1 and g == NG - 1) else None)
            ve_res[b].used_until("m_pe", mv)
            pump(2)
        prev_sweep = P.val("m_pe")

        P.wait("vector", "m_act", P.val("m_act"))
        sv = None
        for c in range(NPAIR):
            pb = psA[:, (c // 4) * 512 + (c % 4) * 128 : (c // 4) * 512 + (c % 4) * 128 + 128]
            P.wait("vector", "m_pe", prev_sweep)
            rv = P.do("vector", lambda e, c=c: e.reciprocal(s3inv[:, c : c + 1], bank(4, 1, c)), inc=("m_dve", 1))
            P.wait("scalar", "m_dve", rv)
            P.wait("scalar", "m_pe", prev_sweep)
            P.do("scalar", lambda e, pb=pb, s=s, c=c: e.activation(s3v_sb[0:64, s, c, :], pb[0:64, 0:64], AF.Identity, scale=s3inv[0:64, c : c + 1]))
            sv = P.do("scalar", lambda e, pb=pb, s=s, c=c: e.activation(s3v_sb[64:128, s, c, :], pb[64:128, 64:128], AF.Identity, scale=s3inv[64:128, c : c + 1]), inc=("m_act", 1))
            if c == NPAIR - 1:
                bankres[4].used_until("m_dve", P.val("m_dve"))
                bankres[4].used_until("m_act", sv)
                for j in range(2):
                    bankres[j].used_until("m_act", sv)
            pump(1)

    pump(10 ** 6)

    P.wait("tensor", "m_act", P.val("m_act"))
    P.wait("tensor", "m_dve", P.val("m_dve"))
    for s in range(SB):
        for c in range(NPAIR):
            bankres[5].acquire("tensor")
            P.do("tensor", lambda e, s=s, c=c: e.matmul(bank(5, 64, 0)[0:64, :], ZT_st[0:64, s, c, :], s3v_sb[0:64, s, c, :], start=True, stop=True))
            gm = P.do("tensor", lambda e, s=s, c=c: e.matmul(bank(5, 64, 64)[64:128, :], ZT_st[64:128, s, c, :], s3v_sb[64:128, s, c, :], start=True, stop=True), inc=("m_pe", 1))
            P.wait("vector", "m_pe", gm)
            P.do("vector", lambda e, s=s, c=c: e.tensor_copy(G2_bd[0:64, s, c, 0:64], bank(5, 64, 0)[0:64, :]))
            ge = P.do("vector", lambda e, s=s, c=c: e.tensor_copy(G2_bd[64:128, s, c, 64:128], bank(5, 64, 64)[64:128, :]), inc=("m_dve", 1))
            bankres[5].used_until("m_dve", ge)


def p3a(P, ns):
    qT_t3, x_t3, attnoT = ns["qT_t3"], ns["x_t3"], ns["attnoT"]
    E1all = ns["E1all"]
    nlv, invb, E1n2 = ns["nlv"], ns["invb"], ns["E1n2"]
    x2_sb, sq_scr = ns["x2_sb"], ns["sq_scr"]
    klT_bd, G2_bd = ns["klT_bd"], ns["G2_bd"]
    Wo_sb, bo_bf = ns["Wo_sb"], ns["bo_bf"]
    q_sc, x_in, x2_sc = ns["q_sc"], ns["x_in"], ns["x2_sc"]
    rstd2, nmr2 = ns["rstd2"], ns["nmr2"]
    e_sum, ones_row = ns["e_sum"], ns["ones_row"]
    bank, bankres, res, sc = ns["bank"], ns["bankres"], ns["res"], ns["sc"]

    P.wait("sync", "p_q", P.val("p_q"))
    P.wait("sync", "m_pe", P.val("m_pe"))
    P.wait("sync", "m_dve", P.val("m_dve"))
    P.wait("sync", "m_act", P.val("m_act"))
    stat4, eps_c = ns["stat4"], ns["eps_c"]
    st_prev = {"g4": 0, "exp": 0, "nmr": 0}
    pending_tail = None

    def stats_tail(pend):
        s_, it_, smv_ = pend
        P.wait("vector", "m_act", smv_)
        P.wait("vector", "m_gp", st_prev["g4"])
        nm2 = P.do("vector", lambda e: e.tensor_tensor(stat4[:, 8:12], stat4[:, 4:8], stat4[:, 4:8], ALU.mult), inc=("m_dve", 1))
        P.wait("gpsimd", "m_dve", nm2)
        P.do("gpsimd", lambda e: e.tensor_scalar_mul(stat4[:, 8:12], stat4[:, 8:12], 1.0 / (1024.0 * 1024.0)))
        P.do("gpsimd", lambda e: e.tensor_scalar_mul(stat4[:, 12:16], stat4[:, 0:4], 1.0 / 1024.0))
        g4v = P.do("gpsimd", lambda e: e.tensor_sub(stat4[:, 12:16], stat4[:, 12:16], stat4[:, 8:12]), inc=("m_gp", 1))
        P.wait("scalar", "m_gp", g4v)
        lnv2 = P.do("scalar", lambda e: e.activation(stat4[:, 16:20], stat4[:, 12:16], AF.Ln, bias=eps_c[:, :]), inc=("m_act", 1))
        P.wait("vector", "m_act", lnv2)
        mhv = P.do("vector", lambda e: e.tensor_scalar(stat4[:, 20:24], stat4[:, 16:20], -0.5, None, op0=ALU.mult), inc=("m_dve", 1))
        P.wait("scalar", "m_dve", mhv)
        expv = P.do("scalar", lambda e, s=s_, it=it_: e.activation(
            rstd2[:, s, it * NG : (it + 1) * NG], stat4[:, 20:24], AF.Exp), inc=("m_act", 1))
        P.wait("vector", "m_act", expv)
        nmrv = P.do("vector", lambda e, s=s_, it=it_: e.scalar_tensor_tensor(
            nmr2[:, s, it * NG : (it + 1) * NG], stat4[:, 4:8], -1.0 / 1024.0,
            rstd2[:, s, it * NG : (it + 1) * NG], op0=ALU.mult, op1=ALU.mult), inc=("m_dve", 1))
        return {"g4": g4v, "exp": expv, "nmr": nmrv}
    q_res = [res("q3_0"), res("q3_1")]
    x3_res = res("x3")
    x2_res = [res(f"x2s_{j}") for j in range(4)]
    chain_res = [res(f"p3a_ch{j}") for j in range(4)]

    NTILES = SB * NT

    def load_q(i):
        s, it = divmod(i, NT)
        buf = i % 2
        q_res[buf].acquire("scalar")
        return P.do("scalar", lambda e, s=s, it=it, b=buf: e.dma_start(
            out=qT_t3[b][:, :, :], in_=q_sc[s, it].rearrange("c p t -> p c t")), inc=(f"s_q{buf}", 16))

    def s1(i, qv, prev_e1ns=None):
        s, it = divmod(i, NT)
        buf = i % 2
        P.wait("tensor", f"s_q{buf}", qv)
        exps = []
        for c in range(NPAIR):
            bsc = c % 2
            bankres[bsc].acquire("tensor")
            mv = P.do("tensor", lambda e, s=s, c=c, b=buf, bk=bsc: e.matmul(
                bank(bk), klT_bd[:, s, c, :], qT_t3[b][:, c, :], start=True, stop=True), inc=("m_pe", 1))
            P.wait("scalar", "m_pe", mv)
            if prev_e1ns is not None:
                P.wait("scalar", "m_dve", prev_e1ns[c])
            ex = P.do("scalar", lambda e, c=c, bk=bsc: e.activation(E1all[:, c, :], bank(bk), AF.Exp), inc=("m_act", 1))
            bankres[bsc].used_until("m_act", ex)
            exps.append(ex)
        q_res[buf].used_until("m_pe", P.val("m_pe"))
        return exps

    def stage2(i, exps):
        s, it = divmod(i, NT)
        e1ns = []
        for c in range(NPAIR):
            bs = c % 2
            cb_ = c % 4
            bankres[bs].acquire("tensor")
            P.wait("tensor", "m_act", exps[c])
            sv = P.do("tensor", lambda e, c=c, bk=bs: e.matmul(bank(bk), e_sum[:, :], E1all[:, c, :], start=True, stop=True), inc=("m_pe", 1))
            chain_res[cb_].acquire("scalar")
            P.wait("scalar", "m_pe", sv)
            lv = P.do("scalar", lambda e, bk=bs, b=cb_: e.activation(nlv[b][:, :], bank(bk), AF.Ln), inc=("m_act", 1))
            bankres[bs].used_until("m_act", lv)
            ie = P.do("scalar", lambda e, b=cb_: e.activation(invb[b][:, :], nlv[b][:, :], AF.Exp, scale=-1.0), inc=("m_act", 1))
            P.wait("vector", "m_act", ie)
            nv = P.do("vector", lambda e, c=c, b=cb_: e.tensor_tensor(E1n2[b][:, :], E1all[:, c, :], invb[b][:, :], ALU.mult), inc=("m_dve", 1))
            chain_res[cb_].used_until("m_dve", nv)
            e1ns.append(nv)
        return e1ns

    def stage3(i, e1ns):
        s, it = divmod(i, NT)
        for c in range(NPAIR):
            bat = 4 + c % 2
            bankres[bat].acquire("tensor")
            P.wait("tensor", "m_dve", e1ns[c])
            av = P.do("tensor", lambda e, s=s, c=c, b=c % 4, bk=bat: e.matmul(bank(bk), G2_bd[:, s, c, :], E1n2[b][:, :], start=True, stop=True), inc=("m_pe", 1))
            chain_res[c % 4].used_until("m_pe", av)
            P.wait("vector", "m_pe", av)
            ae = P.do("vector", lambda e, c=c, bk=bat: e.tensor_copy(attnoT[:, c, :], bank(bk)), inc=("m_dve", 1))
            bankres[bat].used_until("m_dve", ae)
        return P.val("m_dve")

    qvs = {0: load_q(0), 1: load_q(1)}
    exps = s1(0, qvs[0])
    e1ns = stage2(0, exps)

    for i in range(NTILES):
        s, it = divmod(i, NT)
        t0 = it * 512
        if i + 2 < NTILES:
            qvs[i + 2] = load_q(i + 2)
        x3_res.acquire("sync")
        xv = P.do("sync", lambda e, s=s, t0=t0: e.dma_start(
            out=x_t3[:, :, :], in_=x_in[s, t0 : t0 + 512, :].rearrange("(g p) d -> p g d", p=128)), inc=("s_x3", 16))

        attn_done = stage3(i, e1ns)

        if i + 1 < NTILES:
            nexps = s1(i + 1, qvs[i + 1], e1ns)
            e1ns = stage2(i + 1, nexps)
        if pending_tail is not None:
            st_prev = stats_tail(pending_tail)
            pending_tail = None

        P.wait("tensor", "m_dve", attn_done)
        P.wait("tensor", "s_wo", 16)
        P.wait("tensor", "s_bo", 16)
        P.wait("tensor", "s_x3", xv)
        P.wait("vector", "s_x3", xv)
        if not DBG["p3a_wo"]:
            continue
        for gi in range(NG):
            xbuf = gi
            x2_res[xbuf].acquire("vector")
            for hf in range(2):
                bk = 2 + hf
                bankres[bk].acquire("tensor")
                for kc in range(NCH):
                    P.do("tensor", lambda e, gi=gi, hf=hf, kc=kc, bk=bk: e.matmul(
                        bank(bk), attnoT[:, kc, gi * 128 : (gi + 1) * 128],
                        Wo_sb[:, kc, hf * 512 : (hf + 1) * 512],
                        start=(kc == 0), stop=(kc == NCH - 1)))
                wv = P.do("tensor", lambda e, hf=hf, bk=bk: e.matmul(
                    bank(bk), ones_row[:, :], bo_bf[:, hf * 512 : (hf + 1) * 512],
                    start=False, stop=True), inc=("m_pe", 1))
                P.wait("vector", "m_pe", wv)
                evv = P.do("vector", lambda e, gi=gi, hf=hf, bk=bk, xb=xbuf: e.scalar_tensor_tensor(
                    x2_sb[xb][:, hf * 512 : (hf + 1) * 512], bank(bk), 1.0,
                    x_t3[:, gi, hf * 512 : (hf + 1) * 512], op0=ALU.mult, op1=ALU.add), inc=("m_dve", 1))
                bankres[bk].used_until("m_dve", evv)
            x2v = P.val("m_dve")
            P.wait("gpsimd", "m_dve", x2v)
            dst = x2_sc[s, t0 + gi * 128 : t0 + (gi + 1) * 128, :]
            sp = P.do("gpsimd", lambda e, d=dst, xb=xbuf: e.dma_start(out=d, in_=x2_sb[xb][:, :]), inc=("p_x2", 16))
            x2_res[xbuf].used_until("p_x2", sp)
            if gi == 0:
                P.wait("scalar", "m_gp", st_prev["g4"])
                P.wait("scalar", "m_dve", st_prev["nmr"])
            P.wait("scalar", "m_dve", x2v)
            P.do("scalar", lambda e, g=gi, xb=xbuf: e.activation(
                sq_scr[:, :], x2_sb[xb][:, :], AF.Square, accum_out=stat4[:, g : g + 1]), inc=("m_act", 1))
            smv = P.do("scalar", lambda e, g=gi, xb=xbuf: e.activation(
                sq_scr[:, :], x2_sb[xb][:, :], AF.Identity, accum_out=stat4[:, 4 + g : 5 + g]), inc=("m_act", 1))
            x2_res[xbuf].used_until("m_act", smv)
        pending_tail = (s, it, smv)
        x3_res.used_until("m_dve", P.val("m_dve"))
    stats_tail(pending_tail)


def p3b(P, ns):
    x2_t, x2n_bf, x2nT, h_sb, out_sb, silu_g = ns["x2_t"], ns["x2n_bf"], ns["x2nT"], ns["h_sb"], ns["out_sb"], ns["silu_g"]
    wg_st, wu_st, wdT_sb = ns["wg_st"], ns["wu_st"], ns["wdT_sb"]
    x2_sc, wgt_sc, wut_sc, y_out = ns["x2_sc"], ns["wgt_sc"], ns["wut_sc"], ns["y_out"]
    rstd2, nmr2, ident = ns["rstd2"], ns["nmr2"], ns["ident"]
    bank, bankres, psBres, psB, res = ns["bank"], ns["bankres"], ns["psBres"], ns["psB"], ns["res"]

    P.wait("sync", "p_x2", P.val("p_x2"))
    P.wait("sync", "p_wgs", P.val("p_wgs"))
    P.wait("sync", "m_pe", P.val("m_pe"))
    P.wait("sync", "m_dve", P.val("m_dve"))
    P.wait("sync", "m_act", P.val("m_act"))

    x2_res = [res("x2t_0"), res("x2t_1")]
    wg_res = [res("wgst_0"), res("wgst_1")]
    wu_res = [res("wust_0"), res("wust_1")]
    o_res = [res("osb_0"), res("osb_1")]
    silu_res = res("silu")
    xnT_res = res("x2nT")
    rstd2, nmr2 = ns["rstd2"], ns["nmr2"]
    last_tr = P.val("m_pe")
    NTILES3 = SB * NT

    def load_x2(idx):
        s, it = divmod(idx, NT)
        t0 = it * 512
        b = idx % 2
        x2_res[b].acquire("scalar")
        return P.do("scalar", lambda e, s=s, t0=t0, b=b: e.dma_start(
            out=x2_t[b][:, :, :], in_=x2_sc[s, t0 : t0 + 512, :].rearrange("(g p) d -> p g d", p=128)), inc=(f"s_x2{b}", 16))

    xvs = {0: load_x2(0)}
    xns = {}
    last_tr_b = [0, 0, 0, 0]

    def norms(idx2):
        s2_, it2 = divmod(idx2, NT)
        b2 = idx2 % 2
        P.wait("scalar", f"s_x2{b2}", xvs[idx2])
        vs = []
        for gi in range(NG):
            gidx = it2 * NG + gi
            P.wait("scalar", "m_pe", last_tr_b[gi])
            xn = P.do("scalar", lambda e, b=b2, gi=gi, s=s2_, gidx=gidx: e.activation(
                x2n_bf[gi][:, :], x2_t[b][:, gi, :], AF.Identity,
                bias=nmr2[:, s, gidx : gidx + 1], scale=rstd2[:, s, gidx : gidx + 1]), inc=("m_act", 1))
            vs.append(xn)
        xns[idx2] = vs

    def tr_block(idx2):
        out = None
        for gi in range(NG):
            P.wait("tensor", "m_act", xns[idx2][gi])
            for c in range(NCH):
                half = c // 4
                if c % 4 == 0:
                    psBres[half].acquire("tensor")
                pslot = psB[:, half * 1024 + (c % 4) * 128 : half * 1024 + (c % 4) * 128 + 128]
                tv = P.do("tensor", lambda e, gi=gi, c=c, p=pslot: e.transpose(p, x2n_bf[gi][:, c * 128 : (c + 1) * 128], ident[:, :]),
                          inc=("m_pe", 1) if c % 4 == 3 else None)
                if c % 4 == 3:
                    last_tr_b[gi] = tv
                    P.wait("vector", "m_pe", tv)
                    pv = psB[:, half * 1024 : half * 1024 + 512].rearrange("p (c t) -> p c t", c=4)
                    ov = x2nT[:, c - 3 : c + 1, gi * 128 : (gi + 1) * 128]
                    evv = P.do("vector", lambda e, pv=pv, ov=ov: e.tensor_copy(ov, pv), inc=("m_dve", 1))
                    psBres[half].used_until("m_dve", evv)
                    out = evv
        return out

    for s in range(SB):
        for it in range(NT):
            t0 = it * 512
            idx = s * NT + it
            buf = idx % 2
            if idx + 1 < NTILES3:
                xvs[idx + 1] = load_x2(idx + 1)

            if idx == 0:
                norms(0)
            xt_done = tr_block(idx)
            P.wait("tensor", "m_dve", xt_done)

            for hc in range(NHC):
                wb = hc % 2
                wg_res[wb].acquire("sync")
                wgv = P.do("sync", lambda e, hc=hc, b=wb: e.dma_start(out=wg_st[b][:, :, :], in_=wgt_sc[hc]), inc=(f"s_wg{wb}", 16))
                wu_res[wb].acquire("sync")
                wuv = P.do("sync", lambda e, hc=hc, b=wb: e.dma_start(out=wu_st[b][:, :, :], in_=wut_sc[hc]), inc=(f"s_wu{wb}", 16))
                P.wait("tensor", f"s_wg{wb}", wgv)
                bg, bu = hc % 2, 2 + hc % 2
                bankres[bg].acquire("tensor")
                gv = None
                for kc in range(NCH):
                    gv = P.do("tensor", lambda e, kc=kc, b=wb, bk=bg: e.matmul(
                        bank(bk), wg_st[b][:, kc, :], x2nT[:, kc, :], start=(kc == 0), stop=(kc == NCH - 1)),
                        inc=("m_pe", 1) if kc == NCH - 1 else None)
                wg_res[wb].used_until("m_pe", gv)
                P.wait("tensor", f"s_wu{wb}", wuv)
                bankres[bu].acquire("tensor")
                uv = None
                for kc in range(NCH):
                    uv = P.do("tensor", lambda e, kc=kc, b=wb, bk=bu: e.matmul(
                        bank(bk), wu_st[b][:, kc, :], x2nT[:, kc, :], start=(kc == 0), stop=(kc == NCH - 1)),
                        inc=("m_pe", 1) if kc == NCH - 1 else None)
                wu_res[wb].used_until("m_pe", uv)
                silu_res.acquire("scalar")
                P.wait("scalar", "m_pe", gv)
                sg = P.do("scalar", lambda e, bk=bg: e.activation(silu_g[:, :], bank(bk), AF.Silu, scale=S127), inc=("m_act", 1))
                bankres[bg].used_until("m_act", sg)
                P.wait("vector", "m_pe", uv)
                P.wait("vector", "m_act", sg)
                hv = P.do("vector", lambda e, hc=hc, bk=bu: e.scalar_tensor_tensor(
                    h_sb[:, hc, :], bank(bk), 1.0, silu_g[:, :], op0=ALU.mult, op1=ALU.mult), inc=("m_dve", 1))
                bankres[bu].used_until("m_dve", hv)
                silu_res.used_until("m_dve", hv)
            h_done = P.val("m_dve")

            P.wait("tensor", "m_dve", h_done)
            for gi in range(NG):
                ob = gi % 2
                o_res[ob].acquire("vector")
                for hf in range(2):
                    bk = 4 + (gi * 2 + hf) % 2
                    bankres[bk].acquire("tensor")
                    dvv = None
                    for hc in range(NHC):
                        dvv = P.do("tensor", lambda e, gi=gi, hf=hf, hc=hc, bk=bk: e.matmul(
                            bank(bk), h_sb[:, hc, gi * 128 : (gi + 1) * 128],
                            wdT_sb[:, hc, hf * 512 : (hf + 1) * 512],
                            start=(hc == 0), stop=(hc == NHC - 1)),
                            inc=("m_pe", 1) if hc == NHC - 1 else None)
                    P.wait("vector", "m_pe", dvv)
                    evv = P.do("vector", lambda e, gi=gi, hf=hf, bk=bk, b=buf, ob=ob: e.scalar_tensor_tensor(
                        out_sb[ob][:, hf * 512 : (hf + 1) * 512], bank(bk), S127SQ,
                        x2_t[b][:, gi, hf * 512 : (hf + 1) * 512], op0=ALU.mult, op1=ALU.add), inc=("m_dve", 1))
                    bankres[bk].used_until("m_dve", evv)
                ov = P.val("m_dve")
                P.wait("gpsimd", "m_dve", ov)
                dst = y_out[s, t0 + gi * 128 : t0 + (gi + 1) * 128, :]
                sp = P.do("gpsimd", lambda e, d=dst, ob=ob: e.dma_start(out=d, in_=out_sb[ob][:, :]), inc=("p_out", 16))
                o_res[ob].used_until("p_out", sp)
            if idx + 1 < NTILES3:
                norms(idx + 1)
            x2_res[buf].used_until("m_dve", P.val("m_dve"))


def _host_consts():
    ident = np.eye(128, dtype=ml_dtypes.bfloat16)
    e_sum = np.zeros((128, 128), dtype=ml_dtypes.bfloat16)
    e_sum[0:64, 0:64] = 1
    e_sum[64:128, 64:128] = 1
    rows = np.zeros((1, 384), dtype=ml_dtypes.bfloat16)
    rows[0, 0:64] = 1
    rows[0, 192:256] = 1
    rows[0, 256:384] = 1
    brows = np.zeros((128, 256), dtype=ml_dtypes.bfloat16)
    brows[0, 0:64] = 1
    brows[64, 192:256] = 1
    pic = np.zeros((64, 1536), dtype=np.float32)
    for j, v in enumerate((7.0, 15.0, 13.0)):
        blk = v * np.eye(64, dtype=np.float32)
        for r in range(8):
            pic[:, j * 512 + r * 64 : j * 512 + (r + 1) * 64] = blk
    return ident, e_sum, rows, pic, brows


def _in_maps(x, Wq, Wk, Wv, Wo, bo, wg, wu, wd):
    ident, e_sum, rows, pic, brows = _host_consts()
    maps = []
    for cidx in range(NCORES):
        maps.append({
            "x": np.ascontiguousarray(x[cidx * SB : (cidx + 1) * SB]),
            "Wq": Wq, "Wk": Wk, "Wv": Wv, "Wo": Wo, "bo": bo,
            "wg": wg, "wu": wu, "wd": wd,
            "ident": ident, "esum": e_sum, "rows": rows, "pic": pic, "brows": brows,
        })
    return maps


def kernel(L0_attn_x, L0_attn_Wk, L0_attn_Wq, L0_attn_Wv, L0_attn_Wo, L0_attn_bo,
           L0_ffn_w_gate, L0_ffn_w_up, L0_ffn_w_down):
    x = np.ascontiguousarray(np.asarray(L0_attn_x, dtype=np.float32))
    maps = _in_maps(
        x,
        np.ascontiguousarray(np.asarray(L0_attn_Wq, dtype=np.float32)),
        np.ascontiguousarray(np.asarray(L0_attn_Wk, dtype=np.float32)),
        np.ascontiguousarray(np.asarray(L0_attn_Wv, dtype=np.float32)),
        np.ascontiguousarray(np.asarray(L0_attn_Wo, dtype=np.float32)),
        np.ascontiguousarray(np.asarray(L0_attn_bo, dtype=np.float32)),
        np.ascontiguousarray(np.asarray(L0_ffn_w_gate, dtype=np.int32)),
        np.ascontiguousarray(np.asarray(L0_ffn_w_up, dtype=np.int32)),
        np.ascontiguousarray(np.asarray(L0_ffn_w_down, dtype=np.int32)),
    )
    nc = build_program()
    res = run_bass_kernel_spmd(nc, maps, core_ids=list(range(NCORES)))
    return np.concatenate([r["y"] for r in res.results], axis=0)


# version 36
# speedup vs baseline: 1.0550x; 1.0113x over previous
import math
import contextlib
import numpy as np
import ml_dtypes

import concourse.bass as bass
import concourse.mybir as mybir
from concourse.bass_utils import run_bass_kernel_spmd

F32 = mybir.dt.float32
BF16 = mybir.dt.bfloat16
I32 = mybir.dt.int32
U8 = mybir.dt.uint8
AF = mybir.ActivationFunctionType
ALU = mybir.AluOpType
AX = mybir.AxisListType

B, L, A, H, HS, M, HID = 16, 4096, 1024, 16, 64, 64, 3072
NCORES = 8
SB = B // NCORES
NT = L // 512
NG = 4
NCH = A // 128
NHC = HID // 128
NPAIR = H // 2
S127 = float(np.float32(1.0 / 127.0))
S127SQ = float(np.float32(np.float32(1.0 / 127.0) * np.float32(1.0 / 127.0)))

PHASES = ("P1", "P2", "P3A", "P3B")
DBG = {"p3a_tiles": None, "p3a_pairs": None, "p3a_wo": True, "p3a_norm": True, "p2_nointerleave": False, "p2_nopinv": False}


class Prog:

    ENGS = ("sync", "vector", "scalar", "tensor", "gpsimd")

    def __init__(self, nc, es):
        self.nc = nc
        self.es = es
        self.ops = {e: [] for e in self.ENGS}
        self.last_wait = {}
        self.sems = {}
        self.cnt = {}

    def sem(self, name):
        if name not in self.sems:
            self.sems[name] = self.es.enter_context(self.nc.semaphore(name))
            self.cnt[name] = 0
        return self.sems[name]

    def val(self, name):
        self.sem(name)
        return self.cnt[name]

    def do(self, eng, fn, inc=None):
        if inc is None:
            self.ops[eng].append(fn)
            return None
        name, n = inc
        sem = self.sem(name)
        self.cnt[name] += n
        self.ops[eng].append(lambda e: fn(e).then_inc(sem, n))
        return self.cnt[name]

    def wait(self, eng, name, v):
        if v is None or v <= 0:
            return
        assert v <= self.cnt.get(name, 0), f"wait on future value {name}={v}"
        key = (eng, name)
        if self.last_wait.get(key, 0) >= v:
            return
        self.last_wait[key] = v
        sem = self.sem(name)
        self.ops[eng].append(lambda e: e.wait_ge(sem, v))

    def emit(self, block):
        for ename in self.ENGS:
            fns = self.ops[ename]
            if not fns:
                continue

            def runner(eng, fns=fns):
                for f in fns:
                    f(eng)

            getattr(block, ename)(runner)


class Res:

    def __init__(self, prog):
        self.p = prog
        self.rel = []

    def acquire(self, eng):
        for name, v in self.rel:
            self.p.wait(eng, name, v)
        self.rel = []

    def used_until(self, name, v):
        if v:
            self.rel.append((name, v))


def _carver(arena):
    def carve(off, shape, dtype):
        per = math.prod(shape[1:]) * mybir.dt.size(dtype)
        v = arena[0 : shape[0], off : off + per].bitcast(dtype)
        if len(shape) > 2:
            names = [f"d{i}" for i in range(1, len(shape))]
            v = v.rearrange(
                "p (" + " ".join(names) + ") -> p " + " ".join(names),
                **{n: shape[i + 1] for i, n in enumerate(names)},
            )
        return v

    return carve


def build_program(phases=PHASES, debug_spills=False):
    nc = bass.Bass("TRN2", target_bir_lowering=False)
    ns = {}

    x_in = nc.dram_tensor("x", [SB, L, A], F32, kind="ExternalInput").ap()
    Wq_d = nc.dram_tensor("Wq", [A, A], F32, kind="ExternalInput").ap()
    Wk_d = nc.dram_tensor("Wk", [A, A], F32, kind="ExternalInput").ap()
    Wv_d = nc.dram_tensor("Wv", [A, A], F32, kind="ExternalInput").ap()
    Wo_d = nc.dram_tensor("Wo", [A, A], F32, kind="ExternalInput").ap()
    bo_d = nc.dram_tensor("bo", [A], F32, kind="ExternalInput").ap()
    wg_d = nc.dram_tensor("wg", [HID, A], I32, kind="ExternalInput").ap()
    wu_d = nc.dram_tensor("wu", [HID, A], I32, kind="ExternalInput").ap()
    wd_d = nc.dram_tensor("wd", [A, HID], I32, kind="ExternalInput").ap()
    ident_d = nc.dram_tensor("ident", [128, 128], BF16, kind="ExternalInput").ap()
    esum_d = nc.dram_tensor("esum", [128, 128], BF16, kind="ExternalInput").ap()
    rows_d = nc.dram_tensor("rows", [1, 384], BF16, kind="ExternalInput").ap()
    brows_d = nc.dram_tensor("brows", [128, 256], BF16, kind="ExternalInput").ap()
    pic_d = nc.dram_tensor("pic", [64, 1536], F32, kind="ExternalInput").ap()

    spk = "ExternalOutput" if debug_spills else "Internal"
    q_sc = nc.dram_tensor("q_sc", [SB, NT, NCH, 128, 512], BF16, kind=spk).ap()
    k_sc = nc.dram_tensor("k_sc", [SB, NCH, 128, L], BF16, kind=spk).ap()
    v_sc = nc.dram_tensor("v_sc", [SB, L, A], BF16, kind=spk).ap()
    x2_sc = nc.dram_tensor("x2_sc", [SB, L, A], F32, kind=spk).ap()
    wgt_sc = nc.dram_tensor("wgt_sc", [NHC, 128, NCH, 128], BF16, kind=spk).ap()
    wut_sc = nc.dram_tensor("wut_sc", [NHC, 128, NCH, 128], BF16, kind=spk).ap()
    y_out = nc.dram_tensor("y", [SB, L, A], F32, kind="ExternalOutput").ap()
    dbg = {}
    if debug_spills:
        for nm, shape, dt in (
            ("lmq", [128, SB, NPAIR, 64], BF16), ("lmk", [128, SB, NPAIR, 64], BF16),
            ("g2", [128, SB, NPAIR, 128], BF16), ("rstd", [128, SB, 32], F32),
            ("nmr", [128, SB, 32], F32), ("xnl", [128, SB, NCH, 64], F32),
            ("zt", [128, SB, NPAIR, 64], BF16), ("s3v", [128, SB, NPAIR, 64], BF16),
        ):
            dbg[nm] = nc.dram_tensor("dbg_" + nm, shape, dt, kind="ExternalOutput").ap()

    es = contextlib.ExitStack()
    with es:
        P = Prog(nc, es)

        arena_w = es.enter_context(nc.sbuf_tensor("arena_w", [128, 49152], U8))
        cw = _carver(arena_w)
        Wq_sb = cw(0, [128, NCH, 1024], BF16)
        Wk_sb = cw(16384, [128, NCH, 1024], BF16)
        Wv_sb = cw(32768, [128, NCH, 1024], BF16)
        wdT_sb = cw(0, [128, NHC, 1024], BF16)
        Wo_sb = es.enter_context(nc.sbuf_tensor("Wo_sb", [128, NCH, 1024], BF16))

        arena_b = es.enter_context(nc.sbuf_tensor("arena_b", [128, 98304], U8))
        cb = _carver(arena_b)
        x_t = [cb(0 + i * 16384, [128, NG, 1024], F32) for i in range(2)]
        xn_bf4 = cb(32768, [128, NG, 1024], BF16)
        xnT_sb = cb(40960, [128, NCH, 512], BF16)
        qT_st = [cb(49152, [128, NCH, 512], BF16)]
        kT_st = [cb(57344, [128, NCH, 512], BF16)]
        v_st = [cb(65536, [128, NG, 1024], BF16)]
        NI = SB * NPAIR
        qlbd_all = cw(0, [128, NI, 128], BF16)
        s2e_all = cw(4096, [128, NI, 64], F32)
        s2bf_all = cw(8192, [128, NI, 64], BF16)
        s2T_all = cw(10240, [64, NI, 128], BF16)
        Z_all = cw(14336, [64, NI, 128], BF16)
        ZT_all = cw(18432, [64, NI, 128], BF16)
        XZb_all = cw(22528, [64, NI, 128], BF16)
        XZT_all = cw(26624, [64, NI, 128], BF16)
        T_all = cw(30720, [64, NI, 128], BF16)
        zscb = cw(34816, [64, 32], F32)
        csbf_all = cw(35072, [128, 16], BF16)
        ET_all = cb(0, [128, 32, 1024], BF16)
        kT_pair = [cb(73728 + i * 8192, [128, L], BF16) for i in range(2)]
        v4 = [cb(65536, [128, NG, 1024], BF16), cb(90112, [128, NG, 1024], BF16)]
        qT_t3 = [cb(0 + i * 8192, [128, NCH, 512], BF16) for i in range(2)]
        x_t3 = cb(16384, [128, NG, 1024], F32)
        attnoT = cb(32768, [128, NCH, 512], BF16)
        E1all = cb(40960, [128, NPAIR, 512], BF16)
        nlv = [cb(49152 + i * 2048, [128, 512], F32) for i in range(4)]
        invb = [cb(57344 + i * 1024, [128, 512], BF16) for i in range(4)]
        E1n2 = [cb(61440 + i * 1024, [128, 512], BF16) for i in range(4)]
        x2_sb = [cb(65536 + i * 4096, [128, 1024], F32) for i in range(4)]
        x2_t = [cb(0 + i * 16384, [128, NG, 1024], F32) for i in range(2)]
        x2n_bf = [cb(81920 + i * 2048, [128, 1024], BF16) for i in range(4)]
        x2nT = cb(34816, [128, NCH, 512], BF16)
        h_sb = cb(43008, [128, NHC, 512], BF16)
        out_sb = [cb(67584 + i * 4096, [128, 1024], F32) for i in range(2)]
        silu_g = cb(75776, [128, 512], BF16)

        wg_st = [es.enter_context(nc.sbuf_tensor(f"wg_st{i}", [128, NCH, 128], BF16)) for i in range(2)]
        wu_st = [es.enter_context(nc.sbuf_tensor(f"wu_st{i}", [128, NCH, 128], BF16)) for i in range(2)]

        ident = es.enter_context(nc.sbuf_tensor("ident_s", [128, 128], BF16))
        e_sum = es.enter_context(nc.sbuf_tensor("e_sum_s", [128, 128], BF16))
        rows_s = es.enter_context(nc.sbuf_tensor("rows_s", [1, 384], BF16))
        brows = es.enter_context(nc.sbuf_tensor("brows_s", [128, 256], BF16))
        pic_s = es.enter_context(nc.sbuf_tensor("pic_s", [64, 1536], F32))
        bo_bf = es.enter_context(nc.sbuf_tensor("bo_bf", [1, 1024], BF16))
        sq_scr = es.enter_context(nc.sbuf_tensor("sq_scr", [128, 1024], F32))
        stat4 = es.enter_context(nc.sbuf_tensor("stat4", [128, 24], F32))
        xnl = es.enter_context(nc.sbuf_tensor("xnl", [128, SB, NCH, 64], F32))
        xnl_bf = es.enter_context(nc.sbuf_tensor("xnl_bf", [128, SB, NCH, 64], BF16))
        qlT = es.enter_context(nc.sbuf_tensor("qlT", [128, SB, NPAIR, 64], BF16))
        klT = es.enter_context(nc.sbuf_tensor("klT", [128, SB, NPAIR, 64], BF16))
        klT_bd = es.enter_context(nc.sbuf_tensor("klT_bd", [128, SB, NPAIR, 128], BF16))
        qlT_bd = es.enter_context(nc.sbuf_tensor("qlT_bd", [128, 128], BF16))
        G2_bd = es.enter_context(nc.sbuf_tensor("G2_bd", [128, SB, NPAIR, 128], BF16))
        ZT_st = es.enter_context(nc.sbuf_tensor("ZT_st", [128, SB, NPAIR, 64], BF16))
        rstd2 = es.enter_context(nc.sbuf_tensor("rstd2", [128, SB, 32], F32))
        nmr2 = es.enter_context(nc.sbuf_tensor("nmr2", [128, SB, 32], F32))
        scal = es.enter_context(nc.sbuf_tensor("scal", [128, 64], F32))
        eps_c = es.enter_context(nc.sbuf_tensor("eps_c", [128, 1], F32))
        s2e = es.enter_context(nc.sbuf_tensor("s2e", [128, 64], F32))
        s2bf = es.enter_context(nc.sbuf_tensor("s2bf", [128, 64], BF16))
        cs_bf = es.enter_context(nc.sbuf_tensor("cs_bf", [128, 1], BF16))
        zsc_bf = es.enter_context(nc.sbuf_tensor("zsc_bf", [1, 2], BF16))
        s2T = es.enter_context(nc.sbuf_tensor("s2T", [64, 128], BF16))
        Z_f = es.enter_context(nc.sbuf_tensor("Z_f", [64, 128], BF16))
        XZ_bf = es.enter_context(nc.sbuf_tensor("XZ_bf", [64, 128], BF16))
        XZT_f = es.enter_context(nc.sbuf_tensor("XZT_f", [64, 128], BF16))
        ZTf = es.enter_context(nc.sbuf_tensor("ZTf", [64, 128], BF16))
        T_f = es.enter_context(nc.sbuf_tensor("T_f", [64, 128], BF16))
        tiny = es.enter_context(nc.sbuf_tensor("tiny", [64, 64], F32))
        tiny2 = es.enter_context(nc.sbuf_tensor("tiny2", [1, 64], F32))
        zsc_bf32 = es.enter_context(nc.sbuf_tensor("zsc_bf32", [1, 32], BF16))
        ones_col = es.enter_context(nc.sbuf_tensor("ones_col", [128, 1], BF16))
        s3inv = es.enter_context(nc.sbuf_tensor("s3inv", [128, 8], F32))
        s3v_sb = es.enter_context(nc.sbuf_tensor("s3v_sb", [128, SB, NPAIR, 64], BF16))

        psA = es.enter_context(nc.psum_tensor("psA", [128, 3072], F32))
        psB = es.enter_context(nc.psum_tensor("psB", [128, 2048], BF16))

        def bank(i, cols=512, off=0):
            return psA[:, i * 512 + off : i * 512 + off + cols]

        def sc(i):
            return scal[:, i : i + 1]

        bankres = [Res(P) for _ in range(7)]
        psBres = [Res(P), Res(P)]
        resources = {}

        def res(name):
            if name not in resources:
                resources[name] = Res(P)
            return resources[name]

        ns.update(locals())
        ns["ln_rel"] = {}

        P.do("vector", lambda e: e.memset(eps_c[:, :], 1e-5), inc=("m_dve", 1))
        ns["eps_gate"] = P.val("m_dve")
        P.do("gpsimd", lambda e: e.memset(ones_col[:, :], 1.0), inc=("m_gp", 1))
        ns["ones_gate"] = P.val("m_gp")
        for ap_d, sb in ((ident_d, ident), (esum_d, e_sum), (rows_d, rows_s), (pic_d, pic_s), (brows_d, brows)):
            P.do("sync", lambda e, a=ap_d, s=sb: e.dma_start(out=s[:, :], in_=a[:, :]), inc=("s_c", 16))
        ns["C_TOTAL"] = P.val("s_c")
        wsem = {}
        for nm, ap_d, sb in (("wq", Wq_d, Wq_sb), ("wk", Wk_d, Wk_sb), ("wv", Wv_d, Wv_sb), ("wo", Wo_d, Wo_sb)):
            srcw = ap_d.rearrange("(kc p) n -> p kc n", p=128)
            wsem[nm] = P.do("gpsimd", lambda e, a=srcw, s=sb: e.dma_start(out=s[:, :, :], in_=a), inc=(f"s_{nm}", 16))
        wsem["bo"] = P.do("gpsimd", lambda e: e.dma_start(out=bo_bf[:, :], in_=bo_d.rearrange("(a n) -> a n", a=1)), inc=("s_bo", 16))
        ns["wsem"] = wsem
        ns["W1_TOTAL"] = 0
        ns["ea_row"] = rows_s[:, 0:128]
        ns["eb_row"] = rows_s[:, 128:256]
        ns["ones_row"] = rows_s[:, 256:384]

        if "P1" in phases:
            p1(P, ns)
        if "P2" in phases:
            p2a(P, ns)
            p2b(P, ns)
        if "P3A" in phases:
            p3a(P, ns)
        if "P3B" in phases:
            p3b(P, ns)
        if debug_spills:
            P.wait("gpsimd", "m_dve", P.val("m_dve"))
            P.wait("gpsimd", "m_act", P.val("m_act"))
            pairs = [("lmq", qlT), ("lmk", klT), ("g2", G2_bd), ("rstd", rstd2), ("nmr", nmr2), ("xnl", xnl), ("zt", ZT_st), ("s3v", s3v_sb)]
            for nm, t in pairs:
                if nm in dbg:
                    ap = dbg[nm]
                    P.do("gpsimd", lambda e, a=ap, t=t: e.dma_start(out=a, in_=t[:]), inc=("p_dbg", 16))

        for name in ("p_q", "p_k", "p_v", "p_x2", "p_wgs", "p_out", "p_dbg"):
            P.wait("gpsimd", name, P.cnt.get(name, 0))

        block = es.enter_context(nc.Block())
        P.emit(block)
    return nc



def emit_ln_batch(P, ns, items, a5_gate=None):
    sc, sq_scr = ns["sc"], ns["sq_scr"]
    SUMSQ, NEGSUM, NEGMEAN, NM2, VAR, LNV, MH, RSTD, NMR = range(9)

    def slot(it_, k):
        return sc(it_[4] * 16 + k)

    for it_ in items:
        rel = ns["ln_rel"].get(it_[4])
        if rel:
            for eng in ("scalar", "vector", "gpsimd"):
                P.wait(eng, *rel)
    a1 = d1 = None
    sq_eng = ns.get("ln_sq_eng", "scalar")
    for it_ in items:
        if sq_eng == "scalar":
            a1 = P.do("scalar", lambda e, x=it_[0], o=slot(it_, SUMSQ): e.activation(sq_scr[:, :], x, AF.Square, accum_out=o),
                      inc=("m_act", 1) if it_ is items[-1] else None)
        else:
            a1 = P.do("vector", lambda e, x=it_[0], o=slot(it_, SUMSQ): e.scalar_tensor_tensor(sq_scr[:, :], x, 1.0, x, op0=ALU.mult, op1=ALU.mult, accum_out=o),
                      inc=("m_dve", 1) if it_ is items[-1] else None)
    a1sem = "m_act" if sq_eng == "scalar" else "m_dve"
    for it_ in items:
        d1 = P.do("vector", lambda e, x=it_[0], o=slot(it_, NEGSUM): e.reduce_sum(o, x, axis=AX.X, negate=True),
                  inc=("m_dve", 1) if it_ is items[-1] else None)
    P.wait("scalar", "m_dve", d1)
    a2 = None
    for it_ in items:
        a2 = P.do("scalar", lambda e, i=slot(it_, NEGSUM), o=slot(it_, NEGMEAN): e.activation(o, i, AF.Identity, scale=1.0 / 1024.0),
                  inc=("m_act", 1) if it_ is items[-1] else None)
    P.wait("vector", "m_act", a2)
    d2 = None
    for it_ in items:
        d2 = P.do("vector", lambda e, i=slot(it_, NEGMEAN), o=slot(it_, NM2): e.tensor_tensor(o, i, i, ALU.mult),
                  inc=("m_dve", 1) if it_ is items[-1] else None)
    P.wait("gpsimd", a1sem, a1)
    P.wait("gpsimd", "m_dve", d2)
    g1 = None
    for it_ in items:
        P.do("gpsimd", lambda e, i=slot(it_, SUMSQ), o=slot(it_, VAR): e.tensor_scalar_mul(o, i, 1.0 / 1024.0))
        g1 = P.do("gpsimd", lambda e, i=slot(it_, NM2), o=slot(it_, VAR): e.tensor_sub(o, o, i),
                  inc=("m_gp", 1) if it_ is items[-1] else None)
    P.wait("scalar", "m_gp", g1)
    a3 = None
    for it_ in items:
        a3 = P.do("scalar", lambda e, i=slot(it_, VAR), o=slot(it_, LNV): e.activation(o, i, AF.Ln, bias=ns["eps_c"][:, :]),
                  inc=("m_act", 1) if it_ is items[-1] else None)
    P.wait("vector", "m_act", a3)
    d3 = None
    for it_ in items:
        d3 = P.do("vector", lambda e, i=slot(it_, LNV), o=slot(it_, MH): e.tensor_scalar(o, i, -0.5, None, op0=ALU.mult),
                  inc=("m_dve", 1) if it_ is items[-1] else None)
    P.wait("scalar", "m_dve", d3)
    a4 = None
    for it_ in items:
        r_out = it_[2] if it_[2] is not None else slot(it_, RSTD)
        a4 = P.do("scalar", lambda e, i=slot(it_, MH), o=r_out: e.activation(o, i, AF.Exp),
                  inc=("m_act", 1) if it_ is items[-1] else None)
    P.wait("vector", "m_act", a4)
    d4 = None
    for it_ in items:
        r_out = it_[2] if it_[2] is not None else slot(it_, RSTD)
        n_out = it_[3] if it_[3] is not None else slot(it_, NMR)
        d4 = P.do("vector", lambda e, i=slot(it_, NEGMEAN), r=r_out, o=n_out: e.tensor_tensor(o, i, r, ALU.mult),
                  inc=("m_dve", 1) if it_ is items[-1] else None)
    outs = []
    if any(it_[1] is not None for it_ in items):
        P.wait("scalar", "m_dve", d4)
        if a5_gate:
            P.wait("scalar", *a5_gate)
        for it_ in items:
            r_out = it_[2] if it_[2] is not None else slot(it_, RSTD)
            n_out = it_[3] if it_[3] is not None else slot(it_, NMR)
            a5 = P.do("scalar", lambda e, x=it_[0], o=it_[1], r=r_out, n=n_out: e.activation(o, x, AF.Identity, bias=n, scale=r),
                      inc=("m_act", 1))
            outs.append(a5)
            ns["ln_rel"][it_[4]] = ("m_act", a5)
    else:
        for it_ in items:
            ns["ln_rel"][it_[4]] = ("m_dve", d4)
        outs = [d4] * len(items)
    return outs


def p1(P, ns):
    x_in, x_t, xn_bf4, xnT_sb = ns["x_in"], ns["x_t"], ns["xn_bf4"], ns["xnT_sb"]
    qT_st, kT_st, v_st = ns["qT_st"], ns["kT_st"], ns["v_st"]
    Wq_sb, Wk_sb, Wv_sb = ns["Wq_sb"], ns["Wk_sb"], ns["Wv_sb"]
    q_sc, k_sc, v_sc = ns["q_sc"], ns["k_sc"], ns["v_sc"]
    ident, xnl = ns["ident"], ns["xnl"]
    psB = ns["psB"]
    bank, bankres, psBres, res = ns["bank"], ns["bankres"], ns["psBres"], ns["res"]

    x_res = [res("p1_x0"), res("p1_x1")]
    stage_res = res("p1_st")
    xnT_res = res("p1_xnT")
    NTILES = SB * NT
    last_tr = 0

    def load_x(i):
        s, it = divmod(i, NT)
        buf = i % 2
        x_res[buf].acquire("sync")
        srcv = x_in[s, it * 512 : (it + 1) * 512, :].rearrange("(g p) d -> p g d", p=128)
        return P.do("sync", lambda e, a=srcv, b=buf: e.dma_start(out=x_t[b][:, :, :], in_=a), inc=(f"s_x{buf}", 16))

    def emit_lnb(i, xv, gate_tr):
        buf = i % 2
        P.wait("scalar", f"s_x{buf}", xv)
        P.wait("vector", f"s_x{buf}", xv)
        P.wait("scalar", "m_dve", ns["eps_gate"])
        items = [(x_t[buf][:, gi, :], xn_bf4[:, gi, :], None, None, gi) for gi in range(NG)]
        outs = emit_ln_batch(P, ns, items, a5_gate=("m_pe", gate_tr))
        x_res[buf].used_until("m_act", outs[-1])
        x_res[buf].used_until("m_dve", P.val("m_dve"))
        return outs

    xvs = {0: load_x(0), 1: load_x(1)}
    lnv = {0: emit_lnb(0, xvs[0], 0)}

    for i in range(NTILES):
        s, it = divmod(i, NT)
        t0 = it * 512
        if i + 2 < NTILES:
            xvs[i + 2] = load_x(i + 2)

        P.wait("tensor", "s_c", ns["C_TOTAL"])
        first_evac = True
        for gi in range(NG):
            P.wait("tensor", "m_act", lnv[i][gi])
            for c in range(NCH):
                half = c // 4
                if c % 4 == 0:
                    psBres[half].acquire("tensor")
                dst = psB[:, half * 1024 + (c % 4) * 128 : half * 1024 + (c % 4) * 128 + 128]
                tv = P.do("tensor", lambda e, d=dst, c=c, gi=gi: e.transpose(d, xn_bf4[:, gi, c * 128 : (c + 1) * 128], ident[:, :]),
                          inc=("m_pe", 1) if c % 4 == 3 else None)
                if c % 4 == 3:
                    last_tr = tv
                    if first_evac:
                        xnT_res.acquire("vector")
                        first_evac = False
                    P.wait("vector", "m_pe", tv)
                    pv = psB[:, half * 1024 : half * 1024 + 512].rearrange("p (c t) -> p c t", c=4)
                    ov = xnT_sb[:, c - 3 : c + 1, gi * 128 : (gi + 1) * 128]
                    evv = P.do("vector", lambda e, pv=pv, ov=ov: e.tensor_copy(ov, pv), inc=("m_dve", 1))
                    psBres[half].used_until("m_dve", evv)
        xnT_ready = P.val("m_dve")

        if i + 1 < NTILES:
            lnv[i + 1] = emit_lnb(i + 1, xvs[i + 1], last_tr)

        P.wait("vector", "m_dve", xnT_ready)
        for c in range(NCH):
            P.do("vector", lambda e, c=c, s=s, it=it: e.tensor_reduce(
                xnl[:, s, c, it * 8 : it * 8 + 8].rearrange("p (n o) -> p n o", o=1),
                xnT_sb[:, c, :].rearrange("p (n w) -> p n w", w=64),
                axis=AX.X, op=ALU.add), inc=("m_dve", 1) if c == NCH - 1 else None)

        P.wait("tensor", "s_wq", 16)
        P.wait("tensor", "s_wk", 16)
        P.wait("tensor", "s_wv", 16)
        P.wait("tensor", "m_dve", xnT_ready)
        stage = stage_res
        stage.acquire("tensor")
        for W_sb, dst_st, qscale in ((Wq_sb, qT_st[0], True), (Wk_sb, kT_st[0], False)):
            ev = None
            for c in range(NCH):
                bk = c % 4
                bankres[bk].acquire("tensor")
                mv = None
                for kc in range(NCH):
                    mv = P.do("tensor", lambda e, W=W_sb, c=c, kc=kc, bk=bk: e.matmul(
                        bank(bk), W[:, kc, c * 128 : (c + 1) * 128], xnT_sb[:, kc, :],
                        start=(kc == 0), stop=(kc == NCH - 1)),
                        inc=("m_pe", 1) if kc == NCH - 1 else None)
                if qscale:
                    P.wait("scalar", "m_pe", mv)
                    ev = P.do("scalar", lambda e, c=c, bk=bk, d=dst_st: e.activation(d[:, c, :], bank(bk), AF.Identity, scale=0.125), inc=("m_act", 1))
                    bankres[bk].used_until("m_act", ev)
                else:
                    P.wait("vector", "m_pe", mv)
                    ev = P.do("vector", lambda e, c=c, bk=bk, d=dst_st: e.tensor_copy(d[:, c, :], bank(bk)), inc=("m_dve", 1))
                    bankres[bk].used_until("m_dve", ev)
            if qscale:
                P.wait("scalar", "m_act", ev)
                dq = q_sc[s, it].rearrange("c p t -> p c t")
                sp = P.do("scalar", lambda e, d=dq: e.dma_start(out=d, in_=qT_st[0][:, :, :]), inc=("p_q", 16))
                stage.used_until("p_q", sp)
            else:
                P.wait("scalar", "m_dve", ev)
                dk = k_sc[s, :, :, t0 : t0 + 512].rearrange("c p t -> p c t")
                sp = P.do("scalar", lambda e, d=dk: e.dma_start(out=d, in_=kT_st[0][:, :, :]), inc=("p_k", 16))
                stage.used_until("p_k", sp)
        vev = None
        for gi in range(NG):
            for hf in range(2):
                bk = 4 + (gi * 2 + hf) % 2
                bankres[bk].acquire("tensor")
                mv = None
                for kc in range(NCH):
                    mv = P.do("tensor", lambda e, gi=gi, hf=hf, kc=kc, bk=bk: e.matmul(
                        bank(bk), xnT_sb[:, kc, gi * 128 : (gi + 1) * 128],
                        Wv_sb[:, kc, hf * 512 : (hf + 1) * 512],
                        start=(kc == 0), stop=(kc == NCH - 1)),
                        inc=("m_pe", 1) if kc == NCH - 1 else None)
                P.wait("scalar", "m_pe", mv)
                vev = P.do("scalar", lambda e, gi=gi, hf=hf, bk=bk, d=v_st[0]: e.activation(
                    d[:, gi, hf * 512 : (hf + 1) * 512], bank(bk), AF.Identity), inc=("m_act", 1))
                bankres[bk].used_until("m_act", vev)
        xnT_res.used_until("m_pe", P.val("m_pe"))
        P.wait("gpsimd", "m_act", vev)
        dvv = v_sc[s, t0 : t0 + 512, :].rearrange("(g p) d -> p g d", p=128)
        sp3 = P.do("gpsimd", lambda e, d=dvv: e.dma_start(out=d, in_=v_st[0][:, :, :]), inc=("p_v", 16))
        stage.used_until("p_v", sp3)
        if it == NT - 1:
            ns.setdefault("pk_s", {})[s] = P.val("p_k")


def p2a(P, ns):
    Wq_sb, Wk_sb = ns["Wq_sb"], ns["Wk_sb"]
    xnl, xnl_bf, qlT, klT, klT_bd = ns["xnl"], ns["xnl_bf"], ns["qlT"], ns["klT"], ns["klT_bd"]
    bank, bankres = ns["bank"], ns["bankres"]
    G2_bd = ns["G2_bd"]

    P.wait("scalar", "m_dve", P.val("m_dve"))
    cv = P.do("scalar", lambda e: e.copy(
        xnl_bf[:].rearrange("p a b c -> p (a b c)"), xnl[:].rearrange("p a b c -> p (a b c)")), inc=("m_act", 1))
    P.wait("tensor", "m_act", cv)

    for s in range(SB):
        for W_sb, dst, scale in ((Wq_sb, qlT, 0.125 / 64.0), (Wk_sb, klT, 1.0 / 64.0)):
            for c in range(NPAIR):
                bk = c % 4
                bankres[bk].acquire("tensor")
                mv = None
                for kc in range(NCH):
                    mv = P.do("tensor", lambda e, W=W_sb, s=s, c=c, kc=kc, bk=bk: e.matmul(
                        bank(bk, 64), W[:, kc, c * 128 : (c + 1) * 128], xnl_bf[:, s, kc, :],
                        start=(kc == 0), stop=(kc == NCH - 1)),
                        inc=("m_pe", 1) if kc == NCH - 1 else None)
                P.wait("scalar", "m_pe", mv)
                ev = P.do("scalar", lambda e, s=s, c=c, bk=bk, d=dst, sca=scale: e.activation(
                    d[:, s, c, :], bank(bk, 64), AF.Identity, scale=sca), inc=("m_act", 1))
                bankres[bk].used_until("m_act", ev)

    for s in range(SB):
        for c in range(NPAIR):
            P.do("gpsimd", lambda e, s=s, c=c: e.memset(klT_bd[:, s, c, :], 0.0))
            P.do("gpsimd", lambda e, s=s, c=c: e.memset(G2_bd[:, s, c, :], 0.0))
    gz = P.do("gpsimd", lambda e: e.engine_nop(), inc=("m_gp", 1))
    P.wait("vector", "m_gp", gz)
    P.wait("vector", "m_act", P.val("m_act"))
    for s in range(SB):
        for c in range(NPAIR):
            P.do("vector", lambda e, s=s, c=c: e.tensor_copy(klT_bd[0:64, s, c, 0:64], klT[0:64, s, c, :]))
            P.do("vector", lambda e, s=s, c=c: e.tensor_copy(klT_bd[64:128, s, c, 64:128], klT[64:128, s, c, :]))
    P.do("vector", lambda e: e.engine_nop(), inc=("m_dve", 1))

    import itertools
    ns["p2_gen"] = itertools.chain(pinv_gen(P, ns), factory_gen(P, ns))
    next(ns["p2_gen"])


def pinv_gen(P, ns):
    qlT, klT = ns["qlT"], ns["klT"]
    bank, bankres, psBres = ns["bank"], ns["bankres"], ns["psBres"]
    qlbd_all, s2e_all, s2bf_all, s2T_all = ns["qlbd_all"], ns["s2e_all"], ns["s2bf_all"], ns["s2T_all"]
    Z_all, ZT_all, XZb_all, XZT_all, T_all = ns["Z_all"], ns["ZT_all"], ns["XZb_all"], ns["XZT_all"], ns["T_all"]
    zscb, csbf_all = ns["zscb"], ns["csbf_all"]
    tiny, tiny2, zsc_bf32 = ns["tiny"], ns["tiny2"], ns["zsc_bf32"]
    ident, e_sum, pic_s = ns["ident"], ns["e_sum"], ns["pic_s"]
    ZT_st = ns["ZT_st"]
    psB, psA = ns["psB"], ns["psA"]
    scal, sc = ns["scal"], ns["sc"]
    ones_row = ns["ones_row"]
    NI = SB * NPAIR
    inst_sc = [(s, c) for s in range(SB) for c in range(NPAIR)]

    gates = {nm: P.val(nm) for nm in ("m_pe", "m_act", "m_dve", "p_q", "p_k", "p_v")}
    yield
    for eng in ("gpsimd", "vector", "scalar", "tensor"):
        for nm, v in gates.items():
            P.wait(eng, nm, v)

    for i in range(NI):
        P.do("gpsimd", lambda e, i=i: e.memset(qlbd_all[:, i, :], 0.0))
    gz = P.do("gpsimd", lambda e: e.engine_nop(), inc=("m_gp", 1))
    P.wait("vector", "m_gp", gz)
    P.wait("vector", "m_act", P.val("m_act"))
    d1 = None
    for i, (s, c) in enumerate(inst_sc):
        P.do("vector", lambda e, i=i, s=s, c=c: e.tensor_copy(qlbd_all[0:64, i, 0:64], qlT[0:64, s, c, :]))
        d1 = P.do("vector", lambda e, i=i, s=s, c=c: e.tensor_copy(qlbd_all[64:128, i, 64:128], qlT[64:128, s, c, :]),
                  inc=("m_dve", 1) if i == NI - 1 else None)

    yield
    for j in (2, 3):
        bankres[j].acquire("tensor")
    P.wait("tensor", "m_dve", d1)
    mv = None
    for i, (s, c) in enumerate(inst_sc):
        mv = P.do("tensor", lambda e, i=i, s=s, c=c: e.matmul(
            bank(2 + i // 8, 64, (i % 8) * 64), qlbd_all[:, i, :], klT[:, s, c, :],
            start=True, stop=True), inc=("m_pe", 1) if i == NI - 1 else None)
    P.wait("vector", "m_pe", mv)
    d2 = None
    for i in range(NI):
        d2 = P.do("vector", lambda e, i=i: e.reduce_max(scal[:, i : i + 1], bank(2 + i // 8, 64, (i % 8) * 64), axis=AX.X, negate=True),
                  inc=("m_dve", 1) if i == NI - 1 else None)
    P.wait("scalar", "m_dve", d2)
    a1 = None
    for i in range(NI):
        a1 = P.do("scalar", lambda e, i=i: e.activation(s2e_all[:, i, :], bank(2 + i // 8, 64, (i % 8) * 64), AF.Exp,
                                                        bias=scal[:, i : i + 1], accum_out=scal[:, 16 + i : 17 + i]),
                  inc=("m_act", 1) if i == NI - 1 else None)
    for j in (2, 3):
        bankres[j].used_until("m_act", a1)
    yield
    P.wait("vector", "m_act", a1)
    d3 = P.do("vector", lambda e: e.reciprocal(scal[:, 32:48], scal[:, 16:32]), inc=("m_dve", 1))
    P.wait("scalar", "m_dve", d3)
    a2 = None
    for i in range(NI):
        a2 = P.do("scalar", lambda e, i=i: e.activation(s2bf_all[:, i, :], s2e_all[:, i, :], AF.Identity, scale=scal[:, 32 + i : 33 + i]),
                  inc=("m_act", 1) if i == NI - 1 else None)
    P.wait("tensor", "m_act", a2)
    d4 = None
    for rnd in range(2):
        psBres[0].acquire("tensor")
        psBres[1].acquire("tensor")
        tv = None
        for i in range(rnd * 8, rnd * 8 + 8):
            tv = P.do("tensor", lambda e, i=i, rnd=rnd: e.transpose(psB[0:64, (i - rnd * 8) * 128 : (i - rnd * 8 + 1) * 128], s2bf_all[:, i, :], ident[:, :]),
                      inc=("m_pe", 1) if i == rnd * 8 + 7 else None)
        P.wait("vector", "m_pe", tv)
        d4 = P.do("vector", lambda e, rnd=rnd: e.tensor_copy(s2T_all[:].rearrange("p a b -> p (a b)")[:, rnd * 1024 : (rnd + 1) * 1024], psB[0:64, 0:1024]), inc=("m_dve", 1))
        psBres[0].used_until("m_dve", d4)
        psBres[1].used_until("m_dve", d4)
        yield
    P.wait("vector", "m_act", a2)
    d5 = None
    for i in range(NI):
        d5 = P.do("vector", lambda e, i=i: e.reduce_sum(scal[:, 48 + i : 49 + i], s2bf_all[:, i, :], axis=AX.X),
                  inc=("m_dve", 1) if i == NI - 1 else None)
    P.wait("scalar", "m_dve", d5)
    a3 = P.do("scalar", lambda e: e.copy(csbf_all[:, :], scal[:, 48:64]), inc=("m_act", 1))
    P.wait("tensor", "m_act", a3)
    d6 = None
    for rnd in range(2):
        psBres[0].acquire("tensor")
        psBres[1].acquire("tensor")
        tv = None
        for i in range(rnd * 8, rnd * 8 + 8):
            j = i - rnd * 8
            tv = P.do("tensor", lambda e, i=i, j=j: e.transpose(psB[0:1, j * 128 : (j + 1) * 128], csbf_all[:, i : i + 1], ident[:, :]),
                      inc=("m_pe", 1) if j == 7 else None)
        P.wait("vector", "m_pe", tv)
        for i in range(rnd * 8, rnd * 8 + 8):
            j = i - rnd * 8
            P.do("vector", lambda e, i=i, j=j: e.reduce_max(tiny[0:1, i : i + 1], psB[0:1, j * 128 : j * 128 + 64], axis=AX.X))
            d6 = P.do("vector", lambda e, i=i, j=j: e.reduce_max(tiny[0:1, 16 + i : 17 + i], psB[0:1, j * 128 + 64 : j * 128 + 128], axis=AX.X),
                      inc=("m_dve", 1) if j == 7 else None)
        psBres[0].used_until("m_dve", d6)
        psBres[1].used_until("m_dve", d6)
        yield
    yield
    for j in (2, 3):
        bankres[j].acquire("tensor")
    P.wait("tensor", "m_act", a2)
    mv = None
    for i in range(NI):
        mv = P.do("tensor", lambda e, i=i: e.matmul(bank(2 + i // 8, 64, (i % 8) * 64), e_sum[:, :], s2bf_all[:, i, :],
                                                    start=True, stop=True), inc=("m_pe", 1) if i == NI - 1 else None)
    P.wait("vector", "m_pe", mv)
    d7 = None
    for i in range(NI):
        pb = bank(2 + i // 8, 64, (i % 8) * 64)
        P.do("vector", lambda e, i=i, pb=pb: e.reduce_max(tiny[0:1, 32 + i : 33 + i], pb[0:1, :], axis=AX.X))
        d7 = P.do("vector", lambda e, i=i, pb=pb: e.reduce_max(tiny[0:1, 48 + i : 49 + i], pb[64:65, :], axis=AX.X),
                  inc=("m_dve", 1) if i == NI - 1 else None)
    for j in (2, 3):
        bankres[j].used_until("m_dve", d7)
    P.wait("gpsimd", "m_dve", d7)
    P.do("gpsimd", lambda e: e.tensor_mul(tiny2[0:1, 0:16], tiny[0:1, 0:16], tiny[0:1, 32:48]))
    g2 = P.do("gpsimd", lambda e: e.tensor_mul(tiny2[0:1, 16:32], tiny[0:1, 16:32], tiny[0:1, 48:64]), inc=("m_gp", 1))
    P.wait("vector", "m_gp", g2)
    d8 = P.do("vector", lambda e: e.reciprocal(tiny2[0:1, 32:64], tiny2[0:1, 0:32]), inc=("m_dve", 1))
    P.wait("scalar", "m_dve", d8)
    a4 = P.do("scalar", lambda e: e.copy(zsc_bf32[:, :], tiny2[0:1, 32:64]), inc=("m_act", 1))
    yield
    bankres[2].acquire("tensor")
    P.wait("tensor", "m_act", a4)
    mv = None
    for j in range(32):
        mv = P.do("tensor", lambda e, j=j: e.matmul(bank(2, 1, j)[0:64, :], ones_row[:, 0:64], zsc_bf32[:, j : j + 1],
                                                    start=True, stop=True), inc=("m_pe", 1) if j == 31 else None)
    P.wait("vector", "m_pe", mv)
    d9 = P.do("vector", lambda e: e.tensor_copy(zscb[:, :], bank(2, 32, 0)[0:64, :]), inc=("m_dve", 1))
    bankres[2].used_until("m_dve", d9)
    P.wait("scalar", "m_dve", max(d9, d4))
    a5 = None
    for i in range(NI):
        P.do("scalar", lambda e, i=i: e.activation(Z_all[:, i, 0:64], s2T_all[:, i, 0:64], AF.Identity, scale=zscb[:, i : i + 1]))
        a5 = P.do("scalar", lambda e, i=i: e.activation(Z_all[:, i, 64:128], s2T_all[:, i, 64:128], AF.Identity, scale=zscb[:, 16 + i : 17 + i]),
                  inc=("m_act", 1) if i == NI - 1 else None)

    def tr_all(src_t, dst_t, gname, gval):
        P.wait("tensor", gname, gval)
        ev = None
        for rnd in range(2):
            psBres[0].acquire("tensor")
            psBres[1].acquire("tensor")
            tv = None
            for i in range(rnd * 8, rnd * 8 + 8):
                for h in range(2):
                    j = (i - rnd * 8) * 2 + h
                    tv = P.do("tensor", lambda e, i=i, h=h, j=j: e.transpose(
                        psB[0:64, j * 64 : (j + 1) * 64],
                        src_t[:, i, h * 64 : (h + 1) * 64], ident[0:64, 0:64]),
                        inc=("m_pe", 1) if j == 15 else None)
            P.wait("vector", "m_pe", tv)
            ev = P.do("vector", lambda e, rnd=rnd: e.tensor_copy(dst_t[:].rearrange("p a b -> p (a b)")[:, rnd * 1024 : (rnd + 1) * 1024], psB[0:64, 0:1024]), inc=("m_dve", 1))
            psBres[0].used_until("m_dve", ev)
            psBres[1].used_until("m_dve", ev)
        return ev

    yield
    zt = tr_all(Z_all, ZT_all, "m_act", a5)

    z_gate = ("m_act", a5)
    lz = None
    for _ in range(6):
        def rounds(lhs_t, rhs_t, gname, gval, post):
            P.wait("tensor", gname, gval)
            out = None
            for rnd in range(2):
                for j in (2, 3):
                    bankres[j].acquire("tensor")
                mv = None
                for i in range(rnd * 8, rnd * 8 + 8):
                    for h in range(2):
                        col = (i - rnd * 8) * 128 + h * 64
                        last = (i == rnd * 8 + 7 and h == 1)
                        mv = P.do("tensor", lambda e, i=i, h=h, col=col, lhs_t=lhs_t, rhs_t=rhs_t: e.matmul(
                            psA[0:64, 1024 + col : 1024 + col + 64],
                            lhs_t[:, i, h * 64 : (h + 1) * 64], rhs_t[:, i, h * 64 : (h + 1) * 64],
                            start=True, stop=True), inc=("m_pe", 1) if last else None)
                dv, out = post(rnd, mv)
                for j in (2, 3):
                    bankres[j].used_until("m_dve", dv)
            return out

        def t_post(const_off):
            def post(rnd, mv):
                P.wait("vector", "m_pe", mv)
                ev = None
                for b in range(2):
                    pv = psA[0:64, 1024 + b * 512 : 1024 + (b + 1) * 512]
                    ev = P.do("vector", lambda e, b=b, rnd=rnd, pv=pv: e.scalar_tensor_tensor(
                        T_all[:].rearrange("p a b -> p (a b)")[:, rnd * 1024 + b * 512 : rnd * 1024 + (b + 1) * 512],
                        pv, -1.0, pic_s[:, const_off : const_off + 512],
                        op0=ALU.mult, op1=ALU.add), inc=("m_dve", 1) if b == 1 else None)
                return ev, ev
            return post

        def xz_post(rnd, mv):
            P.wait("vector", "m_pe", mv)
            for b in range(2):
                pv = psA[0:64, 1024 + b * 512 : 1024 + (b + 1) * 512]
                P.do("vector", lambda e, b=b, rnd=rnd, pv=pv: e.tensor_copy(
                    XZb_all[:].rearrange("p a b -> p (a b)")[:, rnd * 1024 + b * 512 : rnd * 1024 + (b + 1) * 512], pv))
            ev = None
            for b in range(2):
                pv = psA[0:64, 1024 + b * 512 : 1024 + (b + 1) * 512]
                ev = P.do("vector", lambda e, b=b, rnd=rnd, pv=pv: e.scalar_tensor_tensor(
                    T_all[:].rearrange("p a b -> p (a b)")[:, rnd * 1024 + b * 512 : rnd * 1024 + (b + 1) * 512],
                    pv, -1.0, pic_s[:, 0:512],
                    op0=ALU.mult, op1=ALU.add), inc=("m_dve", 1) if b == 1 else None)
            return ev, ev

        def sc_post(rnd, mv):
            P.wait("vector", "m_pe", mv)
            ev = None
            for b in range(2):
                pv = psA[0:64, 1024 + b * 512 : 1024 + (b + 1) * 512]
                ev = P.do("vector", lambda e, b=b, rnd=rnd, pv=pv: e.tensor_scalar(
                    Z_all[:].rearrange("p a b -> p (a b)")[:, rnd * 1024 + b * 512 : rnd * 1024 + (b + 1) * 512],
                    pv, 0.25, None, op0=ALU.mult), inc=("m_dve", 1) if b == 1 else None)
            return ev, ev

        yield
        t1 = rounds(s2T_all, Z_all, z_gate[0], z_gate[1], xz_post)
        yield
        xzt = tr_all(XZb_all, XZT_all, "m_dve", t1)
        yield
        t2 = rounds(XZT_all, T_all, "m_dve", max(t1, xzt), t_post(512))
        yield
        t3 = rounds(XZT_all, T_all, "m_dve", t2, t_post(1024))
        yield
        lz = rounds(ZT_all, T_all, "m_dve", t3, sc_post)
        z_gate = ("m_dve", lz)
        yield
        zt = tr_all(Z_all, ZT_all, "m_dve", lz)

    yield
    psBres[0].acquire("tensor")
    psBres[1].acquire("tensor")
    P.wait("tensor", "m_dve", lz)
    tv = None
    for i in range(NI):
        tv = P.do("tensor", lambda e, i=i: e.transpose(psB[0:128, i * 64 : (i + 1) * 64], Z_all[:, i, :], ident[0:64, 0:64]),
                  inc=("m_pe", 1) if i == NI - 1 else None)
    P.wait("vector", "m_pe", tv)
    ev = None
    for i, (s, c) in enumerate(inst_sc):
        ev = P.do("vector", lambda e, i=i, s=s, c=c: e.tensor_copy(ZT_st[:, s, c, :], psB[0:128, i * 64 : (i + 1) * 64]),
                  inc=("m_dve", 1) if i == NI - 1 else None)
    psBres[0].used_until("m_dve", ev)
    psBres[1].used_until("m_dve", ev)


def factory_gen(P, ns):
    wdT_sb, ident = ns["wdT_sb"], ns["ident"]
    psB, psBres = ns["psB"], ns["psBres"]
    wg_d, wu_d, wd_d = ns["wg_d"], ns["wu_d"], ns["wd_d"]
    wgt_sc, wut_sc = ns["wgt_sc"], ns["wut_sc"]
    res = ns["res"]
    chunk = [ns["wg_st"][i][:].rearrange("p a b -> p (a b)") for i in range(2)]
    unit = ns["wu_st"]

    gates = {nm: P.val(nm) for nm in ("m_pe", "m_act", "m_dve")}
    yield
    for nm, v in gates.items():
        P.wait("gpsimd", nm, v)

    ch_res = [res("fch0"), res("fch1")]
    wd_loads = [(wd_d, None, dc, q3) for dc in range(NCH) for q3 in range(3)]
    wgu_loads = [(wg_d, wgt_sc, hc, None) for hc in range(NHC)] + [(wu_d, wut_sc, hc, None) for hc in range(NHC)]
    all_loads = wd_loads + wgu_loads
    lvs = {}

    def issue_load(j):
        w_d2, _, a, q3 = all_loads[j]
        b = j % 2
        ch_res[b].acquire("gpsimd")
        if q3 is not None:
            lvs[j] = P.do("gpsimd", lambda e, a=a, q3=q3, b=b: e.dma_start(
                out=chunk[b], in_=w_d2[a * 128 : (a + 1) * 128, q3 * 1024 : (q3 + 1) * 1024]), inc=(f"s_wl{b}", 16))
        else:
            lvs[j] = P.do("gpsimd", lambda e, a=a, b=b, w=w_d2: e.dma_start(
                out=chunk[b], in_=w[a * 128 : (a + 1) * 128, :]), inc=(f"s_wl{b}", 16))

    issue_load(0)
    for jj in range(len(wd_loads)):
        dc, q3 = wd_loads[jj][2], wd_loads[jj][3]
        b = jj % 2
        if jj + 1 < len(all_loads):
            issue_load(jj + 1)
        P.wait("tensor", f"s_wl{b}", lvs[jj])
        last_tv = None
        for j in range(NCH):
            hc = q3 * 8 + j
            half = j // 4
            if j % 4 == 0:
                psBres[half].acquire("tensor")
            pslot = psB[:, half * 1024 + (j % 4) * 128 : half * 1024 + (j % 4) * 128 + 128]
            tv = P.do("tensor", lambda e, j=j, b=b, p=pslot: e.transpose(p, chunk[b][:, j * 128 : (j + 1) * 128], ident[:, :]),
                      inc=("m_pe", 1) if j % 4 == 3 else None)
            if j % 4 == 3:
                last_tv = tv
                P.wait("vector", "m_pe", tv)
                pv = psB[:, half * 1024 : half * 1024 + 512].rearrange("p (c t) -> p c t", c=4)
                ov = wdT_sb[:, hc - 3 : hc + 1, dc * 128 : (dc + 1) * 128]
                evv = P.do("vector", lambda e, pv=pv, ov=ov: e.tensor_copy(ov, pv), inc=("m_dve", 1))
                psBres[half].used_until("m_dve", evv)
        ch_res[b].used_until("m_pe", last_tv)
        yield

    unit_res = [res("fu0"), res("fu1")]
    for j2 in range(len(wgu_loads)):
        jj = len(wd_loads) + j2
        w_d, w_sc, hc, _ = all_loads[jj]
        b = jj % 2
        if jj + 1 < len(all_loads):
            issue_load(jj + 1)
        P.wait("tensor", f"s_wl{b}", lvs[jj])
        unit_res[b].acquire("vector")
        last_tv = None
        for kc in range(NCH):
            half = kc // 4
            if kc % 4 == 0:
                psBres[half].acquire("tensor")
            pslot = psB[:, half * 1024 + (kc % 4) * 128 : half * 1024 + (kc % 4) * 128 + 128]
            tv = P.do("tensor", lambda e, kc=kc, b=b, p=pslot: e.transpose(p, chunk[b][:, kc * 128 : (kc + 1) * 128], ident[:, :]),
                      inc=("m_pe", 1) if kc % 4 == 3 else None)
            if kc % 4 == 3:
                last_tv = tv
                P.wait("vector", "m_pe", tv)
                pv = psB[:, half * 1024 : half * 1024 + 512].rearrange("p (c t) -> p c t", c=4)
                evv = P.do("vector", lambda e, kc=kc, b=b, pv=pv: e.tensor_copy(unit[b][:, kc - 3 : kc + 1, :], pv), inc=("m_dve", 1))
                psBres[half].used_until("m_dve", evv)
        ch_res[b].used_until("m_pe", last_tv)
        dv = P.val("m_dve")
        P.wait("gpsimd", "m_dve", dv)
        sp = P.do("gpsimd", lambda e, w=w_sc, hc=hc, b=b: e.dma_start(out=w[hc], in_=unit[b][:, :, :]), inc=("p_wgs", 16))
        unit_res[b].used_until("p_wgs", sp)
        yield


def p2b(P, ns):
    kT_pair, ET_all, v4 = ns["kT_pair"], ns["ET_all"], ns["v4"]
    ones_col = ns["ones_col"]
    qlT, qlbd = ns["qlT"], ns["qlT_bd"]
    k_sc, v_sc = ns["k_sc"], ns["v_sc"]
    bank, bankres = ns["bank"], ns["bankres"]
    psA = ns["psA"]
    G2_bd, s3inv, s3v_sb, ZT_st = ns["G2_bd"], ns["s3inv"], ns["s3v_sb"], ns["ZT_st"]
    res = ns["res"]
    gen = ns["p2_gen"]

    def pump(n=1):
        if DBG["p2_nopinv"]:
            return
        for _ in range(n):
            try:
                next(gen)
            except StopIteration:
                break

    kt_res = [res("kt0"), res("kt1")]
    ve_res = [res("ve0"), res("ve1")]
    pairs = [(s_, c_) for s_ in range(SB) for c_ in range(NPAIR)]
    kvs = {}

    def load_kt(pi):
        s_, c_ = pairs[pi]
        kb_ = pi % 2
        kt_res[kb_].acquire("scalar")
        P.wait("scalar", "p_k", ns["pk_s"][s_])
        kvs[pi] = P.do("scalar", lambda e, s=s_, c=c_, kb=kb_: e.dma_start(
            out=kT_pair[kb][:, :], in_=k_sc[s, c]), inc=(f"s_kt{kb_}", 16))

    load_kt(0)
    load_kt(1)

    for eng in ("sync", "scalar", "vector"):
        P.wait(eng, "p_q", P.val("p_q"))
        P.wait(eng, "p_k", P.val("p_k"))
        P.wait(eng, "p_v", P.val("p_v"))

    if DBG["p2_nointerleave"]:
        pump(10 ** 6)
    prev_sweep = 0
    for s in range(SB):
        for c in range(NPAIR):
            kb = c % 2
            kv = kvs[s * NPAIR + c]
            prev_pe = P.val("m_pe")
            P.wait("gpsimd", "m_pe", prev_pe)
            gz = P.do("gpsimd", lambda e: e.memset(qlbd[:, :], 0.0), inc=("m_gp", 1))
            P.wait("vector", "m_gp", gz)
            P.do("vector", lambda e, s=s, c=c: e.tensor_copy(qlbd[0:64, 0:64], qlT[0:64, s, c, :]))
            qb = P.do("vector", lambda e, s=s, c=c: e.tensor_copy(qlbd[64:128, 64:128], qlT[64:128, s, c, :]), inc=("m_dve", 1))
            P.wait("tensor", f"s_kt{kb}", kv)
            P.wait("tensor", "m_dve", qb)
            P.wait("tensor", "m_gp", ns["ones_gate"])
            P.wait("scalar", "m_pe", prev_sweep)
            if c == 0:
                bankres[4].acquire("tensor")
            last = None
            exps = []
            for lt in range(32):
                bk = (lt // 4) % 2
                if lt % 4 == 0:
                    bankres[bk].acquire("tensor")
                mv = P.do("tensor", lambda e, lt=lt, bk=bk, kb=kb: e.matmul(
                    bank(bk, 128, (lt % 4) * 128), kT_pair[kb][:, lt * 128 : (lt + 1) * 128], qlbd[:, :],
                    start=True, stop=True), inc=("m_pe", 1) if lt % 4 == 3 else None)
                last = mv
                if lt % 4 == 3:
                    P.wait("scalar", "m_pe", mv)
                    ov = ET_all[:, lt - 3 : lt + 1, c * 128 : (c + 1) * 128]
                    evv = P.do("scalar", lambda e, bk=bk, ov=ov: e.activation(
                        ov, bank(bk).rearrange("p (a b) -> p a b", a=4), AF.Exp), inc=("m_act", 1))
                    bankres[bk].used_until("m_act", evv)
                    exps.append(evv)
                    pump(1)
                    if len(exps) > 1:
                        P.wait("tensor", "m_act", exps[-2])
                        for l2 in range(lt - 7, lt - 3):
                            P.do("tensor", lambda e, l2=l2, c=c: e.matmul(
                                bank(4, 1, c), ET_all[:, l2, c * 128 : (c + 1) * 128], ones_col[:, :],
                                start=(l2 == 0), stop=False))
            P.wait("tensor", "m_act", exps[-1])
            for l2 in range(28, 32):
                P.do("tensor", lambda e, l2=l2, c=c: e.matmul(
                    bank(4, 1, c), ET_all[:, l2, c * 128 : (c + 1) * 128], ones_col[:, :],
                    start=False, stop=(l2 == 31)), inc=("m_pe", 1) if l2 == 31 else None)
            kt_res[kb].used_until("m_pe", last)
            if s * NPAIR + c + 2 < len(pairs):
                load_kt(s * NPAIR + c + 2)

        P.wait("tensor", "m_act", P.val("m_act"))
        for j in range(2):
            bankres[j].acquire("tensor")
        for vt in range(8):
            b = vt % 2
            ve_res[b].acquire("sync")
            srcv = v_sc[s, vt * 512 : (vt + 1) * 512, :].rearrange("(g p) d -> p g d", p=128)
            lv = P.do("sync", lambda e, a=srcv, b=b: e.dma_start(out=v4[b][:, :, :], in_=a), inc=(f"s_v{b}", 16))
            P.wait("tensor", f"s_v{b}", lv)
            mv = None
            for g in range(NG):
                lt = vt * 4 + g
                for c in range(NPAIR):
                    mv = P.do("tensor", lambda e, lt=lt, g=g, c=c, b=b: e.matmul(
                        psA[:, (c // 4) * 512 + (c % 4) * 128 : (c // 4) * 512 + (c % 4) * 128 + 128],
                        ET_all[:, lt, c * 128 : (c + 1) * 128], v4[b][:, g, c * 128 : (c + 1) * 128],
                        start=(lt == 0), stop=(lt == 31)),
                        inc=("m_pe", 1) if (c == NPAIR - 1 and g == NG - 1) else None)
            ve_res[b].used_until("m_pe", mv)
            pump(2)
        prev_sweep = P.val("m_pe")

        P.wait("vector", "m_act", P.val("m_act"))
        sv = None
        for c in range(NPAIR):
            pb = psA[:, (c // 4) * 512 + (c % 4) * 128 : (c // 4) * 512 + (c % 4) * 128 + 128]
            P.wait("vector", "m_pe", prev_sweep)
            rv = P.do("vector", lambda e, c=c: e.reciprocal(s3inv[:, c : c + 1], bank(4, 1, c)), inc=("m_dve", 1))
            P.wait("scalar", "m_dve", rv)
            P.wait("scalar", "m_pe", prev_sweep)
            P.do("scalar", lambda e, pb=pb, s=s, c=c: e.activation(s3v_sb[0:64, s, c, :], pb[0:64, 0:64], AF.Identity, scale=s3inv[0:64, c : c + 1]))
            sv = P.do("scalar", lambda e, pb=pb, s=s, c=c: e.activation(s3v_sb[64:128, s, c, :], pb[64:128, 64:128], AF.Identity, scale=s3inv[64:128, c : c + 1]), inc=("m_act", 1))
            if c == NPAIR - 1:
                bankres[4].used_until("m_dve", P.val("m_dve"))
                bankres[4].used_until("m_act", sv)
                for j in range(2):
                    bankres[j].used_until("m_act", sv)
            pump(1)

    pump(10 ** 6)

    P.wait("tensor", "m_act", P.val("m_act"))
    P.wait("tensor", "m_dve", P.val("m_dve"))
    for s in range(SB):
        for c in range(NPAIR):
            bankres[5].acquire("tensor")
            P.do("tensor", lambda e, s=s, c=c: e.matmul(bank(5, 64, 0)[0:64, :], ZT_st[0:64, s, c, :], s3v_sb[0:64, s, c, :], start=True, stop=True))
            gm = P.do("tensor", lambda e, s=s, c=c: e.matmul(bank(5, 64, 64)[64:128, :], ZT_st[64:128, s, c, :], s3v_sb[64:128, s, c, :], start=True, stop=True), inc=("m_pe", 1))
            P.wait("vector", "m_pe", gm)
            P.do("vector", lambda e, s=s, c=c: e.tensor_copy(G2_bd[0:64, s, c, 0:64], bank(5, 64, 0)[0:64, :]))
            ge = P.do("vector", lambda e, s=s, c=c: e.tensor_copy(G2_bd[64:128, s, c, 64:128], bank(5, 64, 64)[64:128, :]), inc=("m_dve", 1))
            bankres[5].used_until("m_dve", ge)


def p3a(P, ns):
    qT_t3, x_t3, attnoT = ns["qT_t3"], ns["x_t3"], ns["attnoT"]
    E1all = ns["E1all"]
    nlv, invb, E1n2 = ns["nlv"], ns["invb"], ns["E1n2"]
    x2_sb, sq_scr = ns["x2_sb"], ns["sq_scr"]
    klT_bd, G2_bd = ns["klT_bd"], ns["G2_bd"]
    Wo_sb, bo_bf = ns["Wo_sb"], ns["bo_bf"]
    q_sc, x_in, x2_sc = ns["q_sc"], ns["x_in"], ns["x2_sc"]
    rstd2, nmr2 = ns["rstd2"], ns["nmr2"]
    e_sum, ones_row = ns["e_sum"], ns["ones_row"]
    bank, bankres, res, sc = ns["bank"], ns["bankres"], ns["res"], ns["sc"]

    P.wait("sync", "p_q", P.val("p_q"))
    P.wait("sync", "m_pe", P.val("m_pe"))
    P.wait("sync", "m_dve", P.val("m_dve"))
    P.wait("sync", "m_act", P.val("m_act"))
    stat4, eps_c = ns["stat4"], ns["eps_c"]
    st_prev = {"g4": 0, "exp": 0, "nmr": 0}
    pending_tail = None

    def stats_tail(pend):
        s_, it_, smv_ = pend
        P.wait("vector", "m_act", smv_)
        P.wait("vector", "m_gp", st_prev["g4"])
        nm2 = P.do("vector", lambda e: e.tensor_tensor(stat4[:, 8:12], stat4[:, 4:8], stat4[:, 4:8], ALU.mult), inc=("m_dve", 1))
        P.wait("gpsimd", "m_dve", nm2)
        P.do("gpsimd", lambda e: e.tensor_scalar_mul(stat4[:, 8:12], stat4[:, 8:12], 1.0 / (1024.0 * 1024.0)))
        P.do("gpsimd", lambda e: e.tensor_scalar_mul(stat4[:, 12:16], stat4[:, 0:4], 1.0 / 1024.0))
        g4v = P.do("gpsimd", lambda e: e.tensor_sub(stat4[:, 12:16], stat4[:, 12:16], stat4[:, 8:12]), inc=("m_gp", 1))
        P.wait("scalar", "m_gp", g4v)
        lnv2 = P.do("scalar", lambda e: e.activation(stat4[:, 16:20], stat4[:, 12:16], AF.Ln, bias=eps_c[:, :]), inc=("m_act", 1))
        P.wait("vector", "m_act", lnv2)
        mhv = P.do("vector", lambda e: e.tensor_scalar(stat4[:, 20:24], stat4[:, 16:20], -0.5, None, op0=ALU.mult), inc=("m_dve", 1))
        P.wait("scalar", "m_dve", mhv)
        expv = P.do("scalar", lambda e, s=s_, it=it_: e.activation(
            rstd2[:, s, it * NG : (it + 1) * NG], stat4[:, 20:24], AF.Exp), inc=("m_act", 1))
        P.wait("vector", "m_act", expv)
        nmrv = P.do("vector", lambda e, s=s_, it=it_: e.scalar_tensor_tensor(
            nmr2[:, s, it * NG : (it + 1) * NG], stat4[:, 4:8], -1.0 / 1024.0,
            rstd2[:, s, it * NG : (it + 1) * NG], op0=ALU.mult, op1=ALU.mult), inc=("m_dve", 1))
        return {"g4": g4v, "exp": expv, "nmr": nmrv}
    q_res = [res("q3_0"), res("q3_1")]
    x3_res = res("x3")
    x2_res = [res(f"x2s_{j}") for j in range(4)]
    chain_res = [res(f"p3a_ch{j}") for j in range(4)]

    NTILES = SB * NT

    def load_q(i):
        s, it = divmod(i, NT)
        buf = i % 2
        q_res[buf].acquire("scalar")
        return P.do("scalar", lambda e, s=s, it=it, b=buf: e.dma_start(
            out=qT_t3[b][:, :, :], in_=q_sc[s, it].rearrange("c p t -> p c t")), inc=(f"s_q{buf}", 16))

    def s1(i, qv, prev_e1ns=None):
        s, it = divmod(i, NT)
        buf = i % 2
        P.wait("tensor", f"s_q{buf}", qv)
        exps = []
        for c in range(NPAIR):
            bsc = c % 2
            bankres[bsc].acquire("tensor")
            mv = P.do("tensor", lambda e, s=s, c=c, b=buf, bk=bsc: e.matmul(
                bank(bk), klT_bd[:, s, c, :], qT_t3[b][:, c, :], start=True, stop=True), inc=("m_pe", 1))
            P.wait("scalar", "m_pe", mv)
            if prev_e1ns is not None:
                P.wait("scalar", "m_dve", prev_e1ns[c])
            ex = P.do("scalar", lambda e, c=c, bk=bsc: e.activation(E1all[:, c, :], bank(bk), AF.Exp), inc=("m_act", 1))
            bankres[bsc].used_until("m_act", ex)
            exps.append(ex)
        q_res[buf].used_until("m_pe", P.val("m_pe"))
        return exps

    def stage2(i, exps):
        s, it = divmod(i, NT)
        e1ns = []
        for c in range(NPAIR):
            bs = c % 2
            cb_ = c % 4
            bankres[bs].acquire("tensor")
            P.wait("tensor", "m_act", exps[c])
            sv = P.do("tensor", lambda e, c=c, bk=bs: e.matmul(bank(bk), e_sum[:, :], E1all[:, c, :], start=True, stop=True), inc=("m_pe", 1))
            chain_res[cb_].acquire("scalar")
            P.wait("scalar", "m_pe", sv)
            lv = P.do("scalar", lambda e, bk=bs, b=cb_: e.activation(nlv[b][:, :], bank(bk), AF.Ln), inc=("m_act", 1))
            bankres[bs].used_until("m_act", lv)
            ie = P.do("scalar", lambda e, b=cb_: e.activation(invb[b][:, :], nlv[b][:, :], AF.Exp, scale=-1.0), inc=("m_act", 1))
            P.wait("vector", "m_act", ie)
            nv = P.do("vector", lambda e, c=c, b=cb_: e.tensor_tensor(E1n2[b][:, :], E1all[:, c, :], invb[b][:, :], ALU.mult), inc=("m_dve", 1))
            chain_res[cb_].used_until("m_dve", nv)
            e1ns.append(nv)
        return e1ns

    def stage3(i, e1ns):
        s, it = divmod(i, NT)
        for c in range(NPAIR):
            bat = 4 + c % 2
            bankres[bat].acquire("tensor")
            P.wait("tensor", "m_dve", e1ns[c])
            av = P.do("tensor", lambda e, s=s, c=c, b=c % 4, bk=bat: e.matmul(bank(bk), G2_bd[:, s, c, :], E1n2[b][:, :], start=True, stop=True), inc=("m_pe", 1))
            chain_res[c % 4].used_until("m_pe", av)
            P.wait("vector", "m_pe", av)
            ae = P.do("vector", lambda e, c=c, bk=bat: e.tensor_copy(attnoT[:, c, :], bank(bk)), inc=("m_dve", 1))
            bankres[bat].used_until("m_dve", ae)
        return P.val("m_dve")

    qvs = {0: load_q(0), 1: load_q(1)}
    exps = s1(0, qvs[0])
    e1ns = stage2(0, exps)

    for i in range(NTILES):
        s, it = divmod(i, NT)
        t0 = it * 512
        if i + 2 < NTILES:
            qvs[i + 2] = load_q(i + 2)
        x3_res.acquire("sync")
        xv = P.do("sync", lambda e, s=s, t0=t0: e.dma_start(
            out=x_t3[:, :, :], in_=x_in[s, t0 : t0 + 512, :].rearrange("(g p) d -> p g d", p=128)), inc=("s_x3", 16))

        attn_done = stage3(i, e1ns)

        if i + 1 < NTILES:
            nexps = s1(i + 1, qvs[i + 1], e1ns)
            e1ns = stage2(i + 1, nexps)
        if pending_tail is not None:
            st_prev = stats_tail(pending_tail)
            pending_tail = None

        P.wait("tensor", "m_dve", attn_done)
        P.wait("tensor", "s_wo", 16)
        P.wait("tensor", "s_bo", 16)
        P.wait("tensor", "s_x3", xv)
        P.wait("vector", "s_x3", xv)
        if not DBG["p3a_wo"]:
            continue
        for gi in range(NG):
            xbuf = gi
            x2_res[xbuf].acquire("vector")
            for hf in range(2):
                bk = 2 + hf
                bankres[bk].acquire("tensor")
                for kc in range(NCH):
                    P.do("tensor", lambda e, gi=gi, hf=hf, kc=kc, bk=bk: e.matmul(
                        bank(bk), attnoT[:, kc, gi * 128 : (gi + 1) * 128],
                        Wo_sb[:, kc, hf * 512 : (hf + 1) * 512],
                        start=(kc == 0), stop=(kc == NCH - 1)))
                wv = P.do("tensor", lambda e, hf=hf, bk=bk: e.matmul(
                    bank(bk), ones_row[:, :], bo_bf[:, hf * 512 : (hf + 1) * 512],
                    start=False, stop=True), inc=("m_pe", 1))
                P.wait("vector", "m_pe", wv)
                evv = P.do("vector", lambda e, gi=gi, hf=hf, bk=bk, xb=xbuf: e.scalar_tensor_tensor(
                    x2_sb[xb][:, hf * 512 : (hf + 1) * 512], bank(bk), 1.0,
                    x_t3[:, gi, hf * 512 : (hf + 1) * 512], op0=ALU.mult, op1=ALU.add), inc=("m_dve", 1))
                bankres[bk].used_until("m_dve", evv)
            x2v = P.val("m_dve")
            P.wait("gpsimd", "m_dve", x2v)
            dst = x2_sc[s, t0 + gi * 128 : t0 + (gi + 1) * 128, :]
            sp = P.do("gpsimd", lambda e, d=dst, xb=xbuf: e.dma_start(out=d, in_=x2_sb[xb][:, :]), inc=("p_x2", 16))
            x2_res[xbuf].used_until("p_x2", sp)
            if gi == 0:
                P.wait("scalar", "m_gp", st_prev["g4"])
                P.wait("scalar", "m_dve", st_prev["nmr"])
            P.wait("scalar", "m_dve", x2v)
            P.do("scalar", lambda e, g=gi, xb=xbuf: e.activation(
                sq_scr[:, :], x2_sb[xb][:, :], AF.Square, accum_out=stat4[:, g : g + 1]), inc=("m_act", 1))
            smv = P.do("scalar", lambda e, g=gi, xb=xbuf: e.activation(
                sq_scr[:, :], x2_sb[xb][:, :], AF.Identity, accum_out=stat4[:, 4 + g : 5 + g]), inc=("m_act", 1))
            x2_res[xbuf].used_until("m_act", smv)
        pending_tail = (s, it, smv)
        x3_res.used_until("m_dve", P.val("m_dve"))
    stats_tail(pending_tail)


def p3b(P, ns):
    x2_t, x2n_bf, x2nT, h_sb, out_sb, silu_g = ns["x2_t"], ns["x2n_bf"], ns["x2nT"], ns["h_sb"], ns["out_sb"], ns["silu_g"]
    wg_st, wu_st, wdT_sb = ns["wg_st"], ns["wu_st"], ns["wdT_sb"]
    x2_sc, wgt_sc, wut_sc, y_out = ns["x2_sc"], ns["wgt_sc"], ns["wut_sc"], ns["y_out"]
    rstd2, nmr2, ident = ns["rstd2"], ns["nmr2"], ns["ident"]
    bank, bankres, psBres, psB, res = ns["bank"], ns["bankres"], ns["psBres"], ns["psB"], ns["res"]

    P.wait("sync", "p_x2", P.val("p_x2"))
    P.wait("sync", "p_wgs", P.val("p_wgs"))
    P.wait("sync", "m_pe", P.val("m_pe"))
    P.wait("sync", "m_dve", P.val("m_dve"))
    P.wait("sync", "m_act", P.val("m_act"))

    x2_res = [res("x2t_0"), res("x2t_1")]
    wg_res = [res("wgst_0"), res("wgst_1")]
    wu_res = [res("wust_0"), res("wust_1")]
    o_res = [res("osb_0"), res("osb_1")]
    silu_res = res("silu")
    xnT_res = res("x2nT")
    rstd2, nmr2 = ns["rstd2"], ns["nmr2"]
    last_tr = P.val("m_pe")
    NTILES3 = SB * NT

    def load_x2(idx):
        s, it = divmod(idx, NT)
        t0 = it * 512
        b = idx % 2
        x2_res[b].acquire("scalar")
        return P.do("scalar", lambda e, s=s, t0=t0, b=b: e.dma_start(
            out=x2_t[b][:, :, :], in_=x2_sc[s, t0 : t0 + 512, :].rearrange("(g p) d -> p g d", p=128)), inc=(f"s_x2{b}", 16))

    xvs = {0: load_x2(0)}
    xns = {}
    last_tr_b = [0, 0, 0, 0]

    def norms(idx2):
        s2_, it2 = divmod(idx2, NT)
        b2 = idx2 % 2
        P.wait("scalar", f"s_x2{b2}", xvs[idx2])
        vs = []
        for gi in range(NG):
            gidx = it2 * NG + gi
            P.wait("scalar", "m_pe", last_tr_b[gi])
            xn = P.do("scalar", lambda e, b=b2, gi=gi, s=s2_, gidx=gidx: e.activation(
                x2n_bf[gi][:, :], x2_t[b][:, gi, :], AF.Identity,
                bias=nmr2[:, s, gidx : gidx + 1], scale=rstd2[:, s, gidx : gidx + 1]), inc=("m_act", 1))
            vs.append(xn)
        xns[idx2] = vs

    def tr_block(idx2):
        out = None
        for gi in range(NG):
            P.wait("tensor", "m_act", xns[idx2][gi])
            for c in range(NCH):
                half = c // 4
                if c % 4 == 0:
                    psBres[half].acquire("tensor")
                pslot = psB[:, half * 1024 + (c % 4) * 128 : half * 1024 + (c % 4) * 128 + 128]
                tv = P.do("tensor", lambda e, gi=gi, c=c, p=pslot: e.transpose(p, x2n_bf[gi][:, c * 128 : (c + 1) * 128], ident[:, :]),
                          inc=("m_pe", 1) if c % 4 == 3 else None)
                if c % 4 == 3:
                    last_tr_b[gi] = tv
                    P.wait("vector", "m_pe", tv)
                    pv = psB[:, half * 1024 : half * 1024 + 512].rearrange("p (c t) -> p c t", c=4)
                    ov = x2nT[:, c - 3 : c + 1, gi * 128 : (gi + 1) * 128]
                    evv = P.do("vector", lambda e, pv=pv, ov=ov: e.tensor_copy(ov, pv), inc=("m_dve", 1))
                    psBres[half].used_until("m_dve", evv)
                    out = evv
        return out

    for s in range(SB):
        for it in range(NT):
            t0 = it * 512
            idx = s * NT + it
            buf = idx % 2
            if idx + 1 < NTILES3:
                xvs[idx + 1] = load_x2(idx + 1)

            if idx == 0:
                norms(0)
            xt_done = tr_block(idx)
            P.wait("tensor", "m_dve", xt_done)

            for hc in range(NHC):
                wb = hc % 2
                wg_res[wb].acquire("sync")
                wgv = P.do("sync", lambda e, hc=hc, b=wb: e.dma_start(out=wg_st[b][:, :, :], in_=wgt_sc[hc]), inc=(f"s_wg{wb}", 16))
                wu_res[wb].acquire("sync")
                wuv = P.do("sync", lambda e, hc=hc, b=wb: e.dma_start(out=wu_st[b][:, :, :], in_=wut_sc[hc]), inc=(f"s_wu{wb}", 16))
                P.wait("tensor", f"s_wg{wb}", wgv)
                bg, bu = hc % 2, 2 + hc % 2
                bankres[bg].acquire("tensor")
                gv = None
                for kc in range(NCH):
                    gv = P.do("tensor", lambda e, kc=kc, b=wb, bk=bg: e.matmul(
                        bank(bk), wg_st[b][:, kc, :], x2nT[:, kc, :], start=(kc == 0), stop=(kc == NCH - 1)),
                        inc=("m_pe", 1) if kc == NCH - 1 else None)
                wg_res[wb].used_until("m_pe", gv)
                P.wait("tensor", f"s_wu{wb}", wuv)
                bankres[bu].acquire("tensor")
                uv = None
                for kc in range(NCH):
                    uv = P.do("tensor", lambda e, kc=kc, b=wb, bk=bu: e.matmul(
                        bank(bk), wu_st[b][:, kc, :], x2nT[:, kc, :], start=(kc == 0), stop=(kc == NCH - 1)),
                        inc=("m_pe", 1) if kc == NCH - 1 else None)
                wu_res[wb].used_until("m_pe", uv)
                silu_res.acquire("scalar")
                P.wait("scalar", "m_pe", gv)
                sg = P.do("scalar", lambda e, bk=bg: e.activation(silu_g[:, :], bank(bk), AF.Silu, scale=S127), inc=("m_act", 1))
                bankres[bg].used_until("m_act", sg)
                P.wait("vector", "m_pe", uv)
                P.wait("vector", "m_act", sg)
                hv = P.do("vector", lambda e, hc=hc, bk=bu: e.scalar_tensor_tensor(
                    h_sb[:, hc, :], bank(bk), 1.0, silu_g[:, :], op0=ALU.mult, op1=ALU.mult), inc=("m_dve", 1))
                bankres[bu].used_until("m_dve", hv)
                silu_res.used_until("m_dve", hv)
            h_done = P.val("m_dve")

            P.wait("tensor", "m_dve", h_done)
            for gi in range(NG):
                ob = gi % 2
                o_res[ob].acquire("vector")
                for hf in range(2):
                    bk = 4 + (gi * 2 + hf) % 2
                    bankres[bk].acquire("tensor")
                    dvv = None
                    for hc in range(NHC):
                        dvv = P.do("tensor", lambda e, gi=gi, hf=hf, hc=hc, bk=bk: e.matmul(
                            bank(bk), h_sb[:, hc, gi * 128 : (gi + 1) * 128],
                            wdT_sb[:, hc, hf * 512 : (hf + 1) * 512],
                            start=(hc == 0), stop=(hc == NHC - 1)),
                            inc=("m_pe", 1) if hc == NHC - 1 else None)
                    P.wait("vector", "m_pe", dvv)
                    evv = P.do("vector", lambda e, gi=gi, hf=hf, bk=bk, b=buf, ob=ob: e.scalar_tensor_tensor(
                        out_sb[ob][:, hf * 512 : (hf + 1) * 512], bank(bk), S127SQ,
                        x2_t[b][:, gi, hf * 512 : (hf + 1) * 512], op0=ALU.mult, op1=ALU.add), inc=("m_dve", 1))
                    bankres[bk].used_until("m_dve", evv)
                ov = P.val("m_dve")
                P.wait("gpsimd", "m_dve", ov)
                dst = y_out[s, t0 + gi * 128 : t0 + (gi + 1) * 128, :]
                sp = P.do("gpsimd", lambda e, d=dst, ob=ob: e.dma_start(out=d, in_=out_sb[ob][:, :]), inc=("p_out", 16))
                o_res[ob].used_until("p_out", sp)
            if idx + 1 < NTILES3:
                norms(idx + 1)
            x2_res[buf].used_until("m_dve", P.val("m_dve"))


def _host_consts():
    ident = np.eye(128, dtype=ml_dtypes.bfloat16)
    e_sum = np.zeros((128, 128), dtype=ml_dtypes.bfloat16)
    e_sum[0:64, 0:64] = 1
    e_sum[64:128, 64:128] = 1
    rows = np.zeros((1, 384), dtype=ml_dtypes.bfloat16)
    rows[0, 0:64] = 1
    rows[0, 192:256] = 1
    rows[0, 256:384] = 1
    brows = np.zeros((128, 256), dtype=ml_dtypes.bfloat16)
    brows[0, 0:64] = 1
    brows[64, 192:256] = 1
    pic = np.zeros((64, 1536), dtype=np.float32)
    for j, v in enumerate((7.0, 15.0, 13.0)):
        blk = v * np.eye(64, dtype=np.float32)
        for r in range(8):
            pic[:, j * 512 + r * 64 : j * 512 + (r + 1) * 64] = blk
    return ident, e_sum, rows, pic, brows


def _in_maps(x, Wq, Wk, Wv, Wo, bo, wg, wu, wd):
    ident, e_sum, rows, pic, brows = _host_consts()
    maps = []
    for cidx in range(NCORES):
        maps.append({
            "x": np.ascontiguousarray(x[cidx * SB : (cidx + 1) * SB]),
            "Wq": Wq, "Wk": Wk, "Wv": Wv, "Wo": Wo, "bo": bo,
            "wg": wg, "wu": wu, "wd": wd,
            "ident": ident, "esum": e_sum, "rows": rows, "pic": pic, "brows": brows,
        })
    return maps


def kernel(L0_attn_x, L0_attn_Wk, L0_attn_Wq, L0_attn_Wv, L0_attn_Wo, L0_attn_bo,
           L0_ffn_w_gate, L0_ffn_w_up, L0_ffn_w_down):
    x = np.ascontiguousarray(np.asarray(L0_attn_x, dtype=np.float32))
    maps = _in_maps(
        x,
        np.ascontiguousarray(np.asarray(L0_attn_Wq, dtype=np.float32)),
        np.ascontiguousarray(np.asarray(L0_attn_Wk, dtype=np.float32)),
        np.ascontiguousarray(np.asarray(L0_attn_Wv, dtype=np.float32)),
        np.ascontiguousarray(np.asarray(L0_attn_Wo, dtype=np.float32)),
        np.ascontiguousarray(np.asarray(L0_attn_bo, dtype=np.float32)),
        np.ascontiguousarray(np.asarray(L0_ffn_w_gate, dtype=np.int32)),
        np.ascontiguousarray(np.asarray(L0_ffn_w_up, dtype=np.int32)),
        np.ascontiguousarray(np.asarray(L0_ffn_w_down, dtype=np.int32)),
    )
    nc = build_program()
    res = run_bass_kernel_spmd(nc, maps, core_ids=list(range(NCORES)))
    return np.concatenate([r["y"] for r in res.results], axis=0)
